# version 18
# baseline (speedup 1.0000x reference)
import contextlib
import dataclasses
import math

import numpy as np
import ml_dtypes

import concourse.bass as bass
import concourse.mybir as mybir
from concourse.bass_utils import run_bass_kernel_spmd
from concourse.engine_type import EngineType

F32 = mybir.dt.float32
BF16 = mybir.dt.bfloat16
AF = mybir.ActivationFunctionType
ALU = mybir.AluOpType
AX = mybir.AxisListType

NCORES = 8
S = 8192
D = 2048
NH = 16
HD = 128
KVR = 512
RD = 64
TOK = S // NCORES
NT = S // 512
SCALE = float(HD) ** -0.5
EPS_RMS = 1e-6
EPS_LN = 1e-5
INTER = 8192
RS2 = 1.0 / math.sqrt(2048.0)
RS8 = 1.0 / math.sqrt(8192.0)
CASCADE = True


def _yarn_cos_sin():
    MAX_POS, ORIG_POS, BASE = 131072, 8192, 500000.0
    BETA_FAST, BETA_SLOW = 32.0, 1.0
    scaling = MAX_POS / ORIG_POS
    pos_freqs = BASE ** (np.arange(0, RD, 2, dtype=np.float64) / RD)
    extrap = 1.0 / pos_freqs
    interp = 1.0 / (scaling * pos_freqs)
    low = max(math.floor(RD * math.log(ORIG_POS / (BETA_FAST * 2 * math.pi))
                         / (2 * math.log(BASE))), 0)
    high = min(math.ceil(RD * math.log(ORIG_POS / (BETA_SLOW * 2 * math.pi))
                         / (2 * math.log(BASE))), RD - 1)
    i = np.arange(RD // 2, dtype=np.float64)
    smooth = np.clip((i - low) / max(high - low, 1), 0.0, 1.0)
    inv_freq = ((1.0 - smooth) * interp + smooth * extrap).astype(np.float32)
    pos = np.arange(S, dtype=np.float32)
    freqs = pos[:, None] * inv_freq[None, :]
    emb = np.concatenate([freqs, freqs], axis=-1)
    mscale = np.float32(0.1 * math.log(scaling) + 1.0)
    return ((np.cos(emb) * mscale).astype(np.float32),
            (np.sin(emb) * mscale).astype(np.float32))


def _cs_table():
    cos, sin = _yarn_cos_sin()
    cosT, sinT = cos.T, sin.T
    tab = np.empty((128, S), np.float32)
    tab[0:64] = cosT
    tab[64:96] = sinT[32:64]
    tab[96:128] = -sinT[0:32]
    return tab


def _hadamard(n):
    h = np.array([[1.0]], np.float32)
    while h.shape[0] < n:
        h = np.block([[h, h], [h, -h]])
    return h.astype(np.float32)


class Sem:
    def __init__(self, ctx, nc, name):
        self.h = ctx.enter_context(nc.semaphore(name))
        self.n = 0


class B:
    ENGS = ("t", "v", "a", "g", "y")

    def __init__(self):
        self.q = {e: [] for e in B.ENGS}

    def add(self, eng, fn, wait=(), inc=()):
        waits = []
        for w in wait:
            if isinstance(w, Sem):
                waits.append((w.h, w.n))
            else:
                sem, v = w
                waits.append((sem.h if isinstance(sem, Sem) else sem, v))
        incs = []
        for sem, k in inc:
            sem.n += k
            incs.append((sem.h, k))
        self.q[eng].append((waits, fn, incs))

    def run(self, eng_name, eng):
        for waits, fn, incs in self.q[eng_name]:
            for h, v in waits:
                if v > 0:
                    eng.wait_ge(h, v)
            ins = fn(eng)
            for h, k in incs:
                ins.then_inc(h, k)


def _w(*sems):
    return [(s, s.n) for s in sems]


def build_nc(taps=()):
    nc = bass.Bass("TRN2", target_bir_lowering=False, debug=False)

    def din(name, shape, dtype=F32):
        return nc.dram_tensor(name, shape, dtype, kind="ExternalInput")

    xo = din("xo", [D, TOK])
    cs = din("cs", [128, S])
    h128 = din("h128", [128, 128], BF16)
    onesb = din("onesb", [128, 128], BF16)
    onesf = din("onesf", [1, 128])
    qw = din("qw", [D, 2 * HD])
    kvaw = din("kvaw", [D, KVR])
    kvbw = din("kvbw", [KVR, 4 * HD])
    ow = din("ow", [D, D])
    wup = din("wup", [D, INTER])
    wg = din("wg", [D, INTER])
    wd = din("wd", [INTER, D])
    kvnw = din("kvnw", [128, 4])
    suug = din("suug", [128, 16])
    wscup = din("wscup", [128, 64])
    wscg = din("wscg", [128, 64])
    sudp = din("sudp", [128, 64])
    wscdp = din("wscdp", [128, 16])
    epsv = din("epsv", [128, 2])

    out_t = nc.dram_tensor("out", [D, TOK], F32, kind="ExternalOutput")

    pid512 = nc.partition_id(engines=[EngineType.SP]) * 512

    cc_xl_in = nc.dram_tensor("cc_xl_in", [D + KVR, TOK], BF16)
    cc_xl_out = nc.dram_tensor("cc_xl_out", [NCORES * (D + KVR), TOK], BF16,
                               addr_space="Shared")
    cc_a_in0 = nc.dram_tensor("cc_a_in0", [2 * HD, 8 * 512], BF16)
    cc_a_out0 = nc.dram_tensor("cc_a_out0", [NCORES * 2 * HD, 8 * 512], BF16,
                               addr_space="Shared")
    cc_a_in1 = nc.dram_tensor("cc_a_in1", [2 * HD, 8 * 512], BF16)
    cc_a_out1 = nc.dram_tensor("cc_a_out1", [NCORES * 2 * HD, 8 * 512], BF16,
                               addr_space="Shared")
    x1_sp = nc.dram_tensor("x1_sp", [D, TOK], F32, kind="ExternalOutput")
    v2_sp = nc.dram_tensor("v2_sp", [D, TOK], F32, kind="ExternalOutput")

    tap_outs = {}
    for tname, shape, dty in (
        ("q0", [128, S], BF16), ("k0", [128, S], BF16), ("vt0", [128, S], BF16),
        ("lat", [128, 4 * 1024], BF16), ("negm", [128, 2], F32),
        ("aown", [128, 16 * TOK], BF16), ("xs", [128, 16 * TOK], BF16),
        ("u", [128, 8 * TOK], BF16), ("u0", [128, 8 * TOK], BF16),
        ("asb", [128, 512], BF16),
        ("mx", [1, 80], F32), ("sg", [128, 64], F32), ("sup", [128, 64], F32),
    ):
        if tname in taps:
            tap_outs[tname] = nc.dram_tensor("tap_" + tname, shape, dty,
                                             kind="ExternalOutput")

    ctx = contextlib.ExitStack()
    b = B()

    gld = Sem(ctx, nc, "gld")
    sld = Sem(ctx, nc, "sld")
    pe = Sem(ctx, nc, "pe")
    ac = Sem(ctx, nc, "ac")
    dv = Sem(ctx, nc, "dv")
    spe = Sem(ctx, nc, "spe")
    sac = Sem(ctx, nc, "sac")
    spv = Sem(ctx, nc, "spv")
    cc1 = Sem(ctx, nc, "cc1")
    cc2 = Sem(ctx, nc, "cc2")
    cc2b = Sem(ctx, nc, "cc2b")
    gp = Sem(ctx, nc, "gp")

    def DMA(eng, out_ap, in_ap, wait=()):
        sem = gld if eng == "g" else sld
        b.add(eng, (lambda e, o=out_ap, i=in_ap: e.dma_start(out=o, in_=i)),
              wait, [(sem, 16)])

    def MM(out_ap, lhsT, rhs, start=True, stop=True, wait=(), inc=()):
        b.add("t", (lambda e, o=out_ap, l=lhsT, r=rhs, s1=start, s2=stop:
                    e.matmul(o, l, r, start=s1, stop=s2)), wait, inc)

    def ACT(out_ap, in_ap, func, bias=0.0, scale=1.0, wait=(), inc=()):
        b.add("a", (lambda e, o=out_ap, i=in_ap, f=func, bi=bias, sc=scale:
                    e.activation(o, i, f, bias=bi, scale=sc)), wait, inc)

    def VTT(op, out_ap, in0, in1, wait=(), inc=()):
        b.add("v", (lambda e, o=out_ap, a0=in0, a1=in1, p=op:
                    e.tensor_tensor(o, a0, a1, p)), wait, inc)

    def VTS(out_ap, in0, scalar, op=ALU.mult, wait=(), inc=()):
        b.add("v", (lambda e, o=out_ap, a0=in0, scl=scalar, p=op:
                    e.tensor_single_scalar(o, a0, scl, p)), wait, inc)

    def VSTT(out_ap, in0, scalar, in1, op0=ALU.mult, op1=ALU.mult,
             wait=(), inc=()):
        b.add("v", (lambda e, o=out_ap, a0=in0, scl=scalar, a1=in1, p0=op0,
                    p1=op1: e.scalar_tensor_tensor(o, a0, scl, a1, p0, p1)),
              wait, inc)

    def VCP(out_ap, in_ap, wait=(), inc=()):
        b.add("v", (lambda e, o=out_ap, i=in_ap: e.tensor_copy(o, i)),
              wait, inc)

    def VRMAX(out_ap, in_ap, wait=(), inc=()):
        b.add("v", (lambda e, o=out_ap, i=in_ap: e.reduce_max(o, i, AX.X)),
              wait, inc)

    def VRECIP(out_ap, in_ap, wait=(), inc=()):
        b.add("v", (lambda e, o=out_ap, i=in_ap: e.reciprocal(o, i)),
              wait, inc)

    def sb(stack, name, shape, dtype, side=None):
        return stack.enter_context(nc.sbuf_tensor(name, shape, dtype,
                                                  side=side))

    def ps(stack, name, shape, dtype=F32):
        return stack.enter_context(nc.psum_tensor(name, shape, dtype))

    sbP = ctx.enter_context(contextlib.ExitStack())
    h128_sb = sb(sbP, "h128_sb", [128, 128], BF16)
    onesb_sb = sb(sbP, "onesb_sb", [128, 128], BF16)
    onesf_sb = sb(sbP, "onesf_sb", [1, 128], F32)
    kvnw_sb = sb(sbP, "kvnw_sb", [128, 4], F32)
    suug_sb = sb(sbP, "suug_sb", [128, 16], F32)
    sup_sb = sb(sbP, "sup_sb", [128, 64], F32)
    sgate_sb = sb(sbP, "sgate_sb", [128, 64], F32)
    sdp_sb = sb(sbP, "sdp_sb", [128, 16], F32)
    wscup_sb = sb(sbP, "wscup_sb", [128, 64], F32)
    sudp_sb = sb(sbP, "sudp_sb", [128, 64], F32)
    negm_sb = sb(sbP, "negm_sb", [128, 2], F32)
    mx_sb = sb(sbP, "mx_sb", [1, 80], F32)
    st1_sb = sb(sbP, "st1_sb", [1, 1024], F32)
    st2_sb = sb(sbP, "st2_sb", [1, 1024], F32)
    st3_sb = sb(sbP, "st3_sb", [1, 1024], F32)
    epsv_sb = sb(sbP, "epsv_sb", [128, 2], F32)

    sbA = contextlib.ExitStack()
    kvbw_sb = sb(sbA, "kvbw_sb", [128, 4 * 512], BF16)
    lat_sb = [sb(sbA, f"lat_sb{i}", [128, 4 * 512], BF16) for i in range(4)]
    nsq_sb = sb(sbA, "nsq_sb", [128, 512], BF16)
    Q_sb = [sb(sbA, f"q_sbh{h}", [128, S], BF16) for h in range(2)]

    sbA2 = contextlib.ExitStack()
    cs_cos = sb(sbA2, "cs_cos", [64, S], BF16, side="right")
    cs_sin = sb(sbA2, "cs_sin", [64, S], BF16, side="right")
    rt1_sb = sb(sbA2, "rt1_sb", [64, 1024], BF16, side="right")

    sbA1 = contextlib.ExitStack()
    qw_sb = sb(sbA1, "qw_sb", [128, 16 * 256], BF16, side="right")
    xt_sb = [sb(sbA1, f"xt_sb{i}", [128, 16 * 512], BF16, side="right")
             for i in range(2)]

    sbA0 = contextlib.ExitStack()
    xo_sb = sb(sbA0, "xo_sb", [128, 16 * TOK], BF16, side="right")
    kvaw_sb = sb(sbA0, "kvaw_sb", [128, 16 * 512], BF16, side="right")
    kvc_sb = sb(sbA0, "kvc_sb", [128, 8 * 512], BF16, side="right")
    sq_sb = sb(sbA0, "sq_sb", [128, 512], BF16, side="right")
    latst_sb = sb(sbA0, "latst_sb", [128, 512], BF16, side="right")
    st1f_sb = sb(sbA0, "st1f_sb", [128, 512], F32, side="right")
    st2f_sb = sb(sbA0, "st2f_sb", [128, 512], F32, side="right")

    psA_b = contextlib.ExitStack()
    bc1_ps = ps(psA_b, "bc1_ps", [128, 512])
    nrm_ps = ps(psA_b, "nrm_ps", [1, 512])
    rt2_ps = ps(psA_b, "rt2_ps", [64, 1024])
    psA_a = contextlib.ExitStack()
    kv_ps = ps(psA_a, "kv_ps", [128, 512])
    var_ps = ps(psA_a, "var_ps", [128, 512])
    q_ps = [ps(psA_a, f"q_ps{h}", [128, 512]) for h in range(2)]

    DMA("y", h128_sb[:], h128.ap())
    DMA("y", onesb_sb[:], onesb.ap())
    DMA("y", onesf_sb[:], onesf.ap())
    DMA("y", kvnw_sb[:], kvnw.ap())
    DMA("y", suug_sb[:], suug.ap())
    DMA("y", wscup_sb[:], wscup.ap())
    DMA("y", sudp_sb[:], sudp.ap())
    DMA("y", sgate_sb[:], wscg.ap())
    DMA("y", sdp_sb[:], wscdp.ap())
    DMA("y", epsv_sb[:], epsv.ap())

    DMA("g", xo_sb.ap().rearrange("p (i t) -> p i t", i=16),
        xo.ap().rearrange("(i p) t -> p i t", p=128))
    gld_xo = gld.n
    DMA("g", kvaw_sb.ap().rearrange("p (i o) -> p i o", i=16),
        kvaw.ap().rearrange("(i p) o -> p i o", p=128))
    gld_p0 = gld.n
    DMA("g", qw_sb.ap().rearrange("p (i o) -> p i o", i=16),
        qw.ap().rearrange("(i p) o -> p i o", p=128))
    DMA("g", kvbw_sb.ap().rearrange("p (j o) -> p j o", j=4),
        kvbw.ap().rearrange("(j p) o -> p j o", p=128))
    DMA("g", cs_cos[:], cs.ap()[0:64, :])
    DMA("g", cs_sin[:], cs.ap()[64:128, :])
    gld_csq = gld.n

    for i in range(16):
        DMA("y", cc_xl_in.ap()[i * 128:(i + 1) * 128, :],
            xo_sb[:, i * TOK:(i + 1) * TOK],
            wait=[(gld, gld_xo)] if i == 0 else ())

    sld_p0 = sld.n
    VSTT(sup_sb[:], wscup_sb[:], RS2, sudp_sb[:], wait=_w(sld), inc=[(dv, 1)])
    VTS(sgate_sb[:], sgate_sb[:], RS2, wait=_w(sld), inc=[(dv, 1)])
    VTS(sdp_sb[:], sdp_sb[:], RS8, wait=_w(sld), inc=[(dv, 1)])

    last_kv_evac = 0
    last_var_pe = 0
    for t2 in range(2):
        for j in range(4):
            blk = j * 2 + t2
            for i in range(16):
                w0 = []
                if i == 0:
                    w0 = [(gld, gld_p0)]
                    if last_kv_evac:
                        w0.append((dv, last_kv_evac))
                MM(kv_ps[:, 0:512],
                   kvaw_sb[:, i * 512 + j * 128: i * 512 + (j + 1) * 128],
                   xo_sb[:, i * TOK + t2 * 512: i * TOK + t2 * 512 + 512],
                   start=(i == 0), stop=(i == 15),
                   wait=w0, inc=[(pe, 1)] if i == 15 else ())
            VCP(kvc_sb[:, blk * 512:(blk + 1) * 512], kv_ps[:, 0:512],
                wait=_w(pe), inc=[(dv, 1)])
            last_kv_evac = dv.n
            wsq = _w(dv)
            if last_var_pe:
                wsq.append((pe, last_var_pe))
            else:
                wsq.append((sld, sld_p0))
            ACT(sq_sb[:], kvc_sb[:, blk * 512:(blk + 1) * 512], AF.Square,
                wait=wsq, inc=[(ac, 1)])
            MM(var_ps[:, 0:512], onesb_sb[:, 0:128], sq_sb[:],
               start=(j == 0), stop=(j == 3),
               wait=_w(ac), inc=[(pe, 1)])
            last_var_pe = pe.n
        ACT(st1f_sb[:], var_ps[:, 0:512], AF.Ln,
            bias=epsv_sb[:, 0:1], scale=1.0 / KVR, wait=_w(pe, dv),
            inc=[(ac, 1)])
        ACT(st2f_sb[:], st1f_sb[:], AF.Exp, scale=-0.5, inc=[(ac, 1)])
        for j in range(4):
            blk = j * 2 + t2
            wl = _w(ac, sld)
            VSTT(latst_sb[:], kvc_sb[:, blk * 512:(blk + 1) * 512],
                 kvnw_sb[:, j: j + 1], st2f_sb[:],
                 wait=wl, inc=[(dv, 1)])
            DMA("y", cc_xl_in.ap()[D + j * 128: D + (j + 1) * 128,
                                   t2 * 512: t2 * 512 + 512],
                latst_sb[:], wait=_w(dv))
    sld_lat = sld.n

    b.add("g", (lambda e: e.collective_compute(
        "AllGather", ALU.bypass, replica_groups=[list(range(NCORES))],
        ins=[cc_xl_in.ap().opt()], outs=[cc_xl_out.ap().opt()])),
        [(sld, sld_lat)], [(cc1, 1)])

    xv = cc_xl_out.ap().rearrange("(r i p) t -> p r i t", r=NCORES, i=20)
    xt_used_pe = [0, 0]
    q_evac_ac = [0, 0]
    xt_sld = [0] * NT

    def emit_xt_dma(tt):
        bi = tt % 2
        w0 = [(cc1, 1)]
        if xt_used_pe[bi]:
            w0.append((pe, xt_used_pe[bi]))
        DMA("y", xt_sb[bi].ap().rearrange("p (i t) -> p i t", i=16),
            xv[:, tt // 2, 0:16, (tt % 2) * 512:(tt % 2) * 512 + 512],
            wait=w0)
        xt_sld[tt] = sld.n

    emit_xt_dma(0)
    emit_xt_dma(1)
    qe = [[0, 0] for _ in range(NT)]
    for tt in range(NT):
        bi = tt % 2
        for h in range(2):
            for i in range(16):
                w0 = []
                if i == 0:
                    w0 = [(sld, xt_sld[tt])]
                    if q_evac_ac[h]:
                        w0.append((ac, q_evac_ac[h]))
                MM(q_ps[h][:, 0:512],
                   qw_sb[:, i * 256 + h * 128: i * 256 + (h + 1) * 128],
                   xt_sb[bi][:, i * 512:(i + 1) * 512],
                   start=(i == 0), stop=(i == 15),
                   wait=w0, inc=[(pe, 1)] if i == 15 else ())
            ACT(Q_sb[h][:, tt * 512:(tt + 1) * 512], q_ps[h][:, 0:512],
                AF.Copy, scale=SCALE, wait=_w(pe), inc=[(ac, 1)])
            q_evac_ac[h] = ac.n
            qe[tt][h] = ac.n
        xt_used_pe[bi] = pe.n
        if tt + 2 < NT:
            emit_xt_dma(tt + 2)

    rope_prev_dv = [0]

    def rope_step(tgt, stp, ac_val):
        sl = slice(stp * 1024, stp * 1024 + 1024)
        wg = [(ac, ac_val), (gld, gld_csq)]
        if rope_prev_dv[0]:
            wg.append((dv, rope_prev_dv[0]))
        b.add("g", (lambda e, o=rt1_sb[:], a0=tgt[0:64, sl],
                    a1=cs_cos[:, sl]:
                    e.tensor_tensor(o, a0, a1, ALU.mult)), wg, [(gp, 1)])
        VTT(ALU.mult, rt2_ps[:], tgt[0:64, sl], cs_sin[:, sl],
            wait=[(ac, ac_val), (gld, gld_csq)])
        VTT(ALU.add, tgt[0:32, sl], rt1_sb[0:32, :], rt2_ps[32:64, :],
            wait=_w(gp))
        VTT(ALU.add, tgt[32:64, sl], rt1_sb[32:64, :], rt2_ps[0:32, :],
            inc=[(dv, 1)])
        rope_prev_dv[0] = dv.n
        return dv.n

    nrm_state = [0, 0]

    def normmax_tile(tgt, mxslot, tt, rope_dv_val):
        ACT(nsq_sb[:], tgt[:, tt * 512:(tt + 1) * 512], AF.Square,
            wait=[(dv, rope_dv_val), (pe, nrm_state[0])], inc=[(ac, 1)])
        wm = _w(ac)
        if nrm_state[1]:
            wm.append((dv, nrm_state[1]))
        MM(nrm_ps[:, 0:512], onesb_sb[:, 0:1], nsq_sb[:],
           wait=wm, inc=[(pe, 1)])
        nrm_state[0] = pe.n
        VRMAX(mx_sb[:, mxslot: mxslot + 1], nrm_ps[:, 0:512],
              wait=_w(pe), inc=[(dv, 1)])
        nrm_state[1] = dv.n

    for s in range(8):
        d0 = rope_step(Q_sb[0], s, qe[2 * s + 1][0])
        d1 = rope_step(Q_sb[1], s, qe[2 * s + 1][1])
        for tt in (2 * s, 2 * s + 1):
            normmax_tile(Q_sb[0], 0 + tt, tt, d0)
            normmax_tile(Q_sb[1], 16 + tt, tt, d1)

    sbA0.close()
    sbA1.close()
    psA_a.close()

    sbA_kv = contextlib.ExitStack()
    K_sb = [sb(sbA_kv, f"k_sbh{h}", [128, S], BF16) for h in range(2)]
    Vt_sb = [sb(sbA_kv, f"vt_sbh{h}", [128, S], BF16) for h in range(2)]
    psA3 = contextlib.ExitStack()
    k_ps = [ps(psA3, f"k_ps{h}", [128, 512]) for h in range(2)]
    vt_ps = [ps(psA3, f"vt_ps{h}", [128, 512]) for h in range(2)]

    lat_used_pe = [0, 0, 0, 0]
    kv_evac = [0, 0]
    ke = [[0, 0] for _ in range(NT)]
    for tt in range(NT):
        bi = tt % 4
        r = tt // 2
        for j in range(4):
            w0 = []
            if j == 0:
                if tt == 0:
                    w0 = [(cc1, 1)]
                elif lat_used_pe[bi]:
                    w0 = [(pe, lat_used_pe[bi])]
            DMA("g", lat_sb[bi][:, j * 512:(j + 1) * 512],
                xv[:, r, 16 + j, (tt % 2) * 512: (tt % 2) * 512 + 512],
                wait=w0)
        gld_tt = gld.n
        for h in range(2):
            for j in range(4):
                w0 = []
                if j == 0:
                    w0 = [(gld, gld_tt)]
                    if kv_evac[h]:
                        w0.append((ac, kv_evac[h]))
                MM(k_ps[h][:, 0:512],
                   kvbw_sb[:, j * 512 + h * 256: j * 512 + h * 256 + 128],
                   lat_sb[bi][:, j * 512:(j + 1) * 512],
                   start=(j == 0), stop=(j == 3),
                   wait=w0, inc=[(pe, 1)] if j == 3 else ())
            for tc in range(4):
                for j in range(4):
                    MM(vt_ps[h][:, tc * 128:(tc + 1) * 128],
                       lat_sb[bi][:, j * 512 + tc * 128: j * 512 + (tc + 1) * 128],
                       kvbw_sb[:, j * 512 + h * 256 + 128:
                               j * 512 + h * 256 + 256],
                       start=(j == 0), stop=(j == 3),
                       inc=[(pe, 1)] if (tc == 3 and j == 3) else ())
            ACT(K_sb[h][:, tt * 512:(tt + 1) * 512], k_ps[h][:, 0:512],
                AF.Copy, wait=_w(pe), inc=[(ac, 1)])
            ke[tt][h] = ac.n
            ACT(Vt_sb[h][:, tt * 512:(tt + 1) * 512], vt_ps[h][:, 0:512],
                AF.Copy, inc=[(ac, 1)])
            kv_evac[h] = ac.n
        lat_used_pe[bi] = pe.n

    for s in range(8):
        d0 = rope_step(K_sb[0], s, ke[2 * s + 1][0])
        d1 = rope_step(K_sb[1], s, ke[2 * s + 1][1])
        for tt in (2 * s, 2 * s + 1):
            normmax_tile(K_sb[0], 32 + tt, tt, d0)
            normmax_tile(K_sb[1], 48 + tt, tt, d1)

    for h in range(2):
        VRMAX(mx_sb[:, 64 + h: 64 + h + 1], mx_sb[:, h * 16: h * 16 + 16],
              wait=_w(dv))
        VRMAX(mx_sb[:, 66 + h: 66 + h + 1],
              mx_sb[:, 32 + h * 16: 32 + h * 16 + 16],
              inc=[(dv, 1)] if h == 1 else ())
    ACT(mx_sb[:, 68:72], mx_sb[:, 64:68], AF.Ln, wait=_w(dv), inc=[(ac, 1)])
    VTT(ALU.add, mx_sb[:, 72:74], mx_sb[:, 68:70], mx_sb[:, 70:72],
        wait=_w(ac), inc=[(dv, 1)])
    ACT(mx_sb[:, 74:76], mx_sb[:, 72:74], AF.Exp, scale=0.5, wait=_w(dv),
        inc=[(ac, 1)])
    VTS(mx_sb[:, 76:78], mx_sb[:, 74:76], -1.0, wait=_w(ac), inc=[(dv, 1)])
    for h in range(2):
        MM(bc1_ps[:, 0:1], onesf_sb[:], mx_sb[:, 76 + h: 76 + h + 1],
           wait=_w(dv), inc=[(pe, 1)])
        VCP(negm_sb[:, h: h + 1], bc1_ps[:, 0:1], wait=_w(pe), inc=[(dv, 1)])

    if "sg" in tap_outs:
        DMA("y", tap_outs["sg"].ap(), sgate_sb[:], wait=_w(dv))
    if "sup" in tap_outs:
        DMA("y", tap_outs["sup"].ap(), sup_sb[:], wait=_w(dv))
    if "mx" in tap_outs:
        DMA("y", tap_outs["mx"].ap(), mx_sb[:], wait=_w(dv, ac, pe))
    if "q0" in tap_outs:
        DMA("y", tap_outs["q0"].ap(), Q_sb[0][:], wait=_w(dv, ac))
    if "k0" in tap_outs:
        DMA("y", tap_outs["k0"].ap(), K_sb[0][:], wait=_w(dv, ac))
    if "vt0" in tap_outs:
        DMA("y", tap_outs["vt0"].ap(), Vt_sb[0][:], wait=_w(dv, ac))
    if "negm" in tap_outs:
        DMA("y", tap_outs["negm"].ap(), negm_sb[:], wait=_w(dv))
    if "lat" in tap_outs:
        for j in range(4):
            DMA("y", tap_outs["lat"].ap()[:, j * 1024:(j + 1) * 1024],
                cc_xl_out.ap()[D + j * 128: D + (j + 1) * 128, 0:1024],
                wait=[(cc1, 1)])

    sbA2.close()
    psA3.close()
    psA_b.close()
    psB = contextlib.ExitStack()
    s_ps = [ps(psB, f"s_ps{i}", [128, 2 * 512]) for i in range(3)]
    pv_ps = ps(psB, "pv_ps", [128, 512])
    dd_ps = ps(psB, "dd_ps", [128, 512])
    sbA4 = contextlib.ExitStack()
    p_sb = [sb(sbA4, f"p_sb{i}", [128, 2 * 512], BF16) for i in range(3)]
    padd_sb = [sb(sbA4, f"padd{i}", [128, 512], BF16) for i in range(3)]
    araw_sb = [sb(sbA4, f"araw{i}", [128, 512], BF16) for i in range(2)]
    asb_sb = [sb(sbA4, f"asbx{i}", [128, 512], BF16) for i in range(2)]
    rec_sb = [sb(sbA4, f"rec{i}", [128, 512], F32) for i in range(2)]

    NGRP = 32
    s_bank_free = [0, 0, 0]
    p_buf_free = [0, 0, 0]
    araw_free_dv = [0, 0]
    asb_free_sld = [0, 0]
    prev_mult_dv = 0
    prev_recip_dv = 0
    pre_attn = _w(pe, ac, dv, sld, gld)
    attn_entry = _w(dv, ac, pe)
    first_attn = True
    for parity in range(2):
      for h in range(2):
        for qt in range(parity, NT, 2):
            exp_sac = [0] * NGRP
            padd_dv = [0] * NGRP

            def emit_S(g):
                nonlocal first_attn
                bi = g % 3
                for i in range(2):
                    w0 = []
                    if i == 0:
                        if first_attn:
                            w0 += attn_entry
                            first_attn = False
                        if s_bank_free[bi]:
                            w0.append((sac, s_bank_free[bi]))
                    kcc = g * 2 + i
                    MM(s_ps[bi][:, i * 512:(i + 1) * 512],
                       K_sb[h][:, kcc * 128:(kcc + 1) * 128],
                       Q_sb[h][:, qt * 512:(qt + 1) * 512],
                       start=True, stop=True,
                       wait=w0, inc=[(spe, 1)] if i == 1 else ())

            def emit_exp(g):
                bi = g % 3
                we = _w(spe)
                if p_buf_free[bi]:
                    we.append((spv, p_buf_free[bi]))
                ACT(p_sb[bi][:, 0:1024], s_ps[bi][:, 0:1024], AF.Exp,
                    bias=negm_sb[:, h: h + 1], wait=we, inc=[(sac, 1)])
                s_bank_free[bi] = sac.n
                exp_sac[g] = sac.n

            def emit_padd(g):
                bi = g % 3
                wp = [(sac, exp_sac[g])]
                if p_buf_free[bi]:
                    wp.append((spv, p_buf_free[bi]))
                VTT(ALU.add, padd_sb[bi][:], p_sb[bi][:, 0:512],
                    p_sb[bi][:, 512:1024], wait=wp, inc=[(dv, 1)])
                padd_dv[g] = dv.n

            def emit_PVD(g):
                bi = g % 3
                for i in range(2):
                    kcc = g * 2 + i
                    wpv = []
                    if i == 0:
                        wpv = [(sac, exp_sac[g])]
                        if g == 0 and prev_mult_dv:
                            wpv.append((dv, prev_mult_dv))
                    MM(pv_ps[:, 0:512],
                       Vt_sb[h][:, kcc * 128:(kcc + 1) * 128],
                       p_sb[bi][:, i * 512:(i + 1) * 512],
                       start=(g == 0 and i == 0),
                       stop=(g == NGRP - 1 and i == 1),
                       wait=wpv)
                wdd = [(dv, padd_dv[g])]
                if g == 0 and prev_recip_dv:
                    wdd.append((dv, prev_recip_dv))
                MM(dd_ps[:, 0:512], onesb_sb[:, 0:128], padd_sb[bi][:],
                   start=(g == 0), stop=(g == NGRP - 1),
                   wait=wdd, inc=[(spv, 1)])
                p_buf_free[bi] = spv.n

            emit_S(0)
            emit_exp(0)
            emit_padd(0)
            emit_S(1)
            emit_exp(1)
            emit_padd(1)
            for g in range(2, NGRP):
                emit_S(g)
                emit_exp(g)
                emit_padd(g)
                emit_PVD(g - 2)
            emit_PVD(NGRP - 2)
            emit_PVD(NGRP - 1)

            abi = qt % 2
            wq = _w(spv)
            if araw_free_dv[abi]:
                wq.append((dv, araw_free_dv[abi]))
            VCP(araw_sb[abi][:], pv_ps[:, 0:512], wait=wq, inc=[(dv, 1)])
            prev_mult_dv = dv.n
            VRECIP(rec_sb[abi][:], dd_ps[:, 0:512], inc=[(dv, 1)])
            prev_recip_dv = dv.n
            wv = _w(dv)
            if asb_free_sld[abi]:
                wv.append((sld, asb_free_sld[abi]))
            VTT(ALU.mult, asb_sb[abi][:], araw_sb[abi][:],
                rec_sb[abi][:], wait=wv, inc=[(dv, 1)])
            araw_free_dv[abi] = dv.n
            blk = qt // 2
            cc_t = cc_a_in0 if parity == 0 else cc_a_in1
            DMA("y", cc_t.ap()[h * 128:(h + 1) * 128,
                               blk * 512:(blk + 1) * 512],
                asb_sb[abi][:], wait=_w(dv))
            asb_free_sld[abi] = sld.n
            if "asb" in tap_outs and h == 0 and qt == 0:
                DMA("y", tap_outs["asb"].ap(), asb_sb[abi][:], wait=_w(dv))
      if parity == 0:
          sld_a0 = sld.n
          b.add("g", (lambda e: e.collective_compute(
              "AllGather", ALU.bypass, replica_groups=[list(range(NCORES))],
              ins=[cc_a_in0.ap().opt()], outs=[cc_a_out0.ap().opt()])),
              [(sld, sld_a0), (cc1, 1)], [(cc2, 1)])

    sld_a1 = sld.n
    b.add("g", (lambda e: e.collective_compute(
        "AllGather", ALU.bypass, replica_groups=[list(range(NCORES))],
        ins=[cc_a_in1.ap().opt()], outs=[cc_a_out1.ap().opt()])),
        [(sld, sld_a1), (cc2, 1)], [(cc2b, 1)])

    end_attn = _w(pe, ac, dv, sld, gld)
    sbA4.close()
    sbA_kv.close()
    sbA.close()
    psB.close()

    sbB2 = contextlib.ExitStack()
    xs_sb = sb(sbB2, "xs_sb", [128, 16 * TOK], BF16, side="right")

    sbB3 = contextlib.ExitStack()
    vb_sb = sb(sbB3, "vb_sb", [128, 4 * TOK], BF16, side="right")
    sqb_sb = sb(sbB3, "sqb_sb", [128, 4 * TOK], BF16, side="right")
    mbc_sb = sb(sbB3, "mbc_sb", [128, TOK], F32, side="right")
    rbc_sb = sb(sbB3, "rbc_sb", [128, TOK], F32, side="right")
    t1f_sb = sb(sbB3, "t1f_sb", [128, TOK], F32, side="right")

    sbB1 = contextlib.ExitStack()
    aown_sb = sb(sbB1, "aown_sb", [128, 16 * TOK], BF16, side="right")
    v1_sb = sb(sbB1, "v1_sb", [128, 16 * TOK], F32, side="right")
    owb_sb = [sb(sbB1, f"owb{i}", [128, 16 * 128], BF16, side="right")
              for i in range(2)]
    xr_sb = [sb(sbB1, f"xr{i}", [128, TOK], F32, side="right")
             for i in range(2)]

    psC1 = contextlib.ExitStack()
    o_ps = ps(psC1, "o_ps", [128, 1024])
    psC2 = contextlib.ExitStack()
    mean_ps = ps(psC2, "mean_ps", [128, 1024])
    msq_ps = ps(psC2, "msq_ps", [128, 1024])

    for i in range(16):
        src = cc_a_out0.ap()[i * 128:(i + 1) * 128, 0:512]
        src = dataclasses.replace(src, offset=src.offset + pid512)
        DMA("y", aown_sb[:, i * TOK: i * TOK + 512], src,
            wait=[(cc2, 1)] if i == 0 else ())
    sld_aown0 = sld.n
    for i in range(16):
        src = cc_a_out1.ap()[i * 128:(i + 1) * 128, 0:512]
        src = dataclasses.replace(src, offset=src.offset + pid512)
        DMA("y", aown_sb[:, i * TOK + 512: i * TOK + 1024], src,
            wait=[(cc2b, 1)] if i == 0 else ())
    sld_aown1 = sld.n

    ow_v = ow.ap().rearrange("(i p) o -> p i o", p=128)
    ow_used = [0, 0]
    o_done_dv = 0
    stats_pe = 0
    for jo in range(16):
        bi = jo % 2
        DMA("g", owb_sb[bi].ap().rearrange("p (i o) -> p i o", i=16),
            ow_v[:, :, jo * 128:(jo + 1) * 128],
            wait=[(pe, ow_used[bi])] if ow_used[bi] else pre_attn)
        gld_ow = gld.n
        DMA("y", xr_sb[bi][:], xo.ap()[jo * 128:(jo + 1) * 128, :],
            wait=[(dv, o_done_dv)] if o_done_dv else ())
        sld_xr = sld.n
        for h2 in range(2):
            for i in range(16):
                w0 = []
                if i == 0:
                    w0 = [(gld, gld_ow),
                          (sld, sld_aown0 if h2 == 0 else sld_aown1)]
                    if h2 == 0 and o_done_dv:
                        w0.append((dv, o_done_dv))
                MM(o_ps[:, h2 * 512: h2 * 512 + 512],
                   owb_sb[bi][:, i * 128:(i + 1) * 128],
                   aown_sb[:, i * TOK + h2 * 512: i * TOK + h2 * 512 + 512],
                   start=(i == 0), stop=(i == 15),
                   wait=w0,
                   inc=[(pe, 1)] if (i == 15 and h2 == 1) else ())
        ow_used[bi] = pe.n
        VTT(ALU.add, v1_sb[:, jo * TOK:(jo + 1) * TOK], o_ps[:], xr_sb[bi][:],
            wait=_w(pe) + [(sld, sld_xr)], inc=[(dv, 1)])
        o_done_dv = dv.n
        q4 = jo % 4
        VCP(vb_sb[:, q4 * TOK:(q4 + 1) * TOK],
            v1_sb[:, jo * TOK:(jo + 1) * TOK],
            wait=[(pe, stats_pe)] if (q4 == 0 and stats_pe) else (),
            inc=[(dv, 1)])
        ACT(sqb_sb[:, q4 * TOK:(q4 + 1) * TOK],
            v1_sb[:, jo * TOK:(jo + 1) * TOK], AF.Square,
            wait=_w(dv), inc=[(ac, 1)])
        if q4 == 3:
            for qq in range(4):
                jj = jo - 3 + qq
                for h2 in range(2):
                    off = qq * TOK + h2 * 512
                    MM(mean_ps[:, h2 * 512: h2 * 512 + 512],
                       onesb_sb[:, 0:128], vb_sb[:, off: off + 512],
                       start=(jj == 0), stop=(jj == 15),
                       wait=_w(dv, ac) if (qq == 0 and h2 == 0) else ())
                    MM(msq_ps[:, h2 * 512: h2 * 512 + 512],
                       onesb_sb[:, 0:128], sqb_sb[:, off: off + 512],
                       start=(jj == 0), stop=(jj == 15),
                       inc=[(pe, 1)] if (qq == 3 and h2 == 1) else ())
            stats_pe = pe.n

    def ln_stats(mean_src, msq_src, mbc_t, rbc_t, sca, scb):
        VTS(mbc_t, mean_src[:, 0:1024], 1.0 / D, wait=_w(pe),
            inc=[(dv, 1)])
        VTS(sca, msq_src[:, 0:1024], 1.0 / D, inc=[(dv, 1)])
        ACT(scb, mbc_t, AF.Square, wait=_w(dv), inc=[(ac, 1)])
        VTT(ALU.subtract, sca, sca, scb, wait=_w(ac), inc=[(dv, 1)])
        ACT(scb, sca, AF.Ln, bias=epsv_sb[:, 1:2], wait=_w(dv),
            inc=[(ac, 1)])
        ACT(rbc_t, scb, AF.Exp, scale=-0.5, inc=[(ac, 1)])

    ln1sc = sqb_sb.ap().bitcast(F32)
    ln_stats(mean_ps, msq_ps, mbc_sb[:], rbc_sb[:],
             ln1sc[:, 0:1024], ln1sc[:, 1024:2048])
    psC2.close()
    psC3 = contextlib.ExitStack()
    o2_ps = ps(psC3, "o2_ps", [128, 1024])

    for jo in range(16):
        sl = slice(jo * TOK, (jo + 1) * TOK)
        VTT(ALU.subtract, t1f_sb[:], v1_sb[:, sl], mbc_sb[:],
            wait=_w(dv, ac))
        VTT(ALU.mult, v1_sb[:, sl], t1f_sb[:], rbc_sb[:], inc=[(dv, 1)])
        DMA("y", x1_sp.ap()[jo * 128:(jo + 1) * 128, :], v1_sb[:, sl],
            wait=_w(dv))
        VTS(xs_sb[:, sl], v1_sb[:, sl], suug_sb[:, jo: jo + 1],
            inc=[(dv, 1)])
        lvl = 0
        while (jo >> lvl) & 1:
            wdt2 = (1 << lvl) * TOK
            base = (jo - (2 << lvl) + 1) * TOK
            a_ap = xs_sb[:, base: base + wdt2]
            b_ap = xs_sb[:, base + wdt2: base + 2 * wdt2]
            VTT(ALU.add, a_ap, a_ap, b_ap)
            VSTT(b_ap, b_ap, -2.0, a_ap, ALU.mult, ALU.add, inc=[(dv, 1)])
            lvl += 1
    sld_x1 = sld.n

    def butterflies(tgt, nck, width):
        stv = 1
        while stv < nck:
            hi = nck // (2 * stv)
            blkw = stv * width
            if 2 * blkw <= 16384:
                v3 = tgt.ap().rearrange("p (hi rest) -> p hi rest", hi=hi)
                a_ap = v3[:, :, 0:blkw]
                b_ap = v3[:, :, blkw: 2 * blkw]
                VTT(ALU.add, a_ap, a_ap, b_ap, wait=_w(dv))
                VSTT(b_ap, b_ap, -2.0, a_ap, ALU.mult, ALU.add,
                     inc=[(dv, 1)])
            else:
                for g2 in range(hi):
                    base = g2 * 2 * blkw
                    a_ap = tgt[:, base: base + blkw]
                    b_ap = tgt[:, base + blkw: base + 2 * blkw]
                    VTT(ALU.add, a_ap, a_ap, b_ap,
                        wait=_w(dv) if g2 == 0 else ())
                    VSTT(b_ap, b_ap, -2.0, a_ap, ALU.mult, ALU.add,
                         inc=[(dv, 1)] if g2 == hi - 1 else ())
            stv *= 2

    h16_evac = [0, 0]
    for c in range(16):
        hps = o_ps if c % 2 == 0 else o2_ps
        for h2 in range(2):
            w0 = []
            if h2 == 0:
                w0 = _w(dv)
                if h16_evac[c % 2]:
                    w0.append((ac, h16_evac[c % 2]))
            MM(hps[:, h2 * 512: h2 * 512 + 512], h128_sb[:],
               xs_sb[:, c * TOK + h2 * 512: c * TOK + h2 * 512 + 512],
               wait=w0, inc=[(pe, 1)] if h2 == 1 else ())
        ACT(xs_sb[:, c * TOK:(c + 1) * TOK], hps[:], AF.Copy, wait=_w(pe),
            inc=[(ac, 1)])
        h16_evac[c % 2] = ac.n

    if "aown" in tap_outs:
        DMA("y", tap_outs["aown"].ap(), aown_sb[:], wait=[(sld, sld_aown1)])
    if "xs" in tap_outs:
        DMA("y", tap_outs["xs"].ap(), xs_sb[:], wait=_w(dv))

    p6_end = _w(pe, ac, dv, sld, gld)
    sbB1.close()
    sbB3.close()
    psC3.close()
    psC1.close()

    sbC = contextlib.ExitStack()
    u_sb = sb(sbC, "u_sb", [128, 64 * TOK], BF16)
    sbC_w = contextlib.ExitStack()
    wub_sb = [sb(sbC_w, f"wub{i}", [128, 16 * 128], BF16) for i in range(2)]
    wgb_sb = [sb(sbC_w, f"wgb{i}", [128, 16 * 128], BF16) for i in range(2)]
    gsil_sb = sb(sbC_w, "gsil_sb", [128, 1024], F32)

    psD = contextlib.ExitStack()
    up_ps = [ps(psD, "up_ps0", [128, 1024])]
    psD1 = contextlib.ExitStack()
    up_ps.append(ps(psD1, "up_ps1", [128, 1024]))
    psDg = contextlib.ExitStack()
    gt_ps = [ps(psDg, f"gt_ps{i}", [128, 1024]) for i in range(2)]

    wup_v = wup.ap().rearrange("(i p) o -> p i o", p=128)
    wg_v = wg.ap().rearrange("(i p) o -> p i o", p=128)
    ug_used = [0, 0]
    u_done = [0, 0]
    u_done_any = 0
    gt_free = [0, 0]
    for jo in range(64):
        bi = jo % 2
        DMA("g", wub_sb[bi].ap().rearrange("p (i o) -> p i o", i=16),
            wup_v[:, :, jo * 128:(jo + 1) * 128],
            wait=[(pe, ug_used[bi])] if ug_used[bi] else p6_end)
        DMA("g", wgb_sb[bi].ap().rearrange("p (i o) -> p i o", i=16),
            wg_v[:, :, jo * 128:(jo + 1) * 128])
        gld_b = gld.n
        for h2 in range(2):
            for i in range(16):
                w0 = []
                if i == 0 and h2 == 0:
                    w0 = [(gld, gld_b)]
                    if jo < 2:
                        w0 += p6_end
                    if u_done[bi]:
                        w0.append((dv, u_done[bi]))
                MM(up_ps[bi][:, h2 * 512: h2 * 512 + 512],
                   wub_sb[bi][:, i * 128:(i + 1) * 128],
                   xs_sb[:, i * TOK + h2 * 512: i * TOK + h2 * 512 + 512],
                   start=(i == 0), stop=(i == 15),
                   wait=w0,
                   inc=[(pe, 1)] if (i == 15 and h2 == 1) else ())
        for h2 in range(2):
            for i in range(16):
                w0 = []
                if i == 0 and h2 == 0 and gt_free[bi]:
                    w0 = [(ac, gt_free[bi])]
                MM(gt_ps[bi][:, h2 * 512: h2 * 512 + 512],
                   wgb_sb[bi][:, i * 128:(i + 1) * 128],
                   xs_sb[:, i * TOK + h2 * 512: i * TOK + h2 * 512 + 512],
                   start=(i == 0), stop=(i == 15),
                   wait=w0,
                   inc=[(pe, 1)] if (i == 15 and h2 == 1) else ())
        ug_used[bi] = pe.n
        wsil = _w(pe)
        if u_done_any:
            wsil.append((dv, u_done_any))
        ACT(gsil_sb[:], gt_ps[bi][:], AF.Silu, scale=sgate_sb[:, jo: jo + 1],
            wait=wsil, inc=[(ac, 1)])
        gt_free[bi] = ac.n
        VSTT(u_sb[:, jo * TOK:(jo + 1) * TOK], up_ps[bi][:],
             sup_sb[:, jo: jo + 1], gsil_sb[:], wait=_w(ac), inc=[(dv, 1)])
        u_done[bi] = dv.n
        u_done_any = dv.n
        lvl = 0
        while CASCADE and ((jo >> lvl) & 1) and lvl < 5:
            wdt2 = (1 << lvl) * TOK
            base = (jo - (2 << lvl) + 1) * TOK
            a_ap = u_sb[:, base: base + wdt2]
            b_ap = u_sb[:, base + wdt2: base + 2 * wdt2]
            VTT(ALU.add, a_ap, a_ap, b_ap)
            VSTT(b_ap, b_ap, -2.0, a_ap, ALU.mult, ALU.add, inc=[(dv, 1)])
            lvl += 1

    if "u0" in tap_outs:
        for c in range(8):
            DMA("y", tap_outs["u0"].ap()[:, c * TOK:(c + 1) * TOK],
                u_sb[:, c * TOK:(c + 1) * TOK], wait=_w(dv))

    if not CASCADE:
        butterflies(u_sb, 64, TOK)

    h_evac_ac = [0, 0]
    for p4 in range(4):
        base_a = p4 * 8 * TOK
        base_b = (32 + p4 * 8) * TOK
        a_ap = u_sb[:, base_a: base_a + 8 * TOK]
        b_ap = u_sb[:, base_b: base_b + 8 * TOK]
        VTT(ALU.add, a_ap, a_ap, b_ap, wait=_w(dv) if p4 == 0 else ())
        VSTT(b_ap, b_ap, -2.0, a_ap, ALU.mult, ALU.add, inc=[(dv, 1)])
        piece_dv = dv.n
        for c in (list(range(p4 * 8, p4 * 8 + 8))
                  + list(range(32 + p4 * 8, 32 + p4 * 8 + 8))):
            bb = c % 2
            for h2 in range(2):
                w0 = []
                if h2 == 0:
                    w0 = [(dv, piece_dv)]
                    if h_evac_ac[bb]:
                        w0.append((ac, h_evac_ac[bb]))
                MM(up_ps[bb][:, h2 * 512: h2 * 512 + 512], h128_sb[:],
                   u_sb[:, c * TOK + h2 * 512: c * TOK + h2 * 512 + 512],
                   wait=w0, inc=[(pe, 1)] if h2 == 1 else ())
            ACT(u_sb[:, c * TOK:(c + 1) * TOK], up_ps[bb][:], AF.Copy,
                wait=_w(pe), inc=[(ac, 1)])
            h_evac_ac[bb] = ac.n
    u_ready_ac = ac.n

    if "u" in tap_outs:
        for c in range(8):
            DMA("y", tap_outs["u"].ap()[:, c * TOK:(c + 1) * TOK],
                u_sb[:, c * TOK:(c + 1) * TOK], wait=_w(ac))

    psDg.close()
    psD1.close()
    sbB2.close()
    sbC_w.close()
    sbD = contextlib.ExitStack()
    wdb_sb = [sb(sbD, f"wdb{i}", [128, 32 * 128], BF16) for i in range(2)]
    x1r_sb = [sb(sbD, f"x1r{i}", [128, TOK], F32) for i in range(2)]
    v2t_sb = sb(sbD, "v2t_sb", [128, TOK], F32)
    out_sb = [sb(sbD, f"outsb{i}", [128, TOK], F32) for i in range(2)]
    vb2_sb = sb(sbD, "vb2_sb", [128, 4 * TOK], BF16)
    sqb2_sb = sb(sbD, "sqb2_sb", [128, 4 * TOK], BF16)
    mbc2_sb = sb(sbD, "mbc2_sb", [128, TOK], F32)
    rbc2_sb = sb(sbD, "rbc2_sb", [128, TOK], F32)
    t1f2_sb = sb(sbD, "t1f2_sb", [128, TOK], F32)

    psE = contextlib.ExitStack()
    mean2_ps = ps(psE, "mean2_ps", [128, 1024])
    msq2_ps = ps(psE, "msq2_ps", [128, 1024])

    wd_v = wd.ap().rearrange("(a p) o -> p a o", p=128)
    wd_used = [0, 0]
    v2_done = 0
    stats2_pe = 0
    sq2_ac = 0
    for jo in range(16):
        for half in range(2):
            hb = (jo * 2 + half) % 2
            DMA("g",
                wdb_sb[hb].ap().rearrange("p (a o) -> p a o", a=32),
                wd_v[:, half * 32:(half + 1) * 32, jo * 128:(jo + 1) * 128],
                wait=([(pe, wd_used[hb])] if wd_used[hb]
                      else [(ac, u_ready_ac)]))
            gld_wd = gld.n
            for a2 in range(32):
                for h2 in range(2):
                    a = half * 32 + a2
                    w0 = []
                    if h2 == 0 and a2 == 0:
                        w0 = [(gld, gld_wd), (ac, u_ready_ac)]
                        if half == 0 and v2_done:
                            w0.append((dv, v2_done))
                    MM(up_ps[0][:, h2 * 512: h2 * 512 + 512],
                       wdb_sb[hb][:, a2 * 128:(a2 + 1) * 128],
                       u_sb[:, a * TOK + h2 * 512: a * TOK + h2 * 512 + 512],
                       start=(a == 0), stop=(a == 63),
                       wait=w0,
                       inc=[(pe, 1)] if (h2 == 1 and a2 == 31) else ())
            wd_used[hb] = pe.n
        DMA("y", x1r_sb[jo % 2][:], x1_sp.ap()[jo * 128:(jo + 1) * 128, :],
            wait=[(sld, sld_x1), (ac, u_ready_ac)]
                 + ([(dv, v2_done)] if v2_done else []))
        sld_x1r = sld.n
        VSTT(v2t_sb[:], up_ps[0][:], sdp_sb[:, jo: jo + 1], x1r_sb[jo % 2][:],
             ALU.mult, ALU.add,
             wait=_w(pe) + [(sld, sld_x1r)]
                  + ([(ac, sq2_ac)] if sq2_ac else []),
             inc=[(dv, 1)])
        v2_done = dv.n
        DMA("y", v2_sp.ap()[jo * 128:(jo + 1) * 128, :], v2t_sb[:],
            wait=_w(dv))
        q4 = jo % 4
        VCP(vb2_sb[:, q4 * TOK:(q4 + 1) * TOK], v2t_sb[:],
            wait=[(pe, stats2_pe)] if (q4 == 0 and stats2_pe) else (),
            inc=[(dv, 1)])
        ACT(sqb2_sb[:, q4 * TOK:(q4 + 1) * TOK], v2t_sb[:], AF.Square,
            wait=_w(dv), inc=[(ac, 1)])
        sq2_ac = ac.n
        if q4 == 3:
            for qq in range(4):
                jj = jo - 3 + qq
                for h2 in range(2):
                    off = qq * TOK + h2 * 512
                    MM(mean2_ps[:, h2 * 512: h2 * 512 + 512],
                       onesb_sb[:, 0:128], vb2_sb[:, off: off + 512],
                       start=(jj == 0), stop=(jj == 15),
                       wait=_w(dv, ac) if (qq == 0 and h2 == 0) else ())
                    MM(msq2_ps[:, h2 * 512: h2 * 512 + 512],
                       onesb_sb[:, 0:128], sqb2_sb[:, off: off + 512],
                       start=(jj == 0), stop=(jj == 15),
                       inc=[(pe, 1)] if (qq == 3 and h2 == 1) else ())
            stats2_pe = pe.n
    sld_v2 = sld.n

    ln2sc = sqb2_sb.ap().bitcast(F32)
    ln_stats(mean2_ps, msq2_ps, mbc2_sb[:], rbc2_sb[:],
             ln2sc[:, 0:1024], ln2sc[:, 1024:2048])

    for jo in range(16):
        bi = jo % 2
        DMA("y", x1r_sb[bi][:], v2_sp.ap()[jo * 128:(jo + 1) * 128, :],
            wait=[(sld, sld_v2)] + _w(dv))
        sld_r = sld.n
        VTT(ALU.subtract, t1f2_sb[:], x1r_sb[bi][:], mbc2_sb[:],
            wait=[(sld, sld_r)] + _w(ac))
        VTT(ALU.mult, out_sb[bi][:], t1f2_sb[:], rbc2_sb[:], inc=[(dv, 1)])
        DMA("y", out_t.ap()[jo * 128:(jo + 1) * 128, :], out_sb[bi][:],
            wait=_w(dv))

    b.add("y", (lambda e: None), _w(sld, gld), ())

    sbD.close()
    psE.close()
    psD.close()
    sbC.close()

    with nc.Block() as block:
        block.tensor(lambda e: b.run("t", e))
        block.vector(lambda e: b.run("v", e))
        block.scalar(lambda e: b.run("a", e))
        block.sync(lambda e: b.run("y", e))

        block.gpsimd(lambda e: b.run("g", e))

    ctx.close()
    return nc


def _prep_inputs(ins):
    hs = np.asarray(ins["L0_attn_hidden_states"])[0]
    q_w = np.asarray(ins["L0_attn_q_w"])
    kv_a_w = np.asarray(ins["L0_attn_kv_a_w"])
    kv_b_w = np.asarray(ins["L0_attn_kv_b_w"])
    o_w = np.asarray(ins["L0_attn_o_w"])
    kv_norm_w = np.asarray(ins["L0_attn_kv_norm_w"])
    SU_ug = np.asarray(ins["L0_ffn_SU_ug"])
    SU_dp = np.asarray(ins["L0_ffn_SU_dp"])
    Wsc_ug = np.asarray(ins["L0_ffn_Wscale_ug"])
    Wsc_dp = np.asarray(ins["L0_ffn_Wscale_dp"])
    W_up = np.asarray(ins["L0_ffn_W_up"])
    W_gate = np.asarray(ins["L0_ffn_W_gate"])
    W_down = np.asarray(ins["L0_ffn_W_down"])

    xT = np.ascontiguousarray(hs.T)
    kvawT = np.ascontiguousarray(kv_a_w[RD:, :].T)
    owT = np.ascontiguousarray(o_w.T)
    wupT = np.ascontiguousarray(W_up.T)
    wgT = np.ascontiguousarray(W_gate.T)
    wdT = np.ascontiguousarray(W_down.T)

    cs_tab = _cs_table()
    h128b = _hadamard(128).astype(ml_dtypes.bfloat16)
    onesb = np.ones((128, 128), ml_dtypes.bfloat16)
    onesf = np.ones((1, 128), np.float32)

    kvnw = np.ascontiguousarray(kv_norm_w.reshape(4, 128).T)
    suug = np.ascontiguousarray(SU_ug.reshape(16, 128).T)
    wscup = np.ascontiguousarray(Wsc_ug[:INTER].reshape(64, 128).T)
    wscg = np.ascontiguousarray(Wsc_ug[INTER:].reshape(64, 128).T)
    sudp = np.ascontiguousarray(SU_dp.reshape(64, 128).T)
    wscdp = np.ascontiguousarray(Wsc_dp.reshape(16, 128).T)
    epsv = np.tile(np.array([[EPS_RMS, EPS_LN]], np.float32), (128, 1))

    maps = []
    for c in range(NCORES):
        maps.append({
            "xo": np.ascontiguousarray(xT[:, c * TOK:(c + 1) * TOK]),
            "cs": cs_tab,
            "h128": h128b,
            "onesb": onesb,
            "onesf": onesf,
            "qw": np.ascontiguousarray(q_w[c * 256:(c + 1) * 256, :].T),
            "kvaw": kvawT,
            "kvbw": np.ascontiguousarray(kv_b_w[c * 512:(c + 1) * 512, :].T),
            "ow": owT,
            "wup": wupT,
            "wg": wgT,
            "wd": wdT,
            "kvnw": kvnw,
            "suug": suug,
            "wscup": wscup,
            "wscg": wscg,
            "sudp": sudp,
            "wscdp": wscdp,
            "epsv": epsv,
        })
    return maps


def kernel(**ins):
    maps = _prep_inputs(ins)
    nc = build_nc()
    res = run_bass_kernel_spmd(nc, maps, core_ids=list(range(NCORES)))
    full = np.empty((D, S), np.float32)
    for c in range(NCORES):
        full[:, c * TOK:(c + 1) * TOK] = res.results[c]["out"]
    return np.ascontiguousarray(full.T)[None]


# version 19
# speedup vs baseline: 1.0696x; 1.0696x over previous
import contextlib
import dataclasses
import math

import numpy as np
import ml_dtypes

import concourse.bass as bass
import concourse.mybir as mybir
from concourse.bass_utils import run_bass_kernel_spmd
from concourse.engine_type import EngineType

F32 = mybir.dt.float32
BF16 = mybir.dt.bfloat16
AF = mybir.ActivationFunctionType
ALU = mybir.AluOpType
AX = mybir.AxisListType

NCORES = 8
S = 8192
D = 2048
NH = 16
HD = 128
KVR = 512
RD = 64
TOK = S // NCORES
NT = S // 512
SCALE = float(HD) ** -0.5
EPS_RMS = 1e-6
EPS_LN = 1e-5
INTER = 8192
RS2 = 1.0 / math.sqrt(2048.0)
RS8 = 1.0 / math.sqrt(8192.0)
CASCADE = True


def _yarn_cos_sin():
    MAX_POS, ORIG_POS, BASE = 131072, 8192, 500000.0
    BETA_FAST, BETA_SLOW = 32.0, 1.0
    scaling = MAX_POS / ORIG_POS
    pos_freqs = BASE ** (np.arange(0, RD, 2, dtype=np.float64) / RD)
    extrap = 1.0 / pos_freqs
    interp = 1.0 / (scaling * pos_freqs)
    low = max(math.floor(RD * math.log(ORIG_POS / (BETA_FAST * 2 * math.pi))
                         / (2 * math.log(BASE))), 0)
    high = min(math.ceil(RD * math.log(ORIG_POS / (BETA_SLOW * 2 * math.pi))
                         / (2 * math.log(BASE))), RD - 1)
    i = np.arange(RD // 2, dtype=np.float64)
    smooth = np.clip((i - low) / max(high - low, 1), 0.0, 1.0)
    inv_freq = ((1.0 - smooth) * interp + smooth * extrap).astype(np.float32)
    pos = np.arange(S, dtype=np.float32)
    freqs = pos[:, None] * inv_freq[None, :]
    emb = np.concatenate([freqs, freqs], axis=-1)
    mscale = np.float32(0.1 * math.log(scaling) + 1.0)
    return ((np.cos(emb) * mscale).astype(np.float32),
            (np.sin(emb) * mscale).astype(np.float32))


def _cs_table():
    cos, sin = _yarn_cos_sin()
    cosT, sinT = cos.T, sin.T
    tab = np.empty((128, S), np.float32)
    tab[0:64] = cosT
    tab[64:96] = sinT[32:64]
    tab[96:128] = -sinT[0:32]
    return tab


def _hadamard(n):
    h = np.array([[1.0]], np.float32)
    while h.shape[0] < n:
        h = np.block([[h, h], [h, -h]])
    return h.astype(np.float32)


class Sem:
    def __init__(self, ctx, nc, name):
        self.h = ctx.enter_context(nc.semaphore(name))
        self.n = 0


class B:
    ENGS = ("t", "v", "a", "g", "y")

    def __init__(self):
        self.q = {e: [] for e in B.ENGS}

    def add(self, eng, fn, wait=(), inc=()):
        waits = []
        for w in wait:
            if isinstance(w, Sem):
                waits.append((w.h, w.n))
            else:
                sem, v = w
                waits.append((sem.h if isinstance(sem, Sem) else sem, v))
        incs = []
        for sem, k in inc:
            sem.n += k
            incs.append((sem.h, k))
        self.q[eng].append((waits, fn, incs))

    def run(self, eng_name, eng):
        for waits, fn, incs in self.q[eng_name]:
            for h, v in waits:
                if v > 0:
                    eng.wait_ge(h, v)
            ins = fn(eng)
            for h, k in incs:
                ins.then_inc(h, k)


def _w(*sems):
    return [(s, s.n) for s in sems]


def build_nc(taps=()):
    nc = bass.Bass("TRN2", target_bir_lowering=False, debug=False)

    def din(name, shape, dtype=F32):
        return nc.dram_tensor(name, shape, dtype, kind="ExternalInput")

    xo = din("xo", [D, TOK])
    cs = din("cs", [128, S])
    h128 = din("h128", [128, 128], BF16)
    onesb = din("onesb", [128, 128], BF16)
    onesf = din("onesf", [1, 128])
    qw = din("qw", [D, 2 * HD])
    kvaw = din("kvaw", [D, KVR])
    kvbw = din("kvbw", [KVR, 4 * HD])
    ow = din("ow", [D, D])
    wup = din("wup", [D, INTER])
    wg = din("wg", [D, INTER])
    wd = din("wd", [INTER, D])
    kvnw = din("kvnw", [128, 4])
    suug = din("suug", [128, 16])
    wscup = din("wscup", [128, 64])
    wscg = din("wscg", [128, 64])
    sudp = din("sudp", [128, 64])
    wscdp = din("wscdp", [128, 16])
    epsv = din("epsv", [128, 2])

    out_t = nc.dram_tensor("out", [D, TOK], F32, kind="ExternalOutput")

    pid512 = nc.partition_id(engines=[EngineType.SP]) * 512

    cc_xl_in = nc.dram_tensor("cc_xl_in", [D + KVR, TOK], BF16)
    cc_xl_out = nc.dram_tensor("cc_xl_out", [NCORES * (D + KVR), TOK], BF16,
                               addr_space="Shared")
    cc_a_in0 = nc.dram_tensor("cc_a_in0", [2 * HD, 8 * 512], BF16)
    cc_a_out0 = nc.dram_tensor("cc_a_out0", [NCORES * 2 * HD, 8 * 512], BF16,
                               addr_space="Shared")
    cc_a_in1 = nc.dram_tensor("cc_a_in1", [2 * HD, 8 * 512], BF16)
    cc_a_out1 = nc.dram_tensor("cc_a_out1", [NCORES * 2 * HD, 8 * 512], BF16,
                               addr_space="Shared")
    x1_sp = nc.dram_tensor("x1_sp", [D, TOK], F32, kind="ExternalOutput")
    v2_sp = nc.dram_tensor("v2_sp", [D, TOK], F32, kind="ExternalOutput")

    tap_outs = {}
    for tname, shape, dty in (
        ("q0", [128, S], BF16), ("k0", [128, S], BF16), ("vt0", [128, S], BF16),
        ("lat", [128, 4 * 1024], BF16), ("negm", [128, 2], F32),
        ("aown", [128, 16 * TOK], BF16), ("xs", [128, 16 * TOK], BF16),
        ("u", [128, 8 * TOK], BF16), ("u0", [128, 8 * TOK], BF16),
        ("asb", [128, 512], BF16),
        ("mx", [1, 80], F32), ("sg", [128, 64], F32), ("sup", [128, 64], F32),
    ):
        if tname in taps:
            tap_outs[tname] = nc.dram_tensor("tap_" + tname, shape, dty,
                                             kind="ExternalOutput")

    ctx = contextlib.ExitStack()
    b = B()

    gld = Sem(ctx, nc, "gld")
    sld = Sem(ctx, nc, "sld")
    pe = Sem(ctx, nc, "pe")
    ac = Sem(ctx, nc, "ac")
    dv = Sem(ctx, nc, "dv")
    spe = Sem(ctx, nc, "spe")
    sac = Sem(ctx, nc, "sac")
    spv = Sem(ctx, nc, "spv")
    cc1 = Sem(ctx, nc, "cc1")
    cc2 = Sem(ctx, nc, "cc2")
    cc2b = Sem(ctx, nc, "cc2b")
    gp = Sem(ctx, nc, "gp")

    def DMA(eng, out_ap, in_ap, wait=()):
        sem = gld if eng == "g" else sld
        b.add(eng, (lambda e, o=out_ap, i=in_ap: e.dma_start(out=o, in_=i)),
              wait, [(sem, 16)])

    def MM(out_ap, lhsT, rhs, start=True, stop=True, wait=(), inc=()):
        b.add("t", (lambda e, o=out_ap, l=lhsT, r=rhs, s1=start, s2=stop:
                    e.matmul(o, l, r, start=s1, stop=s2)), wait, inc)

    def ACT(out_ap, in_ap, func, bias=0.0, scale=1.0, wait=(), inc=()):
        b.add("a", (lambda e, o=out_ap, i=in_ap, f=func, bi=bias, sc=scale:
                    e.activation(o, i, f, bias=bi, scale=sc)), wait, inc)

    def VTT(op, out_ap, in0, in1, wait=(), inc=()):
        b.add("v", (lambda e, o=out_ap, a0=in0, a1=in1, p=op:
                    e.tensor_tensor(o, a0, a1, p)), wait, inc)

    def VTS(out_ap, in0, scalar, op=ALU.mult, wait=(), inc=()):
        b.add("v", (lambda e, o=out_ap, a0=in0, scl=scalar, p=op:
                    e.tensor_single_scalar(o, a0, scl, p)), wait, inc)

    def VSTT(out_ap, in0, scalar, in1, op0=ALU.mult, op1=ALU.mult,
             wait=(), inc=()):
        b.add("v", (lambda e, o=out_ap, a0=in0, scl=scalar, a1=in1, p0=op0,
                    p1=op1: e.scalar_tensor_tensor(o, a0, scl, a1, p0, p1)),
              wait, inc)

    def VCP(out_ap, in_ap, wait=(), inc=()):
        b.add("v", (lambda e, o=out_ap, i=in_ap: e.tensor_copy(o, i)),
              wait, inc)

    def VRMAX(out_ap, in_ap, wait=(), inc=()):
        b.add("v", (lambda e, o=out_ap, i=in_ap: e.reduce_max(o, i, AX.X)),
              wait, inc)

    def VRECIP(out_ap, in_ap, wait=(), inc=()):
        b.add("v", (lambda e, o=out_ap, i=in_ap: e.reciprocal(o, i)),
              wait, inc)

    def sb(stack, name, shape, dtype, side=None):
        return stack.enter_context(nc.sbuf_tensor(name, shape, dtype,
                                                  side=side))

    def ps(stack, name, shape, dtype=F32):
        return stack.enter_context(nc.psum_tensor(name, shape, dtype))

    sbP = ctx.enter_context(contextlib.ExitStack())
    h128_sb = sb(sbP, "h128_sb", [128, 128], BF16)
    onesb_sb = sb(sbP, "onesb_sb", [128, 128], BF16)
    onesf_sb = sb(sbP, "onesf_sb", [1, 128], F32)
    kvnw_sb = sb(sbP, "kvnw_sb", [128, 4], F32)
    suug_sb = sb(sbP, "suug_sb", [128, 16], F32)
    sup_sb = sb(sbP, "sup_sb", [128, 64], F32)
    sgate_sb = sb(sbP, "sgate_sb", [128, 64], F32)
    sdp_sb = sb(sbP, "sdp_sb", [128, 16], F32)
    wscup_sb = sb(sbP, "wscup_sb", [128, 64], F32)
    sudp_sb = sb(sbP, "sudp_sb", [128, 64], F32)
    negm_sb = sb(sbP, "negm_sb", [128, 2], F32)
    mx_sb = sb(sbP, "mx_sb", [1, 80], F32)
    st1_sb = sb(sbP, "st1_sb", [1, 1024], F32)
    st2_sb = sb(sbP, "st2_sb", [1, 1024], F32)
    st3_sb = sb(sbP, "st3_sb", [1, 1024], F32)
    epsv_sb = sb(sbP, "epsv_sb", [128, 2], F32)

    sbA = contextlib.ExitStack()
    kvbw_sb = sb(sbA, "kvbw_sb", [128, 4 * 512], BF16)
    lat_sb = [sb(sbA, f"lat_sb{i}", [128, 4 * 512], BF16) for i in range(4)]
    nsq_sb = sb(sbA, "nsq_sb", [128, 512], BF16)
    Q_sb = [sb(sbA, f"q_sbh{h}", [128, S], BF16) for h in range(2)]

    sbA2 = contextlib.ExitStack()
    cs_cos = sb(sbA2, "cs_cos", [64, S], BF16, side="right")
    cs_sin = sb(sbA2, "cs_sin", [64, S], BF16, side="right")
    rt1_sb = sb(sbA2, "rt1_sb", [64, 1024], BF16, side="right")

    sbA1 = contextlib.ExitStack()
    qw_sb = sb(sbA1, "qw_sb", [128, 16 * 256], BF16, side="right")
    xt_sb = [sb(sbA1, f"xt_sb{i}", [128, 16 * 512], BF16, side="right")
             for i in range(2)]

    sbA0 = contextlib.ExitStack()
    xo_sb = sb(sbA0, "xo_sb", [128, 16 * TOK], BF16, side="right")
    kvaw_sb = sb(sbA0, "kvaw_sb", [128, 16 * 512], BF16, side="right")
    kvc_sb = sb(sbA0, "kvc_sb", [128, 8 * 512], BF16, side="right")
    sq_sb = sb(sbA0, "sq_sb", [128, 512], BF16, side="right")
    latst_sb = sb(sbA0, "latst_sb", [128, 512], BF16, side="right")
    st1f_sb = sb(sbA0, "st1f_sb", [128, 512], F32, side="right")
    st2f_sb = sb(sbA0, "st2f_sb", [128, 512], F32, side="right")

    psA_b = contextlib.ExitStack()
    bc1_ps = ps(psA_b, "bc1_ps", [128, 512])
    nrm_ps = ps(psA_b, "nrm_ps", [1, 512])
    rt2_ps = ps(psA_b, "rt2_ps", [64, 1024])
    psA_a = contextlib.ExitStack()
    kv_ps = ps(psA_a, "kv_ps", [128, 512])
    var_ps = ps(psA_a, "var_ps", [128, 512])
    q_ps = [ps(psA_a, f"q_ps{h}", [128, 512]) for h in range(2)]

    DMA("y", h128_sb[:], h128.ap())
    DMA("y", onesb_sb[:], onesb.ap())
    DMA("y", onesf_sb[:], onesf.ap())
    DMA("y", kvnw_sb[:], kvnw.ap())
    DMA("y", suug_sb[:], suug.ap())
    DMA("y", wscup_sb[:], wscup.ap())
    DMA("y", sudp_sb[:], sudp.ap())
    DMA("y", sgate_sb[:], wscg.ap())
    DMA("y", sdp_sb[:], wscdp.ap())
    DMA("y", epsv_sb[:], epsv.ap())

    DMA("g", xo_sb.ap().rearrange("p (i t) -> p i t", i=16),
        xo.ap().rearrange("(i p) t -> p i t", p=128))
    gld_xo = gld.n
    DMA("g", kvaw_sb.ap().rearrange("p (i o) -> p i o", i=16),
        kvaw.ap().rearrange("(i p) o -> p i o", p=128))
    gld_p0 = gld.n
    DMA("g", qw_sb.ap().rearrange("p (i o) -> p i o", i=16),
        qw.ap().rearrange("(i p) o -> p i o", p=128))
    DMA("g", kvbw_sb.ap().rearrange("p (j o) -> p j o", j=4),
        kvbw.ap().rearrange("(j p) o -> p j o", p=128))
    DMA("g", cs_cos[:], cs.ap()[0:64, :])
    DMA("g", cs_sin[:], cs.ap()[64:128, :])
    gld_csq = gld.n

    for i in range(16):
        DMA("y", cc_xl_in.ap()[i * 128:(i + 1) * 128, :],
            xo_sb[:, i * TOK:(i + 1) * TOK],
            wait=[(gld, gld_xo)] if i == 0 else ())

    sld_p0 = sld.n
    VSTT(sup_sb[:], wscup_sb[:], RS2, sudp_sb[:], wait=_w(sld), inc=[(dv, 1)])
    VTS(sgate_sb[:], sgate_sb[:], RS2, wait=_w(sld), inc=[(dv, 1)])
    VTS(sdp_sb[:], sdp_sb[:], RS8, wait=_w(sld), inc=[(dv, 1)])

    last_kv_evac = 0
    last_var_pe = 0
    for t2 in range(2):
        for j in range(4):
            blk = j * 2 + t2
            for i in range(16):
                w0 = []
                if i == 0:
                    w0 = [(gld, gld_p0)]
                    if last_kv_evac:
                        w0.append((dv, last_kv_evac))
                MM(kv_ps[:, 0:512],
                   kvaw_sb[:, i * 512 + j * 128: i * 512 + (j + 1) * 128],
                   xo_sb[:, i * TOK + t2 * 512: i * TOK + t2 * 512 + 512],
                   start=(i == 0), stop=(i == 15),
                   wait=w0, inc=[(pe, 1)] if i == 15 else ())
            VCP(kvc_sb[:, blk * 512:(blk + 1) * 512], kv_ps[:, 0:512],
                wait=_w(pe), inc=[(dv, 1)])
            last_kv_evac = dv.n
            wsq = _w(dv)
            if last_var_pe:
                wsq.append((pe, last_var_pe))
            else:
                wsq.append((sld, sld_p0))
            ACT(sq_sb[:], kvc_sb[:, blk * 512:(blk + 1) * 512], AF.Square,
                wait=wsq, inc=[(ac, 1)])
            MM(var_ps[:, 0:512], onesb_sb[:, 0:128], sq_sb[:],
               start=(j == 0), stop=(j == 3),
               wait=_w(ac), inc=[(pe, 1)])
            last_var_pe = pe.n
        ACT(st1f_sb[:], var_ps[:, 0:512], AF.Ln,
            bias=epsv_sb[:, 0:1], scale=1.0 / KVR, wait=_w(pe, dv),
            inc=[(ac, 1)])
        ACT(st2f_sb[:], st1f_sb[:], AF.Exp, scale=-0.5, inc=[(ac, 1)])
        for j in range(4):
            blk = j * 2 + t2
            wl = _w(ac, sld)
            VSTT(latst_sb[:], kvc_sb[:, blk * 512:(blk + 1) * 512],
                 kvnw_sb[:, j: j + 1], st2f_sb[:],
                 wait=wl, inc=[(dv, 1)])
            DMA("y", cc_xl_in.ap()[D + j * 128: D + (j + 1) * 128,
                                   t2 * 512: t2 * 512 + 512],
                latst_sb[:], wait=_w(dv))
    sld_lat = sld.n

    b.add("g", (lambda e: e.collective_compute(
        "AllGather", ALU.bypass, replica_groups=[list(range(NCORES))],
        ins=[cc_xl_in.ap().opt()], outs=[cc_xl_out.ap().opt()])),
        [(sld, sld_lat)], [(cc1, 1)])

    xv = cc_xl_out.ap().rearrange("(r i p) t -> p r i t", r=NCORES, i=20)
    xt_used_pe = [0, 0]
    q_evac_ac = [0, 0]
    xt_sld = [0] * NT

    def emit_xt_dma(tt):
        bi = tt % 2
        w0 = [(cc1, 1)]
        if xt_used_pe[bi]:
            w0.append((pe, xt_used_pe[bi]))
        DMA("y", xt_sb[bi].ap().rearrange("p (i t) -> p i t", i=16),
            xv[:, tt // 2, 0:16, (tt % 2) * 512:(tt % 2) * 512 + 512],
            wait=w0)
        xt_sld[tt] = sld.n

    emit_xt_dma(0)
    emit_xt_dma(1)
    qe = [[0, 0] for _ in range(NT)]
    for tt in range(NT):
        bi = tt % 2
        for h in range(2):
            for i in range(16):
                w0 = []
                if i == 0:
                    w0 = [(sld, xt_sld[tt])]
                    if q_evac_ac[h]:
                        w0.append((ac, q_evac_ac[h]))
                MM(q_ps[h][:, 0:512],
                   qw_sb[:, i * 256 + h * 128: i * 256 + (h + 1) * 128],
                   xt_sb[bi][:, i * 512:(i + 1) * 512],
                   start=(i == 0), stop=(i == 15),
                   wait=w0, inc=[(pe, 1)] if i == 15 else ())
            ACT(Q_sb[h][:, tt * 512:(tt + 1) * 512], q_ps[h][:, 0:512],
                AF.Copy, scale=SCALE, wait=_w(pe), inc=[(ac, 1)])
            q_evac_ac[h] = ac.n
            qe[tt][h] = ac.n
        xt_used_pe[bi] = pe.n
        if tt + 2 < NT:
            emit_xt_dma(tt + 2)

    rope_prev_dv = [0]

    def rope_step(tgt, stp, ac_val):
        sl = slice(stp * 1024, stp * 1024 + 1024)
        wg = [(ac, ac_val), (gld, gld_csq)]
        if rope_prev_dv[0]:
            wg.append((dv, rope_prev_dv[0]))
        b.add("g", (lambda e, o=rt1_sb[:], a0=tgt[0:64, sl],
                    a1=cs_cos[:, sl]:
                    e.tensor_tensor(o, a0, a1, ALU.mult)), wg, [(gp, 1)])
        VTT(ALU.mult, rt2_ps[:], tgt[0:64, sl], cs_sin[:, sl],
            wait=[(ac, ac_val), (gld, gld_csq)])
        VTT(ALU.add, tgt[0:32, sl], rt1_sb[0:32, :], rt2_ps[32:64, :],
            wait=_w(gp))
        VTT(ALU.add, tgt[32:64, sl], rt1_sb[32:64, :], rt2_ps[0:32, :],
            inc=[(dv, 1)])
        rope_prev_dv[0] = dv.n
        return dv.n

    nrm_state = [0, 0]

    def normmax_tile(tgt, mxslot, tt, rope_dv_val):
        ACT(nsq_sb[:], tgt[:, tt * 512:(tt + 1) * 512], AF.Square,
            wait=[(dv, rope_dv_val), (pe, nrm_state[0])], inc=[(ac, 1)])
        wm = _w(ac)
        if nrm_state[1]:
            wm.append((dv, nrm_state[1]))
        MM(nrm_ps[:, 0:512], onesb_sb[:, 0:1], nsq_sb[:],
           wait=wm, inc=[(pe, 1)])
        nrm_state[0] = pe.n
        VRMAX(mx_sb[:, mxslot: mxslot + 1], nrm_ps[:, 0:512],
              wait=_w(pe), inc=[(dv, 1)])
        nrm_state[1] = dv.n

    for s in range(8):
        d0 = rope_step(Q_sb[0], s, qe[2 * s + 1][0])
        d1 = rope_step(Q_sb[1], s, qe[2 * s + 1][1])
        for tt in (2 * s, 2 * s + 1):
            normmax_tile(Q_sb[0], 0 + tt, tt, d0)
            normmax_tile(Q_sb[1], 16 + tt, tt, d1)

    sbA0.close()
    sbA1.close()
    psA_a.close()

    sbA_kv = contextlib.ExitStack()
    K_sb = [sb(sbA_kv, f"k_sbh{h}", [128, S], BF16) for h in range(2)]
    Vt_sb = [sb(sbA_kv, f"vt_sbh{h}", [128, S], BF16) for h in range(2)]
    psA3 = contextlib.ExitStack()
    k_ps = [ps(psA3, f"k_ps{h}", [128, 512]) for h in range(2)]
    vt_ps = [ps(psA3, f"vt_ps{h}", [128, 512]) for h in range(2)]

    lat_used_pe = [0, 0, 0, 0]
    kv_evac = [0, 0]
    ke = [[0, 0] for _ in range(NT)]
    for tt in range(NT):
        bi = tt % 4
        r = tt // 2
        for j in range(4):
            w0 = []
            if j == 0:
                if tt == 0:
                    w0 = [(cc1, 1)]
                elif lat_used_pe[bi]:
                    w0 = [(pe, lat_used_pe[bi])]
            DMA("g", lat_sb[bi][:, j * 512:(j + 1) * 512],
                xv[:, r, 16 + j, (tt % 2) * 512: (tt % 2) * 512 + 512],
                wait=w0)
        gld_tt = gld.n
        for h in range(2):
            for j in range(4):
                w0 = []
                if j == 0:
                    w0 = [(gld, gld_tt)]
                    if kv_evac[h]:
                        w0.append((ac, kv_evac[h]))
                MM(k_ps[h][:, 0:512],
                   kvbw_sb[:, j * 512 + h * 256: j * 512 + h * 256 + 128],
                   lat_sb[bi][:, j * 512:(j + 1) * 512],
                   start=(j == 0), stop=(j == 3),
                   wait=w0, inc=[(pe, 1)] if j == 3 else ())
            for tc in range(4):
                for j in range(4):
                    MM(vt_ps[h][:, tc * 128:(tc + 1) * 128],
                       lat_sb[bi][:, j * 512 + tc * 128: j * 512 + (tc + 1) * 128],
                       kvbw_sb[:, j * 512 + h * 256 + 128:
                               j * 512 + h * 256 + 256],
                       start=(j == 0), stop=(j == 3),
                       inc=[(pe, 1)] if (tc == 3 and j == 3) else ())
            ACT(K_sb[h][:, tt * 512:(tt + 1) * 512], k_ps[h][:, 0:512],
                AF.Copy, wait=_w(pe), inc=[(ac, 1)])
            ke[tt][h] = ac.n
            ACT(Vt_sb[h][:, tt * 512:(tt + 1) * 512], vt_ps[h][:, 0:512],
                AF.Copy, inc=[(ac, 1)])
            kv_evac[h] = ac.n
        lat_used_pe[bi] = pe.n

    for s in range(8):
        d0 = rope_step(K_sb[0], s, ke[2 * s + 1][0])
        d1 = rope_step(K_sb[1], s, ke[2 * s + 1][1])
        for tt in (2 * s, 2 * s + 1):
            normmax_tile(K_sb[0], 32 + tt, tt, d0)
            normmax_tile(K_sb[1], 48 + tt, tt, d1)

    for h in range(2):
        VRMAX(mx_sb[:, 64 + h: 64 + h + 1], mx_sb[:, h * 16: h * 16 + 16],
              wait=_w(dv))
        VRMAX(mx_sb[:, 66 + h: 66 + h + 1],
              mx_sb[:, 32 + h * 16: 32 + h * 16 + 16],
              inc=[(dv, 1)] if h == 1 else ())
    ACT(mx_sb[:, 68:72], mx_sb[:, 64:68], AF.Ln, wait=_w(dv), inc=[(ac, 1)])
    VTT(ALU.add, mx_sb[:, 72:74], mx_sb[:, 68:70], mx_sb[:, 70:72],
        wait=_w(ac), inc=[(dv, 1)])
    ACT(mx_sb[:, 74:76], mx_sb[:, 72:74], AF.Exp, scale=0.5, wait=_w(dv),
        inc=[(ac, 1)])
    VTS(mx_sb[:, 76:78], mx_sb[:, 74:76], -1.0, wait=_w(ac), inc=[(dv, 1)])
    for h in range(2):
        MM(bc1_ps[:, 0:1], onesf_sb[:], mx_sb[:, 76 + h: 76 + h + 1],
           wait=_w(dv), inc=[(pe, 1)])
        VCP(negm_sb[:, h: h + 1], bc1_ps[:, 0:1], wait=_w(pe), inc=[(dv, 1)])

    if "sg" in tap_outs:
        DMA("y", tap_outs["sg"].ap(), sgate_sb[:], wait=_w(dv))
    if "sup" in tap_outs:
        DMA("y", tap_outs["sup"].ap(), sup_sb[:], wait=_w(dv))
    if "mx" in tap_outs:
        DMA("y", tap_outs["mx"].ap(), mx_sb[:], wait=_w(dv, ac, pe))
    if "q0" in tap_outs:
        DMA("y", tap_outs["q0"].ap(), Q_sb[0][:], wait=_w(dv, ac))
    if "k0" in tap_outs:
        DMA("y", tap_outs["k0"].ap(), K_sb[0][:], wait=_w(dv, ac))
    if "vt0" in tap_outs:
        DMA("y", tap_outs["vt0"].ap(), Vt_sb[0][:], wait=_w(dv, ac))
    if "negm" in tap_outs:
        DMA("y", tap_outs["negm"].ap(), negm_sb[:], wait=_w(dv))
    if "lat" in tap_outs:
        for j in range(4):
            DMA("y", tap_outs["lat"].ap()[:, j * 1024:(j + 1) * 1024],
                cc_xl_out.ap()[D + j * 128: D + (j + 1) * 128, 0:1024],
                wait=[(cc1, 1)])

    sbA2.close()
    psA3.close()
    psA_b.close()
    psB = contextlib.ExitStack()
    s_ps = [ps(psB, f"s_ps{i}", [128, 2 * 512]) for i in range(3)]
    pv_ps = ps(psB, "pv_ps", [128, 512])
    dd_ps = ps(psB, "dd_ps", [128, 512])
    sbA4 = contextlib.ExitStack()
    p_sb = [sb(sbA4, f"p_sb{i}", [128, 2 * 512], BF16) for i in range(3)]
    padd_sb = [sb(sbA4, f"padd{i}", [128, 512], BF16) for i in range(3)]
    padd2_sb = [sb(sbA4, f"padd2_{i}", [128, 512], BF16) for i in range(3)]
    araw_sb = [sb(sbA4, f"araw{i}", [128, 512], BF16) for i in range(2)]
    asb_sb = [sb(sbA4, f"asbx{i}", [128, 512], BF16) for i in range(2)]
    rec_sb = [sb(sbA4, f"rec{i}", [128, 512], F32) for i in range(2)]

    NGRP = 32
    s_bank_free = [0, 0, 0]
    p_buf_free = [0, 0, 0]
    pd2_free = [0, 0, 0]
    araw_free_dv = [0, 0]
    asb_free_sld = [0, 0]
    prev_mult_dv = 0
    prev_recip_dv = 0
    pre_attn = _w(pe, ac, dv, sld, gld)
    attn_entry = _w(dv, ac, pe)
    first_attn = True
    for parity in range(2):
      for h in range(2):
        for qt in range(parity, NT, 2):
            exp_sac = [0] * NGRP
            padd_dv = [0] * NGRP
            padd2_dv = [0] * (NGRP // 2)

            def emit_S(g):
                nonlocal first_attn
                bi = g % 3
                for i in range(2):
                    w0 = []
                    if i == 0:
                        if first_attn:
                            w0 += attn_entry
                            first_attn = False
                        if s_bank_free[bi]:
                            w0.append((sac, s_bank_free[bi]))
                    kcc = g * 2 + i
                    MM(s_ps[bi][:, i * 512:(i + 1) * 512],
                       K_sb[h][:, kcc * 128:(kcc + 1) * 128],
                       Q_sb[h][:, qt * 512:(qt + 1) * 512],
                       start=True, stop=True,
                       wait=w0, inc=[(spe, 1)] if i == 1 else ())

            def emit_exp(g):
                bi = g % 3
                we = _w(spe)
                if p_buf_free[bi]:
                    we.append((spv, p_buf_free[bi]))
                ACT(p_sb[bi][:, 0:1024], s_ps[bi][:, 0:1024], AF.Exp,
                    bias=negm_sb[:, h: h + 1], wait=we, inc=[(sac, 1)])
                s_bank_free[bi] = sac.n
                exp_sac[g] = sac.n

            def emit_padd(g):
                bi = g % 3
                VTT(ALU.add, padd_sb[bi][:], p_sb[bi][:, 0:512],
                    p_sb[bi][:, 512:1024], wait=[(sac, exp_sac[g])],
                    inc=[(dv, 1)])
                padd_dv[g] = dv.n
                if g % 2 == 1:
                    j = (g // 2) % 3
                    wp2 = [(dv, padd_dv[g])]
                    if pd2_free[j]:
                        wp2.append((spv, pd2_free[j]))
                    VTT(ALU.add, padd2_sb[j][:], padd_sb[(g - 1) % 3][:],
                        padd_sb[bi][:], wait=wp2, inc=[(dv, 1)])
                    padd2_dv[g // 2] = dv.n

            def emit_PVD(g):
                bi = g % 3
                for i in range(2):
                    kcc = g * 2 + i
                    wpv = []
                    if i == 0:
                        wpv = [(sac, exp_sac[g])]
                        if g == 0 and prev_mult_dv:
                            wpv.append((dv, prev_mult_dv))
                    MM(pv_ps[:, 0:512],
                       Vt_sb[h][:, kcc * 128:(kcc + 1) * 128],
                       p_sb[bi][:, i * 512:(i + 1) * 512],
                       start=(g == 0 and i == 0),
                       stop=(g == NGRP - 1 and i == 1),
                       wait=wpv,
                       inc=[(spv, 1)] if (g % 2 == 0 and i == 1) else ())
                if g % 2 == 1:
                    wdd = [(dv, padd2_dv[g // 2])]
                    if g == 1 and prev_recip_dv:
                        wdd.append((dv, prev_recip_dv))
                    MM(dd_ps[:, 0:512], onesb_sb[:, 0:128],
                       padd2_sb[(g // 2) % 3][:],
                       start=(g == 1), stop=(g == NGRP - 1),
                       wait=wdd, inc=[(spv, 1)])
                    pd2_free[(g // 2) % 3] = spv.n
                p_buf_free[bi] = spv.n

            emit_S(0)
            emit_exp(0)
            emit_padd(0)
            emit_S(1)
            emit_exp(1)
            emit_padd(1)
            for g in range(2, NGRP):
                emit_S(g)
                emit_exp(g)
                emit_padd(g)
                emit_PVD(g - 2)
            emit_PVD(NGRP - 2)
            emit_PVD(NGRP - 1)

            abi = qt % 2
            wq = _w(spv)
            if araw_free_dv[abi]:
                wq.append((dv, araw_free_dv[abi]))
            VCP(araw_sb[abi][:], pv_ps[:, 0:512], wait=wq, inc=[(dv, 1)])
            prev_mult_dv = dv.n
            VRECIP(rec_sb[abi][:], dd_ps[:, 0:512], inc=[(dv, 1)])
            prev_recip_dv = dv.n
            wv = _w(dv)
            if asb_free_sld[abi]:
                wv.append((sld, asb_free_sld[abi]))
            VTT(ALU.mult, asb_sb[abi][:], araw_sb[abi][:],
                rec_sb[abi][:], wait=wv, inc=[(dv, 1)])
            araw_free_dv[abi] = dv.n
            blk = qt // 2
            cc_t = cc_a_in0 if parity == 0 else cc_a_in1
            DMA("y", cc_t.ap()[h * 128:(h + 1) * 128,
                               blk * 512:(blk + 1) * 512],
                asb_sb[abi][:], wait=_w(dv))
            asb_free_sld[abi] = sld.n
            if "asb" in tap_outs and h == 0 and qt == 0:
                DMA("y", tap_outs["asb"].ap(), asb_sb[abi][:], wait=_w(dv))
      if parity == 0:
          sld_a0 = sld.n
          b.add("g", (lambda e: e.collective_compute(
              "AllGather", ALU.bypass, replica_groups=[list(range(NCORES))],
              ins=[cc_a_in0.ap().opt()], outs=[cc_a_out0.ap().opt()])),
              [(sld, sld_a0), (cc1, 1)], [(cc2, 1)])

    sld_a1 = sld.n
    b.add("g", (lambda e: e.collective_compute(
        "AllGather", ALU.bypass, replica_groups=[list(range(NCORES))],
        ins=[cc_a_in1.ap().opt()], outs=[cc_a_out1.ap().opt()])),
        [(sld, sld_a1), (cc2, 1)], [(cc2b, 1)])

    end_attn = _w(pe, ac, dv, sld, gld)
    sbA4.close()
    sbA_kv.close()
    sbA.close()
    psB.close()

    sbB2 = contextlib.ExitStack()
    xs_sb = sb(sbB2, "xs_sb", [128, 16 * TOK], BF16, side="right")

    sbB3 = contextlib.ExitStack()
    vb_sb = sb(sbB3, "vb_sb", [128, 4 * TOK], BF16, side="right")
    sqb_sb = sb(sbB3, "sqb_sb", [128, 4 * TOK], BF16, side="right")
    mbc_sb = sb(sbB3, "mbc_sb", [128, TOK], F32, side="right")
    rbc_sb = sb(sbB3, "rbc_sb", [128, TOK], F32, side="right")
    t1f_sb = sb(sbB3, "t1f_sb", [128, TOK], F32, side="right")

    sbB1 = contextlib.ExitStack()
    aown_sb = sb(sbB1, "aown_sb", [128, 16 * TOK], BF16, side="right")
    v1_sb = sb(sbB1, "v1_sb", [128, 16 * TOK], F32, side="right")
    owb_sb = [sb(sbB1, f"owb{i}", [128, 16 * 128], BF16, side="right")
              for i in range(2)]
    xr_sb = [sb(sbB1, f"xr{i}", [128, TOK], F32, side="right")
             for i in range(2)]

    psC1 = contextlib.ExitStack()
    o_ps = ps(psC1, "o_ps", [128, 1024])
    psC2 = contextlib.ExitStack()
    mean_ps = ps(psC2, "mean_ps", [128, 1024])
    msq_ps = ps(psC2, "msq_ps", [128, 1024])

    for i in range(16):
        src = cc_a_out0.ap()[i * 128:(i + 1) * 128, 0:512]
        src = dataclasses.replace(src, offset=src.offset + pid512)
        DMA("y", aown_sb[:, i * TOK: i * TOK + 512], src,
            wait=[(cc2, 1)] if i == 0 else ())
    sld_aown0 = sld.n
    for i in range(16):
        src = cc_a_out1.ap()[i * 128:(i + 1) * 128, 0:512]
        src = dataclasses.replace(src, offset=src.offset + pid512)
        DMA("y", aown_sb[:, i * TOK + 512: i * TOK + 1024], src,
            wait=[(cc2b, 1)] if i == 0 else ())
    sld_aown1 = sld.n

    ow_v = ow.ap().rearrange("(i p) o -> p i o", p=128)
    ow_used = [0, 0]
    o_done_dv = 0
    stats_pe = 0
    for jo in range(16):
        bi = jo % 2
        DMA("g", owb_sb[bi].ap().rearrange("p (i o) -> p i o", i=16),
            ow_v[:, :, jo * 128:(jo + 1) * 128],
            wait=[(pe, ow_used[bi])] if ow_used[bi] else pre_attn)
        gld_ow = gld.n
        DMA("y", xr_sb[bi][:], xo.ap()[jo * 128:(jo + 1) * 128, :],
            wait=[(dv, o_done_dv)] if o_done_dv else ())
        sld_xr = sld.n
        for h2 in range(2):
            for i in range(16):
                w0 = []
                if i == 0:
                    w0 = [(gld, gld_ow),
                          (sld, sld_aown0 if h2 == 0 else sld_aown1)]
                    if h2 == 0 and o_done_dv:
                        w0.append((dv, o_done_dv))
                MM(o_ps[:, h2 * 512: h2 * 512 + 512],
                   owb_sb[bi][:, i * 128:(i + 1) * 128],
                   aown_sb[:, i * TOK + h2 * 512: i * TOK + h2 * 512 + 512],
                   start=(i == 0), stop=(i == 15),
                   wait=w0,
                   inc=[(pe, 1)] if (i == 15 and h2 == 1) else ())
        ow_used[bi] = pe.n
        VTT(ALU.add, v1_sb[:, jo * TOK:(jo + 1) * TOK], o_ps[:], xr_sb[bi][:],
            wait=_w(pe) + [(sld, sld_xr)], inc=[(dv, 1)])
        o_done_dv = dv.n
        q4 = jo % 4
        VCP(vb_sb[:, q4 * TOK:(q4 + 1) * TOK],
            v1_sb[:, jo * TOK:(jo + 1) * TOK],
            wait=[(pe, stats_pe)] if (q4 == 0 and stats_pe) else (),
            inc=[(dv, 1)])
        ACT(sqb_sb[:, q4 * TOK:(q4 + 1) * TOK],
            v1_sb[:, jo * TOK:(jo + 1) * TOK], AF.Square,
            wait=_w(dv), inc=[(ac, 1)])
        if q4 == 3:
            for qq in range(4):
                jj = jo - 3 + qq
                for h2 in range(2):
                    off = qq * TOK + h2 * 512
                    MM(mean_ps[:, h2 * 512: h2 * 512 + 512],
                       onesb_sb[:, 0:128], vb_sb[:, off: off + 512],
                       start=(jj == 0), stop=(jj == 15),
                       wait=_w(dv, ac) if (qq == 0 and h2 == 0) else ())
                    MM(msq_ps[:, h2 * 512: h2 * 512 + 512],
                       onesb_sb[:, 0:128], sqb_sb[:, off: off + 512],
                       start=(jj == 0), stop=(jj == 15),
                       inc=[(pe, 1)] if (qq == 3 and h2 == 1) else ())
            stats_pe = pe.n

    def ln_stats(mean_src, msq_src, mbc_t, rbc_t, sca, scb):
        VTS(mbc_t, mean_src[:, 0:1024], 1.0 / D, wait=_w(pe),
            inc=[(dv, 1)])
        VTS(sca, msq_src[:, 0:1024], 1.0 / D, inc=[(dv, 1)])
        ACT(scb, mbc_t, AF.Square, wait=_w(dv), inc=[(ac, 1)])
        VTT(ALU.subtract, sca, sca, scb, wait=_w(ac), inc=[(dv, 1)])
        ACT(scb, sca, AF.Ln, bias=epsv_sb[:, 1:2], wait=_w(dv),
            inc=[(ac, 1)])
        ACT(rbc_t, scb, AF.Exp, scale=-0.5, inc=[(ac, 1)])

    ln1sc = sqb_sb.ap().bitcast(F32)
    ln_stats(mean_ps, msq_ps, mbc_sb[:], rbc_sb[:],
             ln1sc[:, 0:1024], ln1sc[:, 1024:2048])
    psC2.close()
    psC3 = contextlib.ExitStack()
    o2_ps = ps(psC3, "o2_ps", [128, 1024])

    for jo in range(16):
        sl = slice(jo * TOK, (jo + 1) * TOK)
        VTT(ALU.subtract, t1f_sb[:], v1_sb[:, sl], mbc_sb[:],
            wait=_w(dv, ac))
        VTT(ALU.mult, v1_sb[:, sl], t1f_sb[:], rbc_sb[:], inc=[(dv, 1)])
        DMA("y", x1_sp.ap()[jo * 128:(jo + 1) * 128, :], v1_sb[:, sl],
            wait=_w(dv))
        VTS(xs_sb[:, sl], v1_sb[:, sl], suug_sb[:, jo: jo + 1],
            inc=[(dv, 1)])
        lvl = 0
        while (jo >> lvl) & 1:
            wdt2 = (1 << lvl) * TOK
            base = (jo - (2 << lvl) + 1) * TOK
            a_ap = xs_sb[:, base: base + wdt2]
            b_ap = xs_sb[:, base + wdt2: base + 2 * wdt2]
            VTT(ALU.add, a_ap, a_ap, b_ap)
            VSTT(b_ap, b_ap, -2.0, a_ap, ALU.mult, ALU.add, inc=[(dv, 1)])
            lvl += 1
    sld_x1 = sld.n

    def butterflies(tgt, nck, width):
        stv = 1
        while stv < nck:
            hi = nck // (2 * stv)
            blkw = stv * width
            if 2 * blkw <= 16384:
                v3 = tgt.ap().rearrange("p (hi rest) -> p hi rest", hi=hi)
                a_ap = v3[:, :, 0:blkw]
                b_ap = v3[:, :, blkw: 2 * blkw]
                VTT(ALU.add, a_ap, a_ap, b_ap, wait=_w(dv))
                VSTT(b_ap, b_ap, -2.0, a_ap, ALU.mult, ALU.add,
                     inc=[(dv, 1)])
            else:
                for g2 in range(hi):
                    base = g2 * 2 * blkw
                    a_ap = tgt[:, base: base + blkw]
                    b_ap = tgt[:, base + blkw: base + 2 * blkw]
                    VTT(ALU.add, a_ap, a_ap, b_ap,
                        wait=_w(dv) if g2 == 0 else ())
                    VSTT(b_ap, b_ap, -2.0, a_ap, ALU.mult, ALU.add,
                         inc=[(dv, 1)] if g2 == hi - 1 else ())
            stv *= 2

    h16_evac = [0, 0]
    for c in range(16):
        hps = o_ps if c % 2 == 0 else o2_ps
        for h2 in range(2):
            w0 = []
            if h2 == 0:
                w0 = _w(dv)
                if h16_evac[c % 2]:
                    w0.append((ac, h16_evac[c % 2]))
            MM(hps[:, h2 * 512: h2 * 512 + 512], h128_sb[:],
               xs_sb[:, c * TOK + h2 * 512: c * TOK + h2 * 512 + 512],
               wait=w0, inc=[(pe, 1)] if h2 == 1 else ())
        ACT(xs_sb[:, c * TOK:(c + 1) * TOK], hps[:], AF.Copy, wait=_w(pe),
            inc=[(ac, 1)])
        h16_evac[c % 2] = ac.n

    if "aown" in tap_outs:
        DMA("y", tap_outs["aown"].ap(), aown_sb[:], wait=[(sld, sld_aown1)])
    if "xs" in tap_outs:
        DMA("y", tap_outs["xs"].ap(), xs_sb[:], wait=_w(dv))

    p6_end = _w(pe, ac, dv, sld, gld)
    sbB1.close()
    sbB3.close()
    psC3.close()
    psC1.close()

    sbC = contextlib.ExitStack()
    u_sb = sb(sbC, "u_sb", [128, 64 * TOK], BF16)
    sbC_w = contextlib.ExitStack()
    wub_sb = [sb(sbC_w, f"wub{i}", [128, 16 * 128], BF16) for i in range(2)]
    wgb_sb = [sb(sbC_w, f"wgb{i}", [128, 16 * 128], BF16) for i in range(2)]
    gsil_sb = sb(sbC_w, "gsil_sb", [128, 1024], F32)

    psD = contextlib.ExitStack()
    up_ps = [ps(psD, "up_ps0", [128, 1024])]
    psD1 = contextlib.ExitStack()
    up_ps.append(ps(psD1, "up_ps1", [128, 1024]))
    psDg = contextlib.ExitStack()
    gt_ps = [ps(psDg, f"gt_ps{i}", [128, 1024]) for i in range(2)]

    wup_v = wup.ap().rearrange("(i p) o -> p i o", p=128)
    wg_v = wg.ap().rearrange("(i p) o -> p i o", p=128)
    ug_used = [0, 0]
    u_done = [0, 0]
    u_done_any = 0
    gt_free = [0, 0]
    for jo in range(64):
        bi = jo % 2
        DMA("g", wub_sb[bi].ap().rearrange("p (i o) -> p i o", i=16),
            wup_v[:, :, jo * 128:(jo + 1) * 128],
            wait=[(pe, ug_used[bi])] if ug_used[bi] else p6_end)
        DMA("g", wgb_sb[bi].ap().rearrange("p (i o) -> p i o", i=16),
            wg_v[:, :, jo * 128:(jo + 1) * 128])
        gld_b = gld.n
        for h2 in range(2):
            for i in range(16):
                w0 = []
                if i == 0 and h2 == 0:
                    w0 = [(gld, gld_b)]
                    if jo < 2:
                        w0 += p6_end
                    if u_done[bi]:
                        w0.append((dv, u_done[bi]))
                MM(up_ps[bi][:, h2 * 512: h2 * 512 + 512],
                   wub_sb[bi][:, i * 128:(i + 1) * 128],
                   xs_sb[:, i * TOK + h2 * 512: i * TOK + h2 * 512 + 512],
                   start=(i == 0), stop=(i == 15),
                   wait=w0,
                   inc=[(pe, 1)] if (i == 15 and h2 == 1) else ())
        for h2 in range(2):
            for i in range(16):
                w0 = []
                if i == 0 and h2 == 0 and gt_free[bi]:
                    w0 = [(ac, gt_free[bi])]
                MM(gt_ps[bi][:, h2 * 512: h2 * 512 + 512],
                   wgb_sb[bi][:, i * 128:(i + 1) * 128],
                   xs_sb[:, i * TOK + h2 * 512: i * TOK + h2 * 512 + 512],
                   start=(i == 0), stop=(i == 15),
                   wait=w0,
                   inc=[(pe, 1)] if (i == 15 and h2 == 1) else ())
        ug_used[bi] = pe.n
        wsil = _w(pe)
        if u_done_any:
            wsil.append((dv, u_done_any))
        ACT(gsil_sb[:], gt_ps[bi][:], AF.Silu, scale=sgate_sb[:, jo: jo + 1],
            wait=wsil, inc=[(ac, 1)])
        gt_free[bi] = ac.n
        VSTT(u_sb[:, jo * TOK:(jo + 1) * TOK], up_ps[bi][:],
             sup_sb[:, jo: jo + 1], gsil_sb[:], wait=_w(ac), inc=[(dv, 1)])
        u_done[bi] = dv.n
        u_done_any = dv.n
        lvl = 0
        while CASCADE and ((jo >> lvl) & 1) and lvl < 5:
            wdt2 = (1 << lvl) * TOK
            base = (jo - (2 << lvl) + 1) * TOK
            a_ap = u_sb[:, base: base + wdt2]
            b_ap = u_sb[:, base + wdt2: base + 2 * wdt2]
            VTT(ALU.add, a_ap, a_ap, b_ap)
            VSTT(b_ap, b_ap, -2.0, a_ap, ALU.mult, ALU.add, inc=[(dv, 1)])
            lvl += 1

    if "u0" in tap_outs:
        for c in range(8):
            DMA("y", tap_outs["u0"].ap()[:, c * TOK:(c + 1) * TOK],
                u_sb[:, c * TOK:(c + 1) * TOK], wait=_w(dv))

    if not CASCADE:
        butterflies(u_sb, 64, TOK)

    h_evac_ac = [0, 0]
    for p4 in range(4):
        base_a = p4 * 8 * TOK
        base_b = (32 + p4 * 8) * TOK
        a_ap = u_sb[:, base_a: base_a + 8 * TOK]
        b_ap = u_sb[:, base_b: base_b + 8 * TOK]
        VTT(ALU.add, a_ap, a_ap, b_ap, wait=_w(dv) if p4 == 0 else ())
        VSTT(b_ap, b_ap, -2.0, a_ap, ALU.mult, ALU.add, inc=[(dv, 1)])
        piece_dv = dv.n
        for c in (list(range(p4 * 8, p4 * 8 + 8))
                  + list(range(32 + p4 * 8, 32 + p4 * 8 + 8))):
            bb = c % 2
            for h2 in range(2):
                w0 = []
                if h2 == 0:
                    w0 = [(dv, piece_dv)]
                    if h_evac_ac[bb]:
                        w0.append((ac, h_evac_ac[bb]))
                MM(up_ps[bb][:, h2 * 512: h2 * 512 + 512], h128_sb[:],
                   u_sb[:, c * TOK + h2 * 512: c * TOK + h2 * 512 + 512],
                   wait=w0, inc=[(pe, 1)] if h2 == 1 else ())
            ACT(u_sb[:, c * TOK:(c + 1) * TOK], up_ps[bb][:], AF.Copy,
                wait=_w(pe), inc=[(ac, 1)])
            h_evac_ac[bb] = ac.n
    u_ready_ac = ac.n

    if "u" in tap_outs:
        for c in range(8):
            DMA("y", tap_outs["u"].ap()[:, c * TOK:(c + 1) * TOK],
                u_sb[:, c * TOK:(c + 1) * TOK], wait=_w(ac))

    psDg.close()
    psD1.close()
    sbB2.close()
    sbC_w.close()
    sbD = contextlib.ExitStack()
    wdb_sb = [sb(sbD, f"wdb{i}", [128, 32 * 128], BF16) for i in range(2)]
    x1r_sb = [sb(sbD, f"x1r{i}", [128, TOK], F32) for i in range(2)]
    v2t_sb = sb(sbD, "v2t_sb", [128, TOK], F32)
    out_sb = [sb(sbD, f"outsb{i}", [128, TOK], F32) for i in range(2)]
    vb2_sb = sb(sbD, "vb2_sb", [128, 4 * TOK], BF16)
    sqb2_sb = sb(sbD, "sqb2_sb", [128, 4 * TOK], BF16)
    mbc2_sb = sb(sbD, "mbc2_sb", [128, TOK], F32)
    rbc2_sb = sb(sbD, "rbc2_sb", [128, TOK], F32)
    t1f2_sb = sb(sbD, "t1f2_sb", [128, TOK], F32)

    psE = contextlib.ExitStack()
    mean2_ps = ps(psE, "mean2_ps", [128, 1024])
    msq2_ps = ps(psE, "msq2_ps", [128, 1024])

    wd_v = wd.ap().rearrange("(a p) o -> p a o", p=128)
    wd_used = [0, 0]
    v2_done = 0
    stats2_pe = 0
    sq2_ac = 0
    for jo in range(16):
        for half in range(2):
            hb = (jo * 2 + half) % 2
            DMA("g",
                wdb_sb[hb].ap().rearrange("p (a o) -> p a o", a=32),
                wd_v[:, half * 32:(half + 1) * 32, jo * 128:(jo + 1) * 128],
                wait=([(pe, wd_used[hb])] if wd_used[hb]
                      else [(ac, u_ready_ac)]))
            gld_wd = gld.n
            for a2 in range(32):
                for h2 in range(2):
                    a = half * 32 + a2
                    w0 = []
                    if h2 == 0 and a2 == 0:
                        w0 = [(gld, gld_wd), (ac, u_ready_ac)]
                        if half == 0 and v2_done:
                            w0.append((dv, v2_done))
                    MM(up_ps[0][:, h2 * 512: h2 * 512 + 512],
                       wdb_sb[hb][:, a2 * 128:(a2 + 1) * 128],
                       u_sb[:, a * TOK + h2 * 512: a * TOK + h2 * 512 + 512],
                       start=(a == 0), stop=(a == 63),
                       wait=w0,
                       inc=[(pe, 1)] if (h2 == 1 and a2 == 31) else ())
            wd_used[hb] = pe.n
        DMA("y", x1r_sb[jo % 2][:], x1_sp.ap()[jo * 128:(jo + 1) * 128, :],
            wait=[(sld, sld_x1), (ac, u_ready_ac)]
                 + ([(dv, v2_done)] if v2_done else []))
        sld_x1r = sld.n
        VSTT(v2t_sb[:], up_ps[0][:], sdp_sb[:, jo: jo + 1], x1r_sb[jo % 2][:],
             ALU.mult, ALU.add,
             wait=_w(pe) + [(sld, sld_x1r)]
                  + ([(ac, sq2_ac)] if sq2_ac else []),
             inc=[(dv, 1)])
        v2_done = dv.n
        DMA("y", v2_sp.ap()[jo * 128:(jo + 1) * 128, :], v2t_sb[:],
            wait=_w(dv))
        q4 = jo % 4
        VCP(vb2_sb[:, q4 * TOK:(q4 + 1) * TOK], v2t_sb[:],
            wait=[(pe, stats2_pe)] if (q4 == 0 and stats2_pe) else (),
            inc=[(dv, 1)])
        ACT(sqb2_sb[:, q4 * TOK:(q4 + 1) * TOK], v2t_sb[:], AF.Square,
            wait=_w(dv), inc=[(ac, 1)])
        sq2_ac = ac.n
        if q4 == 3:
            for qq in range(4):
                jj = jo - 3 + qq
                for h2 in range(2):
                    off = qq * TOK + h2 * 512
                    MM(mean2_ps[:, h2 * 512: h2 * 512 + 512],
                       onesb_sb[:, 0:128], vb2_sb[:, off: off + 512],
                       start=(jj == 0), stop=(jj == 15),
                       wait=_w(dv, ac) if (qq == 0 and h2 == 0) else ())
                    MM(msq2_ps[:, h2 * 512: h2 * 512 + 512],
                       onesb_sb[:, 0:128], sqb2_sb[:, off: off + 512],
                       start=(jj == 0), stop=(jj == 15),
                       inc=[(pe, 1)] if (qq == 3 and h2 == 1) else ())
            stats2_pe = pe.n
    sld_v2 = sld.n

    ln2sc = sqb2_sb.ap().bitcast(F32)
    ln_stats(mean2_ps, msq2_ps, mbc2_sb[:], rbc2_sb[:],
             ln2sc[:, 0:1024], ln2sc[:, 1024:2048])

    for jo in range(16):
        bi = jo % 2
        DMA("y", x1r_sb[bi][:], v2_sp.ap()[jo * 128:(jo + 1) * 128, :],
            wait=[(sld, sld_v2)] + _w(dv))
        sld_r = sld.n
        VTT(ALU.subtract, t1f2_sb[:], x1r_sb[bi][:], mbc2_sb[:],
            wait=[(sld, sld_r)] + _w(ac))
        VTT(ALU.mult, out_sb[bi][:], t1f2_sb[:], rbc2_sb[:], inc=[(dv, 1)])
        DMA("y", out_t.ap()[jo * 128:(jo + 1) * 128, :], out_sb[bi][:],
            wait=_w(dv))

    b.add("y", (lambda e: None), _w(sld, gld), ())

    sbD.close()
    psE.close()
    psD.close()
    sbC.close()

    with nc.Block() as block:
        block.tensor(lambda e: b.run("t", e))
        block.vector(lambda e: b.run("v", e))
        block.scalar(lambda e: b.run("a", e))
        block.sync(lambda e: b.run("y", e))

        block.gpsimd(lambda e: b.run("g", e))

    ctx.close()
    return nc


def _prep_inputs(ins):
    hs = np.asarray(ins["L0_attn_hidden_states"])[0]
    q_w = np.asarray(ins["L0_attn_q_w"])
    kv_a_w = np.asarray(ins["L0_attn_kv_a_w"])
    kv_b_w = np.asarray(ins["L0_attn_kv_b_w"])
    o_w = np.asarray(ins["L0_attn_o_w"])
    kv_norm_w = np.asarray(ins["L0_attn_kv_norm_w"])
    SU_ug = np.asarray(ins["L0_ffn_SU_ug"])
    SU_dp = np.asarray(ins["L0_ffn_SU_dp"])
    Wsc_ug = np.asarray(ins["L0_ffn_Wscale_ug"])
    Wsc_dp = np.asarray(ins["L0_ffn_Wscale_dp"])
    W_up = np.asarray(ins["L0_ffn_W_up"])
    W_gate = np.asarray(ins["L0_ffn_W_gate"])
    W_down = np.asarray(ins["L0_ffn_W_down"])

    xT = np.ascontiguousarray(hs.T)
    kvawT = np.ascontiguousarray(kv_a_w[RD:, :].T)
    owT = np.ascontiguousarray(o_w.T)
    wupT = np.ascontiguousarray(W_up.T)
    wgT = np.ascontiguousarray(W_gate.T)
    wdT = np.ascontiguousarray(W_down.T)

    cs_tab = _cs_table()
    h128b = _hadamard(128).astype(ml_dtypes.bfloat16)
    onesb = np.ones((128, 128), ml_dtypes.bfloat16)
    onesf = np.ones((1, 128), np.float32)

    kvnw = np.ascontiguousarray(kv_norm_w.reshape(4, 128).T)
    suug = np.ascontiguousarray(SU_ug.reshape(16, 128).T)
    wscup = np.ascontiguousarray(Wsc_ug[:INTER].reshape(64, 128).T)
    wscg = np.ascontiguousarray(Wsc_ug[INTER:].reshape(64, 128).T)
    sudp = np.ascontiguousarray(SU_dp.reshape(64, 128).T)
    wscdp = np.ascontiguousarray(Wsc_dp.reshape(16, 128).T)
    epsv = np.tile(np.array([[EPS_RMS, EPS_LN]], np.float32), (128, 1))

    maps = []
    for c in range(NCORES):
        maps.append({
            "xo": np.ascontiguousarray(xT[:, c * TOK:(c + 1) * TOK]),
            "cs": cs_tab,
            "h128": h128b,
            "onesb": onesb,
            "onesf": onesf,
            "qw": np.ascontiguousarray(q_w[c * 256:(c + 1) * 256, :].T),
            "kvaw": kvawT,
            "kvbw": np.ascontiguousarray(kv_b_w[c * 512:(c + 1) * 512, :].T),
            "ow": owT,
            "wup": wupT,
            "wg": wgT,
            "wd": wdT,
            "kvnw": kvnw,
            "suug": suug,
            "wscup": wscup,
            "wscg": wscg,
            "sudp": sudp,
            "wscdp": wscdp,
            "epsv": epsv,
        })
    return maps


def kernel(**ins):
    maps = _prep_inputs(ins)
    nc = build_nc()
    res = run_bass_kernel_spmd(nc, maps, core_ids=list(range(NCORES)))
    full = np.empty((D, S), np.float32)
    for c in range(NCORES):
        full[:, c * TOK:(c + 1) * TOK] = res.results[c]["out"]
    return np.ascontiguousarray(full.T)[None]


# version 22
# speedup vs baseline: 1.0785x; 1.0083x over previous
import contextlib
import dataclasses
import math

import numpy as np
import ml_dtypes

import concourse.bass as bass
import concourse.mybir as mybir
from concourse.bass_utils import run_bass_kernel_spmd
from concourse.engine_type import EngineType

F32 = mybir.dt.float32
BF16 = mybir.dt.bfloat16
AF = mybir.ActivationFunctionType
ALU = mybir.AluOpType
AX = mybir.AxisListType

NCORES = 8
S = 8192
D = 2048
NH = 16
HD = 128
KVR = 512
RD = 64
TOK = S // NCORES
NT = S // 512
SCALE = float(HD) ** -0.5
EPS_RMS = 1e-6
EPS_LN = 1e-5
INTER = 8192
RS2 = 1.0 / math.sqrt(2048.0)
RS8 = 1.0 / math.sqrt(8192.0)
CASCADE = True


def _yarn_cos_sin():
    MAX_POS, ORIG_POS, BASE = 131072, 8192, 500000.0
    BETA_FAST, BETA_SLOW = 32.0, 1.0
    scaling = MAX_POS / ORIG_POS
    pos_freqs = BASE ** (np.arange(0, RD, 2, dtype=np.float64) / RD)
    extrap = 1.0 / pos_freqs
    interp = 1.0 / (scaling * pos_freqs)
    low = max(math.floor(RD * math.log(ORIG_POS / (BETA_FAST * 2 * math.pi))
                         / (2 * math.log(BASE))), 0)
    high = min(math.ceil(RD * math.log(ORIG_POS / (BETA_SLOW * 2 * math.pi))
                         / (2 * math.log(BASE))), RD - 1)
    i = np.arange(RD // 2, dtype=np.float64)
    smooth = np.clip((i - low) / max(high - low, 1), 0.0, 1.0)
    inv_freq = ((1.0 - smooth) * interp + smooth * extrap).astype(np.float32)
    pos = np.arange(S, dtype=np.float32)
    freqs = pos[:, None] * inv_freq[None, :]
    emb = np.concatenate([freqs, freqs], axis=-1)
    mscale = np.float32(0.1 * math.log(scaling) + 1.0)
    return ((np.cos(emb) * mscale).astype(np.float32),
            (np.sin(emb) * mscale).astype(np.float32))


def _cs_table():
    cos, sin = _yarn_cos_sin()
    cosT, sinT = cos.T, sin.T
    tab = np.empty((128, S), np.float32)
    tab[0:64] = cosT
    tab[64:96] = sinT[32:64]
    tab[96:128] = -sinT[0:32]
    return tab


def _hadamard(n):
    h = np.array([[1.0]], np.float32)
    while h.shape[0] < n:
        h = np.block([[h, h], [h, -h]])
    return h.astype(np.float32)


class Sem:
    def __init__(self, ctx, nc, name):
        self.h = ctx.enter_context(nc.semaphore(name))
        self.n = 0


class B:
    ENGS = ("t", "v", "a", "g", "y")

    def __init__(self):
        self.q = {e: [] for e in B.ENGS}

    def add(self, eng, fn, wait=(), inc=()):
        waits = []
        for w in wait:
            if isinstance(w, Sem):
                waits.append((w.h, w.n))
            else:
                sem, v = w
                waits.append((sem.h if isinstance(sem, Sem) else sem, v))
        incs = []
        for sem, k in inc:
            sem.n += k
            incs.append((sem.h, k))
        self.q[eng].append((waits, fn, incs))

    def run(self, eng_name, eng):
        for waits, fn, incs in self.q[eng_name]:
            for h, v in waits:
                if v > 0:
                    eng.wait_ge(h, v)
            ins = fn(eng)
            for h, k in incs:
                ins.then_inc(h, k)


def _w(*sems):
    return [(s, s.n) for s in sems]


def build_nc(taps=()):
    nc = bass.Bass("TRN2", target_bir_lowering=False, debug=False)

    def din(name, shape, dtype=F32):
        return nc.dram_tensor(name, shape, dtype, kind="ExternalInput")

    xo = din("xo", [D, TOK])
    cs = din("cs", [128, S])
    h128 = din("h128", [128, 128], BF16)
    onesb = din("onesb", [128, 128], BF16)
    onesf = din("onesf", [1, 128])
    qw = din("qw", [D, 2 * HD])
    kvaw = din("kvaw", [D, KVR])
    kvbw = din("kvbw", [KVR, 4 * HD])
    ow = din("ow", [D, D])
    wup = din("wup", [D, INTER])
    wg = din("wg", [D, INTER])
    wd = din("wd", [INTER, D])
    kvnw = din("kvnw", [128, 4])
    suug = din("suug", [128, 16])
    wscup = din("wscup", [128, 64])
    wscg = din("wscg", [128, 64])
    sudp = din("sudp", [128, 64])
    wscdp = din("wscdp", [128, 16])
    epsv = din("epsv", [128, 2])

    out_t = nc.dram_tensor("out", [D, TOK], F32, kind="ExternalOutput")

    pid512 = nc.partition_id(engines=[EngineType.SP]) * 512

    cc_xl_in = nc.dram_tensor("cc_xl_in", [D + KVR, TOK], BF16)
    cc_xl_out = nc.dram_tensor("cc_xl_out", [NCORES * (D + KVR), TOK], BF16,
                               addr_space="Shared")
    cc_a_in0 = nc.dram_tensor("cc_a_in0", [2 * HD, 8 * 512], BF16)
    cc_a_out0 = nc.dram_tensor("cc_a_out0", [NCORES * 2 * HD, 8 * 512], BF16,
                               addr_space="Shared")
    cc_a_in1 = nc.dram_tensor("cc_a_in1", [2 * HD, 8 * 512], BF16)
    cc_a_out1 = nc.dram_tensor("cc_a_out1", [NCORES * 2 * HD, 8 * 512], BF16,
                               addr_space="Shared")
    x1_sp = nc.dram_tensor("x1_sp", [D, TOK], F32, kind="ExternalOutput")
    v2_sp = nc.dram_tensor("v2_sp", [D, TOK], F32, kind="ExternalOutput")

    tap_outs = {}
    for tname, shape, dty in (
        ("q0", [128, S], BF16), ("k0", [128, S], BF16), ("vt0", [128, S], BF16),
        ("lat", [128, 4 * 1024], BF16), ("negm", [128, 2], F32),
        ("aown", [128, 16 * TOK], BF16), ("xs", [128, 16 * TOK], BF16),
        ("u", [128, 8 * TOK], BF16), ("u0", [128, 8 * TOK], BF16),
        ("asb", [128, 512], BF16),
        ("mx", [1, 80], F32), ("sg", [128, 64], F32), ("sup", [128, 64], F32),
    ):
        if tname in taps:
            tap_outs[tname] = nc.dram_tensor("tap_" + tname, shape, dty,
                                             kind="ExternalOutput")

    ctx = contextlib.ExitStack()
    b = B()

    gld = Sem(ctx, nc, "gld")
    sld = Sem(ctx, nc, "sld")
    pe = Sem(ctx, nc, "pe")
    ac = Sem(ctx, nc, "ac")
    dv = Sem(ctx, nc, "dv")
    spe = Sem(ctx, nc, "spe")
    sac = Sem(ctx, nc, "sac")
    spv = Sem(ctx, nc, "spv")
    cc1 = Sem(ctx, nc, "cc1")
    cc2 = Sem(ctx, nc, "cc2")
    cc2b = Sem(ctx, nc, "cc2b")
    gp = Sem(ctx, nc, "gp")

    def DMA(eng, out_ap, in_ap, wait=()):
        sem = gld if eng == "g" else sld
        b.add(eng, (lambda e, o=out_ap, i=in_ap: e.dma_start(out=o, in_=i)),
              wait, [(sem, 16)])

    def MM(out_ap, lhsT, rhs, start=True, stop=True, wait=(), inc=()):
        b.add("t", (lambda e, o=out_ap, l=lhsT, r=rhs, s1=start, s2=stop:
                    e.matmul(o, l, r, start=s1, stop=s2)), wait, inc)

    def ACT(out_ap, in_ap, func, bias=0.0, scale=1.0, wait=(), inc=()):
        b.add("a", (lambda e, o=out_ap, i=in_ap, f=func, bi=bias, sc=scale:
                    e.activation(o, i, f, bias=bi, scale=sc)), wait, inc)

    def VTT(op, out_ap, in0, in1, wait=(), inc=()):
        b.add("v", (lambda e, o=out_ap, a0=in0, a1=in1, p=op:
                    e.tensor_tensor(o, a0, a1, p)), wait, inc)

    def VTS(out_ap, in0, scalar, op=ALU.mult, wait=(), inc=()):
        b.add("v", (lambda e, o=out_ap, a0=in0, scl=scalar, p=op:
                    e.tensor_single_scalar(o, a0, scl, p)), wait, inc)

    def VSTT(out_ap, in0, scalar, in1, op0=ALU.mult, op1=ALU.mult,
             wait=(), inc=()):
        b.add("v", (lambda e, o=out_ap, a0=in0, scl=scalar, a1=in1, p0=op0,
                    p1=op1: e.scalar_tensor_tensor(o, a0, scl, a1, p0, p1)),
              wait, inc)

    def VCP(out_ap, in_ap, wait=(), inc=()):
        b.add("v", (lambda e, o=out_ap, i=in_ap: e.tensor_copy(o, i)),
              wait, inc)

    def VRMAX(out_ap, in_ap, wait=(), inc=()):
        b.add("v", (lambda e, o=out_ap, i=in_ap: e.reduce_max(o, i, AX.X)),
              wait, inc)

    def VRECIP(out_ap, in_ap, wait=(), inc=()):
        b.add("v", (lambda e, o=out_ap, i=in_ap: e.reciprocal(o, i)),
              wait, inc)

    def sb(stack, name, shape, dtype, side=None):
        return stack.enter_context(nc.sbuf_tensor(name, shape, dtype,
                                                  side=side))

    def ps(stack, name, shape, dtype=F32):
        return stack.enter_context(nc.psum_tensor(name, shape, dtype))

    sbP = ctx.enter_context(contextlib.ExitStack())
    h128_sb = sb(sbP, "h128_sb", [128, 128], BF16)
    onesb_sb = sb(sbP, "onesb_sb", [128, 128], BF16)
    onesf_sb = sb(sbP, "onesf_sb", [1, 128], F32)
    kvnw_sb = sb(sbP, "kvnw_sb", [128, 4], F32)
    suug_sb = sb(sbP, "suug_sb", [128, 16], F32)
    sup_sb = sb(sbP, "sup_sb", [128, 64], F32)
    sgate_sb = sb(sbP, "sgate_sb", [128, 64], F32)
    sdp_sb = sb(sbP, "sdp_sb", [128, 16], F32)
    wscup_sb = sb(sbP, "wscup_sb", [128, 64], F32)
    sudp_sb = sb(sbP, "sudp_sb", [128, 64], F32)
    negm_sb = sb(sbP, "negm_sb", [128, 2], F32)
    mx_sb = sb(sbP, "mx_sb", [1, 80], F32)
    st1_sb = sb(sbP, "st1_sb", [1, 1024], F32)
    st2_sb = sb(sbP, "st2_sb", [1, 1024], F32)
    st3_sb = sb(sbP, "st3_sb", [1, 1024], F32)
    epsv_sb = sb(sbP, "epsv_sb", [128, 2], F32)

    sbA = contextlib.ExitStack()
    kvbw_sb = sb(sbA, "kvbw_sb", [128, 4 * 512], BF16)
    lat_sb = [sb(sbA, f"lat_sb{i}", [128, 4 * 512], BF16) for i in range(4)]
    nsq_sb = sb(sbA, "nsq_sb", [128, 512], BF16)
    Q_sb = [sb(sbA, f"q_sbh{h}", [128, S], BF16) for h in range(2)]

    sbA2 = contextlib.ExitStack()
    cs_cos = sb(sbA2, "cs_cos", [64, S], BF16, side="right")
    cs_sin = sb(sbA2, "cs_sin", [64, S], BF16, side="right")
    rt1_sb = sb(sbA2, "rt1_sb", [64, 1024], BF16, side="right")

    sbA1 = contextlib.ExitStack()
    qw_sb = sb(sbA1, "qw_sb", [128, 16 * 256], BF16, side="right")
    xt_sb = [sb(sbA1, f"xt_sb{i}", [128, 16 * 512], BF16, side="right")
             for i in range(2)]

    sbA0 = contextlib.ExitStack()
    xo_sb = sb(sbA0, "xo_sb", [128, 16 * TOK], BF16, side="right")
    kvaw_sb = sb(sbA0, "kvaw_sb", [128, 16 * 512], BF16, side="right")
    kvc_sb = sb(sbA0, "kvc_sb", [128, 8 * 512], BF16, side="right")
    sq_sb = sb(sbA0, "sq_sb", [128, 512], BF16, side="right")
    latst_sb = sb(sbA0, "latst_sb", [128, 512], BF16, side="right")
    st1f_sb = sb(sbA0, "st1f_sb", [128, 512], F32, side="right")
    st2f_sb = sb(sbA0, "st2f_sb", [128, 512], F32, side="right")

    psA_b = contextlib.ExitStack()
    bc1_ps = ps(psA_b, "bc1_ps", [128, 512])
    nrm_ps = ps(psA_b, "nrm_ps", [1, 512])
    rt2_ps = ps(psA_b, "rt2_ps", [64, 1024])
    psA_a = contextlib.ExitStack()
    kv_ps = ps(psA_a, "kv_ps", [128, 512])
    var_ps = ps(psA_a, "var_ps", [128, 512])
    q_ps = [ps(psA_a, f"q_ps{h}", [128, 512]) for h in range(2)]

    DMA("y", h128_sb[:], h128.ap())
    DMA("y", onesb_sb[:], onesb.ap())
    DMA("y", onesf_sb[:], onesf.ap())
    DMA("y", kvnw_sb[:], kvnw.ap())
    DMA("y", suug_sb[:], suug.ap())
    DMA("y", wscup_sb[:], wscup.ap())
    DMA("y", sudp_sb[:], sudp.ap())
    DMA("y", sgate_sb[:], wscg.ap())
    DMA("y", sdp_sb[:], wscdp.ap())
    DMA("y", epsv_sb[:], epsv.ap())

    DMA("g", xo_sb.ap().rearrange("p (i t) -> p i t", i=16),
        xo.ap().rearrange("(i p) t -> p i t", p=128))
    gld_xo = gld.n
    DMA("g", kvaw_sb.ap().rearrange("p (i o) -> p i o", i=16),
        kvaw.ap().rearrange("(i p) o -> p i o", p=128))
    gld_p0 = gld.n
    DMA("g", qw_sb.ap().rearrange("p (i o) -> p i o", i=16),
        qw.ap().rearrange("(i p) o -> p i o", p=128))
    DMA("g", kvbw_sb.ap().rearrange("p (j o) -> p j o", j=4),
        kvbw.ap().rearrange("(j p) o -> p j o", p=128))
    DMA("g", cs_cos[:], cs.ap()[0:64, :])
    DMA("g", cs_sin[:], cs.ap()[64:128, :])
    gld_csq = gld.n

    for i in range(16):
        DMA("y", cc_xl_in.ap()[i * 128:(i + 1) * 128, :],
            xo_sb[:, i * TOK:(i + 1) * TOK],
            wait=[(gld, gld_xo)] if i == 0 else ())

    sld_p0 = sld.n
    VSTT(sup_sb[:], wscup_sb[:], RS2, sudp_sb[:], wait=_w(sld), inc=[(dv, 1)])
    VTS(sgate_sb[:], sgate_sb[:], RS2, wait=_w(sld), inc=[(dv, 1)])
    VTS(sdp_sb[:], sdp_sb[:], RS8, wait=_w(sld), inc=[(dv, 1)])

    last_kv_evac = 0
    last_var_pe = 0
    for t2 in range(2):
        for j in range(4):
            blk = j * 2 + t2
            for i in range(16):
                w0 = []
                if i == 0:
                    w0 = [(gld, gld_p0)]
                    if last_kv_evac:
                        w0.append((dv, last_kv_evac))
                MM(kv_ps[:, 0:512],
                   kvaw_sb[:, i * 512 + j * 128: i * 512 + (j + 1) * 128],
                   xo_sb[:, i * TOK + t2 * 512: i * TOK + t2 * 512 + 512],
                   start=(i == 0), stop=(i == 15),
                   wait=w0, inc=[(pe, 1)] if i == 15 else ())
            VCP(kvc_sb[:, blk * 512:(blk + 1) * 512], kv_ps[:, 0:512],
                wait=_w(pe), inc=[(dv, 1)])
            last_kv_evac = dv.n
            wsq = _w(dv)
            if last_var_pe:
                wsq.append((pe, last_var_pe))
            else:
                wsq.append((sld, sld_p0))
            ACT(sq_sb[:], kvc_sb[:, blk * 512:(blk + 1) * 512], AF.Square,
                wait=wsq, inc=[(ac, 1)])
            MM(var_ps[:, 0:512], onesb_sb[:, 0:128], sq_sb[:],
               start=(j == 0), stop=(j == 3),
               wait=_w(ac), inc=[(pe, 1)])
            last_var_pe = pe.n
        ACT(st1f_sb[:], var_ps[:, 0:512], AF.Ln,
            bias=epsv_sb[:, 0:1], scale=1.0 / KVR, wait=_w(pe, dv),
            inc=[(ac, 1)])
        ACT(st2f_sb[:], st1f_sb[:], AF.Exp, scale=-0.5, inc=[(ac, 1)])
        for j in range(4):
            blk = j * 2 + t2
            wl = _w(ac, sld)
            VSTT(latst_sb[:], kvc_sb[:, blk * 512:(blk + 1) * 512],
                 kvnw_sb[:, j: j + 1], st2f_sb[:],
                 wait=wl, inc=[(dv, 1)])
            DMA("y", cc_xl_in.ap()[D + j * 128: D + (j + 1) * 128,
                                   t2 * 512: t2 * 512 + 512],
                latst_sb[:], wait=_w(dv))
    sld_lat = sld.n

    b.add("g", (lambda e: e.collective_compute(
        "AllGather", ALU.bypass, replica_groups=[list(range(NCORES))],
        ins=[cc_xl_in.ap().opt()], outs=[cc_xl_out.ap().opt()])),
        [(sld, sld_lat)], [(cc1, 1)])

    xv = cc_xl_out.ap().rearrange("(r i p) t -> p r i t", r=NCORES, i=20)
    xt_used_pe = [0, 0]
    q_evac_ac = [0, 0]
    xt_sld = [0] * NT

    def emit_xt_dma(tt):
        bi = tt % 2
        w0 = [(cc1, 1)]
        if xt_used_pe[bi]:
            w0.append((pe, xt_used_pe[bi]))
        DMA("y", xt_sb[bi].ap().rearrange("p (i t) -> p i t", i=16),
            xv[:, tt // 2, 0:16, (tt % 2) * 512:(tt % 2) * 512 + 512],
            wait=w0)
        xt_sld[tt] = sld.n

    emit_xt_dma(0)
    emit_xt_dma(1)
    qe = [[0, 0] for _ in range(NT)]
    for tt in range(NT):
        bi = tt % 2
        for h in range(2):
            for i in range(16):
                w0 = []
                if i == 0:
                    w0 = [(sld, xt_sld[tt])]
                    if q_evac_ac[h]:
                        w0.append((ac, q_evac_ac[h]))
                MM(q_ps[h][:, 0:512],
                   qw_sb[:, i * 256 + h * 128: i * 256 + (h + 1) * 128],
                   xt_sb[bi][:, i * 512:(i + 1) * 512],
                   start=(i == 0), stop=(i == 15),
                   wait=w0, inc=[(pe, 1)] if i == 15 else ())
            ACT(Q_sb[h][:, tt * 512:(tt + 1) * 512], q_ps[h][:, 0:512],
                AF.Copy, scale=SCALE, wait=_w(pe), inc=[(ac, 1)])
            q_evac_ac[h] = ac.n
            qe[tt][h] = ac.n
        xt_used_pe[bi] = pe.n
        if tt + 2 < NT:
            emit_xt_dma(tt + 2)

    rope_prev_dv = [0]

    def rope_step(tgt, stp, ac_val):
        sl = slice(stp * 1024, stp * 1024 + 1024)
        wg = [(ac, ac_val), (gld, gld_csq)]
        if rope_prev_dv[0]:
            wg.append((dv, rope_prev_dv[0]))
        b.add("g", (lambda e, o=rt1_sb[:], a0=tgt[0:64, sl],
                    a1=cs_cos[:, sl]:
                    e.tensor_tensor(o, a0, a1, ALU.mult)), wg, [(gp, 1)])
        VTT(ALU.mult, rt2_ps[:], tgt[0:64, sl], cs_sin[:, sl],
            wait=[(ac, ac_val), (gld, gld_csq)])
        VTT(ALU.add, tgt[0:32, sl], rt1_sb[0:32, :], rt2_ps[32:64, :],
            wait=_w(gp))
        VTT(ALU.add, tgt[32:64, sl], rt1_sb[32:64, :], rt2_ps[0:32, :],
            inc=[(dv, 1)])
        rope_prev_dv[0] = dv.n
        return dv.n

    nrm_state = [0, 0]

    def normmax_tile(tgt, mxslot, tt, rope_dv_val):
        ACT(nsq_sb[:], tgt[:, tt * 512:(tt + 1) * 512], AF.Square,
            wait=[(dv, rope_dv_val), (pe, nrm_state[0])], inc=[(ac, 1)])
        wm = _w(ac)
        if nrm_state[1]:
            wm.append((dv, nrm_state[1]))
        MM(nrm_ps[:, 0:512], onesb_sb[:, 0:1], nsq_sb[:],
           wait=wm, inc=[(pe, 1)])
        nrm_state[0] = pe.n
        VRMAX(mx_sb[:, mxslot: mxslot + 1], nrm_ps[:, 0:512],
              wait=_w(pe), inc=[(dv, 1)])
        nrm_state[1] = dv.n

    for s in range(8):
        d0 = rope_step(Q_sb[0], s, qe[2 * s + 1][0])
        d1 = rope_step(Q_sb[1], s, qe[2 * s + 1][1])
        for tt in (2 * s, 2 * s + 1):
            normmax_tile(Q_sb[0], 0 + tt, tt, d0)
            normmax_tile(Q_sb[1], 16 + tt, tt, d1)

    sbA0.close()
    sbA1.close()
    psA_a.close()

    sbA_kv = contextlib.ExitStack()
    K_sb = [sb(sbA_kv, f"k_sbh{h}", [128, S], BF16) for h in range(2)]
    Vt_sb = [sb(sbA_kv, f"vt_sbh{h}", [128, S], BF16) for h in range(2)]
    psA3 = contextlib.ExitStack()
    k_ps = [ps(psA3, f"k_ps{h}", [128, 512]) for h in range(2)]
    vt_ps = [ps(psA3, f"vt_ps{h}", [128, 512]) for h in range(2)]

    lat_used_pe = [0, 0, 0, 0]
    kv_evac = [0, 0]
    ke = [[0, 0] for _ in range(NT)]
    for tt in range(NT):
        bi = tt % 4
        r = tt // 2
        for j in range(4):
            w0 = []
            if j == 0:
                if tt == 0:
                    w0 = [(cc1, 1)]
                elif lat_used_pe[bi]:
                    w0 = [(pe, lat_used_pe[bi])]
            DMA("g", lat_sb[bi][:, j * 512:(j + 1) * 512],
                xv[:, r, 16 + j, (tt % 2) * 512: (tt % 2) * 512 + 512],
                wait=w0)
        gld_tt = gld.n
        for h in range(2):
            for j in range(4):
                w0 = []
                if j == 0:
                    w0 = [(gld, gld_tt)]
                    if kv_evac[h]:
                        w0.append((ac, kv_evac[h]))
                MM(k_ps[h][:, 0:512],
                   kvbw_sb[:, j * 512 + h * 256: j * 512 + h * 256 + 128],
                   lat_sb[bi][:, j * 512:(j + 1) * 512],
                   start=(j == 0), stop=(j == 3),
                   wait=w0, inc=[(pe, 1)] if j == 3 else ())
            for tc in range(4):
                for j in range(4):
                    MM(vt_ps[h][:, tc * 128:(tc + 1) * 128],
                       lat_sb[bi][:, j * 512 + tc * 128: j * 512 + (tc + 1) * 128],
                       kvbw_sb[:, j * 512 + h * 256 + 128:
                               j * 512 + h * 256 + 256],
                       start=(j == 0), stop=(j == 3),
                       inc=[(pe, 1)] if (tc == 3 and j == 3) else ())
            ACT(K_sb[h][:, tt * 512:(tt + 1) * 512], k_ps[h][:, 0:512],
                AF.Copy, wait=_w(pe), inc=[(ac, 1)])
            ke[tt][h] = ac.n
            ACT(Vt_sb[h][:, tt * 512:(tt + 1) * 512], vt_ps[h][:, 0:512],
                AF.Copy, inc=[(ac, 1)])
            kv_evac[h] = ac.n
        lat_used_pe[bi] = pe.n

    for s in range(8):
        d0 = rope_step(K_sb[0], s, ke[2 * s + 1][0])
        d1 = rope_step(K_sb[1], s, ke[2 * s + 1][1])
        for tt in (2 * s, 2 * s + 1):
            normmax_tile(K_sb[0], 32 + tt, tt, d0)
            normmax_tile(K_sb[1], 48 + tt, tt, d1)

    for h in range(2):
        VRMAX(mx_sb[:, 64 + h: 64 + h + 1], mx_sb[:, h * 16: h * 16 + 16],
              wait=_w(dv))
        VRMAX(mx_sb[:, 66 + h: 66 + h + 1],
              mx_sb[:, 32 + h * 16: 32 + h * 16 + 16],
              inc=[(dv, 1)] if h == 1 else ())
    ACT(mx_sb[:, 68:72], mx_sb[:, 64:68], AF.Ln, wait=_w(dv), inc=[(ac, 1)])
    VTT(ALU.add, mx_sb[:, 72:74], mx_sb[:, 68:70], mx_sb[:, 70:72],
        wait=_w(ac), inc=[(dv, 1)])
    ACT(mx_sb[:, 74:76], mx_sb[:, 72:74], AF.Exp, scale=0.5, wait=_w(dv),
        inc=[(ac, 1)])
    VTS(mx_sb[:, 76:78], mx_sb[:, 74:76], -1.0, wait=_w(ac), inc=[(dv, 1)])
    for h in range(2):
        MM(bc1_ps[:, 0:1], onesf_sb[:], mx_sb[:, 76 + h: 76 + h + 1],
           wait=_w(dv), inc=[(pe, 1)])
        VCP(negm_sb[:, h: h + 1], bc1_ps[:, 0:1], wait=_w(pe), inc=[(dv, 1)])

    if "sg" in tap_outs:
        DMA("y", tap_outs["sg"].ap(), sgate_sb[:], wait=_w(dv))
    if "sup" in tap_outs:
        DMA("y", tap_outs["sup"].ap(), sup_sb[:], wait=_w(dv))
    if "mx" in tap_outs:
        DMA("y", tap_outs["mx"].ap(), mx_sb[:], wait=_w(dv, ac, pe))
    if "q0" in tap_outs:
        DMA("y", tap_outs["q0"].ap(), Q_sb[0][:], wait=_w(dv, ac))
    if "k0" in tap_outs:
        DMA("y", tap_outs["k0"].ap(), K_sb[0][:], wait=_w(dv, ac))
    if "vt0" in tap_outs:
        DMA("y", tap_outs["vt0"].ap(), Vt_sb[0][:], wait=_w(dv, ac))
    if "negm" in tap_outs:
        DMA("y", tap_outs["negm"].ap(), negm_sb[:], wait=_w(dv))
    if "lat" in tap_outs:
        for j in range(4):
            DMA("y", tap_outs["lat"].ap()[:, j * 1024:(j + 1) * 1024],
                cc_xl_out.ap()[D + j * 128: D + (j + 1) * 128, 0:1024],
                wait=[(cc1, 1)])

    sbA2.close()
    psA3.close()
    psA_b.close()
    psB = contextlib.ExitStack()
    s_ps = [ps(psB, f"s_ps{i}", [128, 2 * 512]) for i in range(3)]
    pv_ps = ps(psB, "pv_ps", [128, 512])
    dd_ps = ps(psB, "dd_ps", [128, 512])
    sbA4 = contextlib.ExitStack()
    p_sb = [sb(sbA4, f"p_sb{i}", [128, 2 * 512], BF16) for i in range(3)]
    padd_sb = [sb(sbA4, f"padd{i}", [128, 512], BF16) for i in range(3)]
    padd2_sb = [sb(sbA4, f"padd2_{i}", [128, 512], BF16) for i in range(3)]
    araw_sb = [sb(sbA4, f"araw{i}", [128, 512], BF16) for i in range(2)]
    asb_sb = [sb(sbA4, f"asbx{i}", [128, 512], BF16) for i in range(2)]
    rec_sb = [sb(sbA4, f"rec{i}", [128, 512], F32) for i in range(2)]

    NGRP = 32
    s_bank_free = [0, 0, 0]
    p_buf_free = [0, 0, 0]
    pd2_free = [0, 0, 0]
    araw_free_dv = [0, 0]
    asb_free_sld = [0, 0]
    prev_mult_dv = 0
    prev_recip_dv = 0
    pre_attn = _w(pe, ac, dv, sld, gld)
    attn_entry = _w(dv, ac, pe)
    first_attn = True
    for parity in range(2):
      for h in range(2):
        for qt in range(parity, NT, 2):
            exp_sac = [0] * NGRP
            padd_dv = [0] * NGRP
            padd2_dv = [0] * (NGRP // 2)

            def emit_S(g):
                nonlocal first_attn
                bi = g % 3
                for i in range(2):
                    w0 = []
                    if i == 0:
                        if first_attn:
                            w0 += attn_entry
                            first_attn = False
                        if s_bank_free[bi]:
                            w0.append((sac, s_bank_free[bi]))
                    kcc = g * 2 + i
                    MM(s_ps[bi][:, i * 512:(i + 1) * 512],
                       K_sb[h][:, kcc * 128:(kcc + 1) * 128],
                       Q_sb[h][:, qt * 512:(qt + 1) * 512],
                       start=True, stop=True,
                       wait=w0, inc=[(spe, 1)] if i == 1 else ())

            def emit_exp(g):
                bi = g % 3
                we = _w(spe)
                if p_buf_free[bi]:
                    we.append((spv, p_buf_free[bi]))
                ACT(p_sb[bi][:, 0:1024], s_ps[bi][:, 0:1024], AF.Exp,
                    bias=negm_sb[:, h: h + 1], wait=we, inc=[(sac, 1)])
                s_bank_free[bi] = sac.n
                exp_sac[g] = sac.n

            def emit_padd(g):
                bi = g % 3
                VTT(ALU.add, padd_sb[bi][:], p_sb[bi][:, 0:512],
                    p_sb[bi][:, 512:1024], wait=[(sac, exp_sac[g])],
                    inc=[(dv, 1)])
                padd_dv[g] = dv.n
                if g % 2 == 1:
                    j = (g // 2) % 3
                    wp2 = [(dv, padd_dv[g])]
                    if pd2_free[j]:
                        wp2.append((spv, pd2_free[j]))
                    VTT(ALU.add, padd2_sb[j][:], padd_sb[(g - 1) % 3][:],
                        padd_sb[bi][:], wait=wp2, inc=[(dv, 1)])
                    padd2_dv[g // 2] = dv.n

            def emit_PVD(g):
                bi = g % 3
                for i in range(2):
                    kcc = g * 2 + i
                    wpv = []
                    if i == 0:
                        wpv = [(sac, exp_sac[g])]
                        if g == 0 and prev_mult_dv:
                            wpv.append((dv, prev_mult_dv))
                    MM(pv_ps[:, 0:512],
                       Vt_sb[h][:, kcc * 128:(kcc + 1) * 128],
                       p_sb[bi][:, i * 512:(i + 1) * 512],
                       start=(g == 0 and i == 0),
                       stop=(g == NGRP - 1 and i == 1),
                       wait=wpv,
                       inc=[(spv, 1)] if (g % 2 == 0 and i == 1) else ())
                if g % 2 == 1:
                    wdd = [(dv, padd2_dv[g // 2])]
                    if g == 1 and prev_recip_dv:
                        wdd.append((dv, prev_recip_dv))
                    MM(dd_ps[:, 0:512], onesb_sb[:, 0:128],
                       padd2_sb[(g // 2) % 3][:],
                       start=(g == 1), stop=(g == NGRP - 1),
                       wait=wdd, inc=[(spv, 1)])
                    pd2_free[(g // 2) % 3] = spv.n
                p_buf_free[bi] = spv.n

            emit_S(0)
            emit_exp(0)
            emit_padd(0)
            emit_S(1)
            emit_exp(1)
            emit_padd(1)
            for g in range(2, NGRP):
                emit_S(g)
                emit_exp(g)
                emit_padd(g)
                emit_PVD(g - 2)
            emit_PVD(NGRP - 2)
            emit_PVD(NGRP - 1)

            abi = qt % 2
            wq = _w(spv)
            if araw_free_dv[abi]:
                wq.append((dv, araw_free_dv[abi]))
            VCP(araw_sb[abi][:], pv_ps[:, 0:512], wait=wq, inc=[(dv, 1)])
            prev_mult_dv = dv.n
            VRECIP(rec_sb[abi][:], dd_ps[:, 0:512], inc=[(dv, 1)])
            prev_recip_dv = dv.n
            wv = _w(dv)
            if asb_free_sld[abi]:
                wv.append((sld, asb_free_sld[abi]))
            VTT(ALU.mult, asb_sb[abi][:], araw_sb[abi][:],
                rec_sb[abi][:], wait=wv, inc=[(dv, 1)])
            araw_free_dv[abi] = dv.n
            blk = qt // 2
            cc_t = cc_a_in0 if parity == 0 else cc_a_in1
            DMA("y", cc_t.ap()[h * 128:(h + 1) * 128,
                               blk * 512:(blk + 1) * 512],
                asb_sb[abi][:], wait=_w(dv))
            asb_free_sld[abi] = sld.n
            if "asb" in tap_outs and h == 0 and qt == 0:
                DMA("y", tap_outs["asb"].ap(), asb_sb[abi][:], wait=_w(dv))
      if parity == 0:
          sld_a0 = sld.n
          b.add("g", (lambda e: e.collective_compute(
              "AllGather", ALU.bypass, replica_groups=[list(range(NCORES))],
              ins=[cc_a_in0.ap().opt()], outs=[cc_a_out0.ap().opt()])),
              [(sld, sld_a0), (cc1, 1)], [(cc2, 1)])

    sld_a1 = sld.n
    b.add("g", (lambda e: e.collective_compute(
        "AllGather", ALU.bypass, replica_groups=[list(range(NCORES))],
        ins=[cc_a_in1.ap().opt()], outs=[cc_a_out1.ap().opt()])),
        [(sld, sld_a1), (cc2, 1)], [(cc2b, 1)])

    end_attn = _w(pe, ac, dv, sld, gld)
    sbA4.close()
    sbA_kv.close()
    sbA.close()
    psB.close()

    sbB2 = contextlib.ExitStack()
    xs_sb = sb(sbB2, "xs_sb", [128, 16 * TOK], BF16, side="right")

    sbB3 = contextlib.ExitStack()
    vb_sb = sb(sbB3, "vb_sb", [128, 4 * TOK], BF16, side="right")
    sqb_sb = sb(sbB3, "sqb_sb", [128, 4 * TOK], BF16, side="right")
    mbc_sb = sb(sbB3, "mbc_sb", [128, TOK], F32, side="right")
    rbc_sb = sb(sbB3, "rbc_sb", [128, TOK], F32, side="right")
    t1f_sb = sb(sbB3, "t1f_sb", [128, TOK], F32, side="right")

    sbB1 = contextlib.ExitStack()
    aown_sb = sb(sbB1, "aown_sb", [128, 16 * TOK], BF16, side="right")
    v1_sb = sb(sbB1, "v1_sb", [128, 16 * TOK], F32, side="right")
    owb_sb = [sb(sbB1, f"owb{i}", [128, 16 * 128], BF16, side="right")
              for i in range(2)]
    xr_sb = [sb(sbB1, f"xr{i}", [128, TOK], F32, side="right")
             for i in range(2)]

    psC1 = contextlib.ExitStack()
    o_ps = ps(psC1, "o_ps", [128, 1024])
    psC2 = contextlib.ExitStack()
    mean_ps = ps(psC2, "mean_ps", [128, 1024])
    msq_ps = ps(psC2, "msq_ps", [128, 1024])

    for i in range(16):
        src = cc_a_out0.ap()[i * 128:(i + 1) * 128, 0:512]
        src = dataclasses.replace(src, offset=src.offset + pid512)
        DMA("y", aown_sb[:, i * TOK: i * TOK + 512], src,
            wait=[(cc2, 1)] if i == 0 else ())
    sld_aown0 = sld.n
    for i in range(16):
        src = cc_a_out1.ap()[i * 128:(i + 1) * 128, 0:512]
        src = dataclasses.replace(src, offset=src.offset + pid512)
        DMA("y", aown_sb[:, i * TOK + 512: i * TOK + 1024], src,
            wait=[(cc2b, 1)] if i == 0 else ())
    sld_aown1 = sld.n

    ow_v = ow.ap().rearrange("(i p) o -> p i o", p=128)
    ow_used = [0, 0]
    o_done_dv = 0
    stats_pe = 0
    for jo in range(16):
        bi = jo % 2
        DMA("g", owb_sb[bi].ap().rearrange("p (i o) -> p i o", i=16),
            ow_v[:, :, jo * 128:(jo + 1) * 128],
            wait=[(pe, ow_used[bi])] if ow_used[bi] else pre_attn)
        gld_ow = gld.n
        DMA("y", xr_sb[bi][:], xo.ap()[jo * 128:(jo + 1) * 128, :],
            wait=[(dv, o_done_dv)] if o_done_dv else ())
        sld_xr = sld.n
        for h2 in range(2):
            for i in range(16):
                w0 = []
                if i == 0:
                    w0 = [(gld, gld_ow),
                          (sld, sld_aown0 if h2 == 0 else sld_aown1)]
                    if h2 == 0 and o_done_dv:
                        w0.append((dv, o_done_dv))
                MM(o_ps[:, h2 * 512: h2 * 512 + 512],
                   owb_sb[bi][:, i * 128:(i + 1) * 128],
                   aown_sb[:, i * TOK + h2 * 512: i * TOK + h2 * 512 + 512],
                   start=(i == 0), stop=(i == 15),
                   wait=w0,
                   inc=[(pe, 1)] if (i == 15 and h2 == 1) else ())
        ow_used[bi] = pe.n
        VTT(ALU.add, v1_sb[:, jo * TOK:(jo + 1) * TOK], o_ps[:], xr_sb[bi][:],
            wait=_w(pe) + [(sld, sld_xr)], inc=[(dv, 1)])
        o_done_dv = dv.n
        q4 = jo % 4
        VCP(vb_sb[:, q4 * TOK:(q4 + 1) * TOK],
            v1_sb[:, jo * TOK:(jo + 1) * TOK],
            wait=[(pe, stats_pe)] if (q4 == 0 and stats_pe) else (),
            inc=[(dv, 1)])
        ACT(sqb_sb[:, q4 * TOK:(q4 + 1) * TOK],
            v1_sb[:, jo * TOK:(jo + 1) * TOK], AF.Square,
            wait=_w(dv), inc=[(ac, 1)])
        if q4 == 3:
            for qq in range(4):
                jj = jo - 3 + qq
                for h2 in range(2):
                    off = qq * TOK + h2 * 512
                    MM(mean_ps[:, h2 * 512: h2 * 512 + 512],
                       onesb_sb[:, 0:128], vb_sb[:, off: off + 512],
                       start=(jj == 0), stop=(jj == 15),
                       wait=_w(dv, ac) if (qq == 0 and h2 == 0) else ())
                    MM(msq_ps[:, h2 * 512: h2 * 512 + 512],
                       onesb_sb[:, 0:128], sqb_sb[:, off: off + 512],
                       start=(jj == 0), stop=(jj == 15),
                       inc=[(pe, 1)] if (qq == 3 and h2 == 1) else ())
            stats_pe = pe.n

    def ln_stats(mean_src, msq_src, mbc_t, rbc_t, sca, scb):
        VTS(mbc_t, mean_src[:, 0:1024], 1.0 / D, wait=_w(pe),
            inc=[(dv, 1)])
        VTS(sca, msq_src[:, 0:1024], 1.0 / D, inc=[(dv, 1)])
        ACT(scb, mbc_t, AF.Square, wait=_w(dv), inc=[(ac, 1)])
        VTT(ALU.subtract, sca, sca, scb, wait=_w(ac), inc=[(dv, 1)])
        ACT(scb, sca, AF.Ln, bias=epsv_sb[:, 1:2], wait=_w(dv),
            inc=[(ac, 1)])
        ACT(rbc_t, scb, AF.Exp, scale=-0.5, inc=[(ac, 1)])

    ln1sc = sqb_sb.ap().bitcast(F32)
    ln_stats(mean_ps, msq_ps, mbc_sb[:], rbc_sb[:],
             ln1sc[:, 0:1024], ln1sc[:, 1024:2048])
    psC2.close()
    psC3 = contextlib.ExitStack()
    o2_ps = ps(psC3, "o2_ps", [128, 1024])

    for jo in range(16):
        sl = slice(jo * TOK, (jo + 1) * TOK)
        VTT(ALU.subtract, t1f_sb[:], v1_sb[:, sl], mbc_sb[:],
            wait=_w(dv, ac))
        VTT(ALU.mult, v1_sb[:, sl], t1f_sb[:], rbc_sb[:], inc=[(dv, 1)])
        DMA("y", x1_sp.ap()[jo * 128:(jo + 1) * 128, :], v1_sb[:, sl],
            wait=_w(dv))
        VTS(xs_sb[:, sl], v1_sb[:, sl], suug_sb[:, jo: jo + 1],
            inc=[(dv, 1)])
        lvl = 0
        while (jo >> lvl) & 1:
            wdt2 = (1 << lvl) * TOK
            base = (jo - (2 << lvl) + 1) * TOK
            a_ap = xs_sb[:, base: base + wdt2]
            b_ap = xs_sb[:, base + wdt2: base + 2 * wdt2]
            VTT(ALU.add, a_ap, a_ap, b_ap)
            VSTT(b_ap, b_ap, -2.0, a_ap, ALU.mult, ALU.add, inc=[(dv, 1)])
            lvl += 1
    sld_x1 = sld.n

    def butterflies(tgt, nck, width):
        stv = 1
        while stv < nck:
            hi = nck // (2 * stv)
            blkw = stv * width
            if 2 * blkw <= 16384:
                v3 = tgt.ap().rearrange("p (hi rest) -> p hi rest", hi=hi)
                a_ap = v3[:, :, 0:blkw]
                b_ap = v3[:, :, blkw: 2 * blkw]
                VTT(ALU.add, a_ap, a_ap, b_ap, wait=_w(dv))
                VSTT(b_ap, b_ap, -2.0, a_ap, ALU.mult, ALU.add,
                     inc=[(dv, 1)])
            else:
                for g2 in range(hi):
                    base = g2 * 2 * blkw
                    a_ap = tgt[:, base: base + blkw]
                    b_ap = tgt[:, base + blkw: base + 2 * blkw]
                    VTT(ALU.add, a_ap, a_ap, b_ap,
                        wait=_w(dv) if g2 == 0 else ())
                    VSTT(b_ap, b_ap, -2.0, a_ap, ALU.mult, ALU.add,
                         inc=[(dv, 1)] if g2 == hi - 1 else ())
            stv *= 2

    h16_evac = [0, 0]
    for c in range(16):
        hps = o_ps if c % 2 == 0 else o2_ps
        for h2 in range(2):
            w0 = []
            if h2 == 0:
                w0 = _w(dv)
                if h16_evac[c % 2]:
                    w0.append((ac, h16_evac[c % 2]))
            MM(hps[:, h2 * 512: h2 * 512 + 512], h128_sb[:],
               xs_sb[:, c * TOK + h2 * 512: c * TOK + h2 * 512 + 512],
               wait=w0, inc=[(pe, 1)] if h2 == 1 else ())
        ACT(xs_sb[:, c * TOK:(c + 1) * TOK], hps[:], AF.Copy, wait=_w(pe),
            inc=[(ac, 1)])
        h16_evac[c % 2] = ac.n

    if "aown" in tap_outs:
        DMA("y", tap_outs["aown"].ap(), aown_sb[:], wait=[(sld, sld_aown1)])
    if "xs" in tap_outs:
        DMA("y", tap_outs["xs"].ap(), xs_sb[:], wait=_w(dv))

    p6_end = _w(pe, ac, dv, sld, gld)
    sbB1.close()
    sbB3.close()
    psC3.close()
    psC1.close()

    sbC = contextlib.ExitStack()
    u_sb = sb(sbC, "u_sb", [128, 64 * TOK], BF16)
    sbC_w = contextlib.ExitStack()
    wub_sb = [sb(sbC_w, f"wub{i}", [128, 16 * 128], BF16) for i in range(2)]
    wgb_sb = [sb(sbC_w, f"wgb{i}", [128, 16 * 128], BF16) for i in range(2)]
    gsil_sb = sb(sbC_w, "gsil_sb", [128, 1024], F32)

    psD = contextlib.ExitStack()
    up_ps = [ps(psD, "up_ps0", [128, 1024])]
    psD1 = contextlib.ExitStack()
    up_ps.append(ps(psD1, "up_ps1", [128, 1024]))
    psDg = contextlib.ExitStack()
    gt_ps = [ps(psDg, f"gt_ps{i}", [128, 1024]) for i in range(2)]

    wup_v = wup.ap().rearrange("(i p) o -> p i o", p=128)
    wg_v = wg.ap().rearrange("(i p) o -> p i o", p=128)
    ug_used = [0, 0]
    u_done = [0, 0]
    u_done_any = 0
    gt_free = [0, 0]
    for jo in range(64):
        bi = jo % 2
        DMA("g", wub_sb[bi].ap().rearrange("p (i o) -> p i o", i=16),
            wup_v[:, :, jo * 128:(jo + 1) * 128],
            wait=[(pe, ug_used[bi])] if ug_used[bi] else p6_end)
        DMA("g", wgb_sb[bi].ap().rearrange("p (i o) -> p i o", i=16),
            wg_v[:, :, jo * 128:(jo + 1) * 128])
        gld_b = gld.n
        for h2 in range(2):
            for i in range(16):
                w0 = []
                if i == 0 and h2 == 0:
                    w0 = [(gld, gld_b)]
                    if jo < 2:
                        w0 += p6_end
                    if u_done[bi]:
                        w0.append((dv, u_done[bi]))
                MM(up_ps[bi][:, h2 * 512: h2 * 512 + 512],
                   wub_sb[bi][:, i * 128:(i + 1) * 128],
                   xs_sb[:, i * TOK + h2 * 512: i * TOK + h2 * 512 + 512],
                   start=(i == 0), stop=(i == 15),
                   wait=w0,
                   inc=[(pe, 1)] if (i == 15 and h2 == 1) else ())
        for h2 in range(2):
            for i in range(16):
                w0 = []
                if i == 0 and h2 == 0 and gt_free[bi]:
                    w0 = [(ac, gt_free[bi])]
                MM(gt_ps[bi][:, h2 * 512: h2 * 512 + 512],
                   wgb_sb[bi][:, i * 128:(i + 1) * 128],
                   xs_sb[:, i * TOK + h2 * 512: i * TOK + h2 * 512 + 512],
                   start=(i == 0), stop=(i == 15),
                   wait=w0,
                   inc=[(pe, 1)] if (i == 15 and h2 == 1) else ())
        ug_used[bi] = pe.n
        wsil = _w(pe)
        if u_done_any:
            wsil.append((dv, u_done_any))
        ACT(gsil_sb[:], gt_ps[bi][:], AF.Silu, scale=sgate_sb[:, jo: jo + 1],
            wait=wsil, inc=[(ac, 1)])
        gt_free[bi] = ac.n
        VSTT(u_sb[:, jo * TOK:(jo + 1) * TOK], up_ps[bi][:],
             sup_sb[:, jo: jo + 1], gsil_sb[:], wait=_w(ac), inc=[(dv, 1)])
        u_done[bi] = dv.n
        u_done_any = dv.n
        lvl = 0
        while CASCADE and ((jo >> lvl) & 1) and lvl < 5:
            wdt2 = (1 << lvl) * TOK
            base = (jo - (2 << lvl) + 1) * TOK
            a_ap = u_sb[:, base: base + wdt2]
            b_ap = u_sb[:, base + wdt2: base + 2 * wdt2]
            VTT(ALU.add, a_ap, a_ap, b_ap)
            VSTT(b_ap, b_ap, -2.0, a_ap, ALU.mult, ALU.add, inc=[(dv, 1)])
            lvl += 1

    if "u0" in tap_outs:
        for c in range(8):
            DMA("y", tap_outs["u0"].ap()[:, c * TOK:(c + 1) * TOK],
                u_sb[:, c * TOK:(c + 1) * TOK], wait=_w(dv))

    if not CASCADE:
        butterflies(u_sb, 64, TOK)

    h_evac_ac = [0, 0]
    for p4 in range(4):
        base_a = p4 * 8 * TOK
        base_b = (32 + p4 * 8) * TOK
        a_ap = u_sb[:, base_a: base_a + 8 * TOK]
        b_ap = u_sb[:, base_b: base_b + 8 * TOK]
        VTT(ALU.add, a_ap, a_ap, b_ap, wait=_w(dv) if p4 == 0 else ())
        VSTT(b_ap, b_ap, -2.0, a_ap, ALU.mult, ALU.add, inc=[(dv, 1)])
        piece_dv = dv.n
        for c in (list(range(p4 * 8, p4 * 8 + 8))
                  + list(range(32 + p4 * 8, 32 + p4 * 8 + 8))):
            bb = c % 2
            for h2 in range(2):
                w0 = []
                if h2 == 0:
                    w0 = [(dv, piece_dv)]
                    if h_evac_ac[bb]:
                        w0.append((ac, h_evac_ac[bb]))
                MM(up_ps[bb][:, h2 * 512: h2 * 512 + 512], h128_sb[:],
                   u_sb[:, c * TOK + h2 * 512: c * TOK + h2 * 512 + 512],
                   wait=w0, inc=[(pe, 1)] if h2 == 1 else ())
            ACT(u_sb[:, c * TOK:(c + 1) * TOK], up_ps[bb][:], AF.Copy,
                wait=_w(pe), inc=[(ac, 1)])
            h_evac_ac[bb] = ac.n
    u_ready_ac = ac.n

    if "u" in tap_outs:
        for c in range(8):
            DMA("y", tap_outs["u"].ap()[:, c * TOK:(c + 1) * TOK],
                u_sb[:, c * TOK:(c + 1) * TOK], wait=_w(ac))

    psDg.close()
    psD1.close()
    sbB2.close()
    sbC_w.close()
    sbD = contextlib.ExitStack()
    wdb_sb = [sb(sbD, f"wdb{i}", [128, 32 * 128], BF16) for i in range(2)]
    x1r_sb = [sb(sbD, f"x1r{i}", [128, TOK], F32) for i in range(2)]
    vread_sb = x1r_sb + [sb(sbD, f"vread{i}", [128, TOK], F32)
                         for i in range(2)]
    v2t_sb = sb(sbD, "v2t_sb", [128, TOK], F32)
    out_sb = [v2t_sb, sb(sbD, "outsb1", [128, TOK], F32)]
    vb2_sb = sb(sbD, "vb2_sb", [128, 4 * TOK], BF16)
    sqb2_sb = sb(sbD, "sqb2_sb", [128, 4 * TOK], BF16)
    mbc2_sb = sb(sbD, "mbc2_sb", [128, TOK], F32)
    rbc2_sb = sb(sbD, "rbc2_sb", [128, TOK], F32)

    psE = contextlib.ExitStack()
    mean2_ps = ps(psE, "mean2_ps", [128, 1024])
    msq2_ps = ps(psE, "msq2_ps", [128, 1024])

    wd_v = wd.ap().rearrange("(a p) o -> p a o", p=128)
    wd_used = [0, 0]
    v2_done = 0
    stats2_pe = 0
    sq2_ac = 0
    v2st = [0] * 16
    for jo in range(16):
        for half in range(2):
            hb = (jo * 2 + half) % 2
            DMA("g",
                wdb_sb[hb].ap().rearrange("p (a o) -> p a o", a=32),
                wd_v[:, half * 32:(half + 1) * 32, jo * 128:(jo + 1) * 128],
                wait=([(pe, wd_used[hb])] if wd_used[hb]
                      else [(ac, u_ready_ac)]))
            gld_wd = gld.n
            for a2 in range(32):
                for h2 in range(2):
                    a = half * 32 + a2
                    w0 = []
                    if h2 == 0 and a2 == 0:
                        w0 = [(gld, gld_wd), (ac, u_ready_ac)]
                        if half == 0 and v2_done:
                            w0.append((dv, v2_done))
                    MM(up_ps[0][:, h2 * 512: h2 * 512 + 512],
                       wdb_sb[hb][:, a2 * 128:(a2 + 1) * 128],
                       u_sb[:, a * TOK + h2 * 512: a * TOK + h2 * 512 + 512],
                       start=(a == 0), stop=(a == 63),
                       wait=w0,
                       inc=[(pe, 1)] if (h2 == 1 and a2 == 31) else ())
            wd_used[hb] = pe.n
        DMA("y", x1r_sb[jo % 2][:], x1_sp.ap()[jo * 128:(jo + 1) * 128, :],
            wait=[(sld, sld_x1), (ac, u_ready_ac)]
                 + ([(dv, v2_done)] if v2_done else []))
        sld_x1r = sld.n
        VSTT(v2t_sb[:], up_ps[0][:], sdp_sb[:, jo: jo + 1], x1r_sb[jo % 2][:],
             ALU.mult, ALU.add,
             wait=_w(pe) + [(sld, sld_x1r)]
                  + ([(ac, sq2_ac)] if sq2_ac else []),
             inc=[(dv, 1)])
        v2_done = dv.n
        DMA("y", v2_sp.ap()[jo * 128:(jo + 1) * 128, :], v2t_sb[:],
            wait=_w(dv))
        v2st[jo] = sld.n
        q4 = jo % 4
        VCP(vb2_sb[:, q4 * TOK:(q4 + 1) * TOK], v2t_sb[:],
            wait=[(pe, stats2_pe)] if (q4 == 0 and stats2_pe) else (),
            inc=[(dv, 1)])
        ACT(sqb2_sb[:, q4 * TOK:(q4 + 1) * TOK], v2t_sb[:], AF.Square,
            wait=_w(dv), inc=[(ac, 1)])
        sq2_ac = ac.n
        if q4 == 3:
            for qq in range(4):
                jj = jo - 3 + qq
                for h2 in range(2):
                    off = qq * TOK + h2 * 512
                    MM(mean2_ps[:, h2 * 512: h2 * 512 + 512],
                       onesb_sb[:, 0:128], vb2_sb[:, off: off + 512],
                       start=(jj == 0), stop=(jj == 15),
                       wait=_w(dv, ac) if (qq == 0 and h2 == 0) else ())
                    MM(msq2_ps[:, h2 * 512: h2 * 512 + 512],
                       onesb_sb[:, 0:128], sqb2_sb[:, off: off + 512],
                       start=(jj == 0), stop=(jj == 15),
                       inc=[(pe, 1)] if (qq == 3 and h2 == 1) else ())
            stats2_pe = pe.n
    sld_v2 = sld.n

    ln2sc = sqb2_sb.ap().bitcast(F32)
    ln_stats(mean2_ps, msq2_ps, mbc2_sb[:], rbc2_sb[:],
             ln2sc[:, 0:1024], ln2sc[:, 1024:2048])

    vrd_gld = [0] * 16

    def v2_read(jo, wextra=()):
        DMA("g", vread_sb[jo % 4][:], v2_sp.ap()[jo * 128:(jo + 1) * 128, :],
            wait=[(sld, v2st[jo])] + list(wextra))
        vrd_gld[jo] = gld.n

    for jo in range(4):
        v2_read(jo, [(dv, v2_done)] if jo < 2 else ())
    outst = [0, 0]
    for jo in range(16):
        bi = jo % 2
        VTT(ALU.subtract, vread_sb[jo % 4][:], vread_sb[jo % 4][:],
            mbc2_sb[:], wait=[(gld, vrd_gld[jo])] + _w(ac))
        wm = []
        if outst[bi]:
            wm = [(sld, outst[bi])]
        VTT(ALU.mult, out_sb[bi][:], vread_sb[jo % 4][:], rbc2_sb[:],
            wait=wm, inc=[(dv, 1)])
        if jo + 4 < 16:
            v2_read(jo + 4, [(dv, dv.n)])
        DMA("y", out_t.ap()[jo * 128:(jo + 1) * 128, :], out_sb[bi][:],
            wait=_w(dv))
        outst[bi] = sld.n

    b.add("y", (lambda e: None), _w(sld, gld), ())

    sbD.close()
    psE.close()
    psD.close()
    sbC.close()

    with nc.Block() as block:
        block.tensor(lambda e: b.run("t", e))
        block.vector(lambda e: b.run("v", e))
        block.scalar(lambda e: b.run("a", e))
        block.sync(lambda e: b.run("y", e))

        block.gpsimd(lambda e: b.run("g", e))

    ctx.close()
    return nc


def _prep_inputs(ins):
    hs = np.asarray(ins["L0_attn_hidden_states"])[0]
    q_w = np.asarray(ins["L0_attn_q_w"])
    kv_a_w = np.asarray(ins["L0_attn_kv_a_w"])
    kv_b_w = np.asarray(ins["L0_attn_kv_b_w"])
    o_w = np.asarray(ins["L0_attn_o_w"])
    kv_norm_w = np.asarray(ins["L0_attn_kv_norm_w"])
    SU_ug = np.asarray(ins["L0_ffn_SU_ug"])
    SU_dp = np.asarray(ins["L0_ffn_SU_dp"])
    Wsc_ug = np.asarray(ins["L0_ffn_Wscale_ug"])
    Wsc_dp = np.asarray(ins["L0_ffn_Wscale_dp"])
    W_up = np.asarray(ins["L0_ffn_W_up"])
    W_gate = np.asarray(ins["L0_ffn_W_gate"])
    W_down = np.asarray(ins["L0_ffn_W_down"])

    xT = np.ascontiguousarray(hs.T)
    kvawT = np.ascontiguousarray(kv_a_w[RD:, :].T)
    owT = np.ascontiguousarray(o_w.T)
    wupT = np.ascontiguousarray(W_up.T)
    wgT = np.ascontiguousarray(W_gate.T)
    wdT = np.ascontiguousarray(W_down.T)

    cs_tab = _cs_table()
    h128b = _hadamard(128).astype(ml_dtypes.bfloat16)
    onesb = np.ones((128, 128), ml_dtypes.bfloat16)
    onesf = np.ones((1, 128), np.float32)

    kvnw = np.ascontiguousarray(kv_norm_w.reshape(4, 128).T)
    suug = np.ascontiguousarray(SU_ug.reshape(16, 128).T)
    wscup = np.ascontiguousarray(Wsc_ug[:INTER].reshape(64, 128).T)
    wscg = np.ascontiguousarray(Wsc_ug[INTER:].reshape(64, 128).T)
    sudp = np.ascontiguousarray(SU_dp.reshape(64, 128).T)
    wscdp = np.ascontiguousarray(Wsc_dp.reshape(16, 128).T)
    epsv = np.tile(np.array([[EPS_RMS, EPS_LN]], np.float32), (128, 1))

    maps = []
    for c in range(NCORES):
        maps.append({
            "xo": np.ascontiguousarray(xT[:, c * TOK:(c + 1) * TOK]),
            "cs": cs_tab,
            "h128": h128b,
            "onesb": onesb,
            "onesf": onesf,
            "qw": np.ascontiguousarray(q_w[c * 256:(c + 1) * 256, :].T),
            "kvaw": kvawT,
            "kvbw": np.ascontiguousarray(kv_b_w[c * 512:(c + 1) * 512, :].T),
            "ow": owT,
            "wup": wupT,
            "wg": wgT,
            "wd": wdT,
            "kvnw": kvnw,
            "suug": suug,
            "wscup": wscup,
            "wscg": wscg,
            "sudp": sudp,
            "wscdp": wscdp,
            "epsv": epsv,
        })
    return maps


def kernel(**ins):
    maps = _prep_inputs(ins)
    nc = build_nc()
    res = run_bass_kernel_spmd(nc, maps, core_ids=list(range(NCORES)))
    full = np.empty((D, S), np.float32)
    for c in range(NCORES):
        full[:, c * TOK:(c + 1) * TOK] = res.results[c]["out"]
    return np.ascontiguousarray(full.T)[None]


# version 23
# speedup vs baseline: 1.0789x; 1.0004x over previous
import contextlib
import dataclasses
import math

import numpy as np
import ml_dtypes

import concourse.bass as bass
import concourse.mybir as mybir
from concourse.bass_utils import run_bass_kernel_spmd
from concourse.engine_type import EngineType

F32 = mybir.dt.float32
BF16 = mybir.dt.bfloat16
AF = mybir.ActivationFunctionType
ALU = mybir.AluOpType
AX = mybir.AxisListType

NCORES = 8
S = 8192
D = 2048
NH = 16
HD = 128
KVR = 512
RD = 64
TOK = S // NCORES
NT = S // 512
SCALE = float(HD) ** -0.5
EPS_RMS = 1e-6
EPS_LN = 1e-5
INTER = 8192
RS2 = 1.0 / math.sqrt(2048.0)
RS8 = 1.0 / math.sqrt(8192.0)
CASCADE = True


def _yarn_cos_sin():
    MAX_POS, ORIG_POS, BASE = 131072, 8192, 500000.0
    BETA_FAST, BETA_SLOW = 32.0, 1.0
    scaling = MAX_POS / ORIG_POS
    pos_freqs = BASE ** (np.arange(0, RD, 2, dtype=np.float64) / RD)
    extrap = 1.0 / pos_freqs
    interp = 1.0 / (scaling * pos_freqs)
    low = max(math.floor(RD * math.log(ORIG_POS / (BETA_FAST * 2 * math.pi))
                         / (2 * math.log(BASE))), 0)
    high = min(math.ceil(RD * math.log(ORIG_POS / (BETA_SLOW * 2 * math.pi))
                         / (2 * math.log(BASE))), RD - 1)
    i = np.arange(RD // 2, dtype=np.float64)
    smooth = np.clip((i - low) / max(high - low, 1), 0.0, 1.0)
    inv_freq = ((1.0 - smooth) * interp + smooth * extrap).astype(np.float32)
    pos = np.arange(S, dtype=np.float32)
    freqs = pos[:, None] * inv_freq[None, :]
    emb = np.concatenate([freqs, freqs], axis=-1)
    mscale = np.float32(0.1 * math.log(scaling) + 1.0)
    return ((np.cos(emb) * mscale).astype(np.float32),
            (np.sin(emb) * mscale).astype(np.float32))


def _cs_table():
    cos, sin = _yarn_cos_sin()
    cosT, sinT = cos.T, sin.T
    tab = np.empty((128, S), np.float32)
    tab[0:64] = cosT
    tab[64:96] = sinT[32:64]
    tab[96:128] = -sinT[0:32]
    return tab


def _hadamard(n):
    h = np.array([[1.0]], np.float32)
    while h.shape[0] < n:
        h = np.block([[h, h], [h, -h]])
    return h.astype(np.float32)


class Sem:
    def __init__(self, ctx, nc, name):
        self.h = ctx.enter_context(nc.semaphore(name))
        self.n = 0


class B:
    ENGS = ("t", "v", "a", "g", "y")

    def __init__(self):
        self.q = {e: [] for e in B.ENGS}

    def add(self, eng, fn, wait=(), inc=()):
        waits = []
        for w in wait:
            if isinstance(w, Sem):
                waits.append((w.h, w.n))
            else:
                sem, v = w
                waits.append((sem.h if isinstance(sem, Sem) else sem, v))
        incs = []
        for sem, k in inc:
            sem.n += k
            incs.append((sem.h, k))
        self.q[eng].append((waits, fn, incs))

    def run(self, eng_name, eng):
        for waits, fn, incs in self.q[eng_name]:
            for h, v in waits:
                if v > 0:
                    eng.wait_ge(h, v)
            ins = fn(eng)
            for h, k in incs:
                ins.then_inc(h, k)


def _w(*sems):
    return [(s, s.n) for s in sems]


def build_nc(taps=()):
    nc = bass.Bass("TRN2", target_bir_lowering=False, debug=False)

    def din(name, shape, dtype=F32):
        return nc.dram_tensor(name, shape, dtype, kind="ExternalInput")

    xo = din("xo", [D, TOK])
    cs = din("cs", [128, S])
    h128 = din("h128", [128, 128], BF16)
    onesb = din("onesb", [128, 128], BF16)
    onesf = din("onesf", [1, 128])
    qw = din("qw", [D, 2 * HD])
    kvaw = din("kvaw", [D, KVR])
    kvbw = din("kvbw", [KVR, 4 * HD])
    ow = din("ow", [D, D])
    wup = din("wup", [D, INTER])
    wg = din("wg", [D, INTER])
    wd = din("wd", [INTER, D])
    kvnw = din("kvnw", [128, 4])
    suug = din("suug", [128, 16])
    wscup = din("wscup", [128, 64])
    wscg = din("wscg", [128, 64])
    sudp = din("sudp", [128, 64])
    wscdp = din("wscdp", [128, 16])
    epsv = din("epsv", [128, 2])

    out_t = nc.dram_tensor("out", [D, TOK], F32, kind="ExternalOutput")

    pid512 = nc.partition_id(engines=[EngineType.SP]) * 512

    cc_xl_in = nc.dram_tensor("cc_xl_in", [D + KVR, TOK], BF16)
    cc_xl_out = nc.dram_tensor("cc_xl_out", [NCORES * (D + KVR), TOK], BF16,
                               addr_space="Shared")
    cc_a_in0 = nc.dram_tensor("cc_a_in0", [2 * HD, 8 * 512], BF16)
    cc_a_out0 = nc.dram_tensor("cc_a_out0", [NCORES * 2 * HD, 8 * 512], BF16,
                               addr_space="Shared")
    cc_a_in1 = nc.dram_tensor("cc_a_in1", [2 * HD, 8 * 512], BF16)
    cc_a_out1 = nc.dram_tensor("cc_a_out1", [NCORES * 2 * HD, 8 * 512], BF16,
                               addr_space="Shared")
    x1_sp = nc.dram_tensor("x1_sp", [D, TOK], F32, kind="ExternalOutput")
    v2_sp = nc.dram_tensor("v2_sp", [D, TOK], F32, kind="ExternalOutput")

    tap_outs = {}
    for tname, shape, dty in (
        ("q0", [128, S], BF16), ("k0", [128, S], BF16), ("vt0", [128, S], BF16),
        ("lat", [128, 4 * 1024], BF16), ("negm", [128, 2], F32),
        ("aown", [128, 16 * TOK], BF16), ("xs", [128, 16 * TOK], BF16),
        ("u", [128, 8 * TOK], BF16), ("u0", [128, 8 * TOK], BF16),
        ("asb", [128, 512], BF16),
        ("mx", [1, 80], F32), ("sg", [128, 64], F32), ("sup", [128, 64], F32),
    ):
        if tname in taps:
            tap_outs[tname] = nc.dram_tensor("tap_" + tname, shape, dty,
                                             kind="ExternalOutput")

    ctx = contextlib.ExitStack()
    b = B()

    gld = Sem(ctx, nc, "gld")
    sld = Sem(ctx, nc, "sld")
    pe = Sem(ctx, nc, "pe")
    ac = Sem(ctx, nc, "ac")
    dv = Sem(ctx, nc, "dv")
    spe = Sem(ctx, nc, "spe")
    sac = Sem(ctx, nc, "sac")
    spv = Sem(ctx, nc, "spv")
    cc1 = Sem(ctx, nc, "cc1")
    cc2 = Sem(ctx, nc, "cc2")
    cc2b = Sem(ctx, nc, "cc2b")
    gp = Sem(ctx, nc, "gp")

    def DMA(eng, out_ap, in_ap, wait=()):
        sem = gld if eng == "g" else sld
        b.add(eng, (lambda e, o=out_ap, i=in_ap: e.dma_start(out=o, in_=i)),
              wait, [(sem, 16)])

    def MM(out_ap, lhsT, rhs, start=True, stop=True, wait=(), inc=()):
        b.add("t", (lambda e, o=out_ap, l=lhsT, r=rhs, s1=start, s2=stop:
                    e.matmul(o, l, r, start=s1, stop=s2)), wait, inc)

    def ACT(out_ap, in_ap, func, bias=0.0, scale=1.0, wait=(), inc=()):
        b.add("a", (lambda e, o=out_ap, i=in_ap, f=func, bi=bias, sc=scale:
                    e.activation(o, i, f, bias=bi, scale=sc)), wait, inc)

    def VTT(op, out_ap, in0, in1, wait=(), inc=()):
        b.add("v", (lambda e, o=out_ap, a0=in0, a1=in1, p=op:
                    e.tensor_tensor(o, a0, a1, p)), wait, inc)

    def VTS(out_ap, in0, scalar, op=ALU.mult, wait=(), inc=()):
        b.add("v", (lambda e, o=out_ap, a0=in0, scl=scalar, p=op:
                    e.tensor_single_scalar(o, a0, scl, p)), wait, inc)

    def VSTT(out_ap, in0, scalar, in1, op0=ALU.mult, op1=ALU.mult,
             wait=(), inc=()):
        b.add("v", (lambda e, o=out_ap, a0=in0, scl=scalar, a1=in1, p0=op0,
                    p1=op1: e.scalar_tensor_tensor(o, a0, scl, a1, p0, p1)),
              wait, inc)

    def VCP(out_ap, in_ap, wait=(), inc=()):
        b.add("v", (lambda e, o=out_ap, i=in_ap: e.tensor_copy(o, i)),
              wait, inc)

    def VRMAX(out_ap, in_ap, wait=(), inc=()):
        b.add("v", (lambda e, o=out_ap, i=in_ap: e.reduce_max(o, i, AX.X)),
              wait, inc)

    def VRECIP(out_ap, in_ap, wait=(), inc=()):
        b.add("v", (lambda e, o=out_ap, i=in_ap: e.reciprocal(o, i)),
              wait, inc)

    def sb(stack, name, shape, dtype, side=None):
        return stack.enter_context(nc.sbuf_tensor(name, shape, dtype,
                                                  side=side))

    def ps(stack, name, shape, dtype=F32):
        return stack.enter_context(nc.psum_tensor(name, shape, dtype))

    sbP = ctx.enter_context(contextlib.ExitStack())
    h128_sb = sb(sbP, "h128_sb", [128, 128], BF16)
    onesb_sb = sb(sbP, "onesb_sb", [128, 128], BF16)
    onesf_sb = sb(sbP, "onesf_sb", [1, 128], F32)
    kvnw_sb = sb(sbP, "kvnw_sb", [128, 4], F32)
    suug_sb = sb(sbP, "suug_sb", [128, 16], F32)
    sup_sb = sb(sbP, "sup_sb", [128, 64], F32)
    sgate_sb = sb(sbP, "sgate_sb", [128, 64], F32)
    sdp_sb = sb(sbP, "sdp_sb", [128, 16], F32)
    wscup_sb = sb(sbP, "wscup_sb", [128, 64], F32)
    sudp_sb = sb(sbP, "sudp_sb", [128, 64], F32)
    negm_sb = sb(sbP, "negm_sb", [128, 2], F32)
    mx_sb = sb(sbP, "mx_sb", [1, 80], F32)
    st1_sb = sb(sbP, "st1_sb", [1, 1024], F32)
    st2_sb = sb(sbP, "st2_sb", [1, 1024], F32)
    st3_sb = sb(sbP, "st3_sb", [1, 1024], F32)
    epsv_sb = sb(sbP, "epsv_sb", [128, 2], F32)

    sbA = contextlib.ExitStack()
    kvbw_sb = sb(sbA, "kvbw_sb", [128, 4 * 512], BF16)
    lat_sb = [sb(sbA, f"lat_sb{i}", [128, 4 * 512], BF16) for i in range(4)]
    nsq_sb = sb(sbA, "nsq_sb", [128, 512], BF16)
    Q_sb = [sb(sbA, f"q_sbh{h}", [128, S], BF16) for h in range(2)]

    sbA2 = contextlib.ExitStack()
    cs_cos = sb(sbA2, "cs_cos", [64, S], BF16, side="right")
    cs_sin = sb(sbA2, "cs_sin", [64, S], BF16, side="right")
    rt1_sb = sb(sbA2, "rt1_sb", [64, 1024], BF16, side="right")

    sbA1 = contextlib.ExitStack()
    qw_sb = sb(sbA1, "qw_sb", [128, 16 * 256], BF16, side="right")
    xt_sb = [sb(sbA1, f"xt_sb{i}", [128, 16 * 512], BF16, side="right")
             for i in range(2)]

    sbA0 = contextlib.ExitStack()
    xo_sb = sb(sbA0, "xo_sb", [128, 16 * TOK], BF16, side="right")
    kvaw_sb = sb(sbA0, "kvaw_sb", [128, 16 * 512], BF16, side="right")
    kvc_sb = sb(sbA0, "kvc_sb", [128, 8 * 512], BF16, side="right")
    sq_sb = sb(sbA0, "sq_sb", [128, 512], BF16, side="right")
    latst_sb = sb(sbA0, "latst_sb", [128, 512], BF16, side="right")
    st1f_sb = sb(sbA0, "st1f_sb", [128, 512], F32, side="right")
    st2f_sb = sb(sbA0, "st2f_sb", [128, 512], F32, side="right")

    psA_b = contextlib.ExitStack()
    bc1_ps = ps(psA_b, "bc1_ps", [128, 512])
    nrm_ps = ps(psA_b, "nrm_ps", [1, 512])
    rt2_ps = ps(psA_b, "rt2_ps", [64, 1024])
    psA_a = contextlib.ExitStack()
    kv_ps = ps(psA_a, "kv_ps", [128, 512])
    var_ps = ps(psA_a, "var_ps", [128, 512])
    q_ps = [ps(psA_a, f"q_ps{h}", [128, 512]) for h in range(2)]

    DMA("y", h128_sb[:], h128.ap())
    DMA("y", onesb_sb[:], onesb.ap())
    DMA("y", onesf_sb[:], onesf.ap())
    DMA("y", kvnw_sb[:], kvnw.ap())
    DMA("y", suug_sb[:], suug.ap())
    DMA("y", wscup_sb[:], wscup.ap())
    DMA("y", sudp_sb[:], sudp.ap())
    DMA("y", sgate_sb[:], wscg.ap())
    DMA("y", sdp_sb[:], wscdp.ap())
    DMA("y", epsv_sb[:], epsv.ap())

    DMA("g", xo_sb.ap().rearrange("p (i t) -> p i t", i=16),
        xo.ap().rearrange("(i p) t -> p i t", p=128))
    gld_xo = gld.n
    DMA("g", kvaw_sb.ap().rearrange("p (i o) -> p i o", i=16),
        kvaw.ap().rearrange("(i p) o -> p i o", p=128))
    gld_p0 = gld.n
    DMA("g", qw_sb.ap().rearrange("p (i o) -> p i o", i=16),
        qw.ap().rearrange("(i p) o -> p i o", p=128))
    DMA("g", kvbw_sb.ap().rearrange("p (j o) -> p j o", j=4),
        kvbw.ap().rearrange("(j p) o -> p j o", p=128))
    DMA("g", cs_cos[:], cs.ap()[0:64, :])
    DMA("g", cs_sin[:], cs.ap()[64:128, :])
    gld_csq = gld.n

    for i in range(16):
        DMA("y", cc_xl_in.ap()[i * 128:(i + 1) * 128, :],
            xo_sb[:, i * TOK:(i + 1) * TOK],
            wait=[(gld, gld_xo)] if i == 0 else ())

    sld_p0 = sld.n
    VSTT(sup_sb[:], wscup_sb[:], RS2, sudp_sb[:], wait=_w(sld), inc=[(dv, 1)])
    VTS(sgate_sb[:], sgate_sb[:], RS2, wait=_w(sld), inc=[(dv, 1)])
    VTS(sdp_sb[:], sdp_sb[:], RS8, wait=_w(sld), inc=[(dv, 1)])

    last_kv_evac = 0
    last_var_pe = 0
    for t2 in range(2):
        for j in range(4):
            blk = j * 2 + t2
            for i in range(16):
                w0 = []
                if i == 0:
                    w0 = [(gld, gld_p0)]
                    if last_kv_evac:
                        w0.append((dv, last_kv_evac))
                MM(kv_ps[:, 0:512],
                   kvaw_sb[:, i * 512 + j * 128: i * 512 + (j + 1) * 128],
                   xo_sb[:, i * TOK + t2 * 512: i * TOK + t2 * 512 + 512],
                   start=(i == 0), stop=(i == 15),
                   wait=w0, inc=[(pe, 1)] if i == 15 else ())
            VCP(kvc_sb[:, blk * 512:(blk + 1) * 512], kv_ps[:, 0:512],
                wait=_w(pe), inc=[(dv, 1)])
            last_kv_evac = dv.n
            wsq = _w(dv)
            if last_var_pe:
                wsq.append((pe, last_var_pe))
            else:
                wsq.append((sld, sld_p0))
            ACT(sq_sb[:], kvc_sb[:, blk * 512:(blk + 1) * 512], AF.Square,
                wait=wsq, inc=[(ac, 1)])
            MM(var_ps[:, 0:512], onesb_sb[:, 0:128], sq_sb[:],
               start=(j == 0), stop=(j == 3),
               wait=_w(ac), inc=[(pe, 1)])
            last_var_pe = pe.n
        ACT(st1f_sb[:], var_ps[:, 0:512], AF.Ln,
            bias=epsv_sb[:, 0:1], scale=1.0 / KVR, wait=_w(pe, dv),
            inc=[(ac, 1)])
        ACT(st2f_sb[:], st1f_sb[:], AF.Exp, scale=-0.5, inc=[(ac, 1)])
        for j in range(4):
            blk = j * 2 + t2
            wl = _w(ac, sld)
            VSTT(latst_sb[:], kvc_sb[:, blk * 512:(blk + 1) * 512],
                 kvnw_sb[:, j: j + 1], st2f_sb[:],
                 wait=wl, inc=[(dv, 1)])
            DMA("y", cc_xl_in.ap()[D + j * 128: D + (j + 1) * 128,
                                   t2 * 512: t2 * 512 + 512],
                latst_sb[:], wait=_w(dv))
    sld_lat = sld.n

    b.add("g", (lambda e: e.collective_compute(
        "AllGather", ALU.bypass, replica_groups=[list(range(NCORES))],
        ins=[cc_xl_in.ap().opt()], outs=[cc_xl_out.ap().opt()])),
        [(sld, sld_lat)], [(cc1, 1)])

    xv = cc_xl_out.ap().rearrange("(r i p) t -> p r i t", r=NCORES, i=20)
    xt_used_pe = [0, 0]
    q_evac_ac = [0, 0]
    xt_sld = [0] * NT

    def emit_xt_dma(tt):
        bi = tt % 2
        w0 = [(cc1, 1)]
        if xt_used_pe[bi]:
            w0.append((pe, xt_used_pe[bi]))
        DMA("y", xt_sb[bi].ap().rearrange("p (i t) -> p i t", i=16),
            xv[:, tt // 2, 0:16, (tt % 2) * 512:(tt % 2) * 512 + 512],
            wait=w0)
        xt_sld[tt] = sld.n

    emit_xt_dma(0)
    emit_xt_dma(1)
    qe = [[0, 0] for _ in range(NT)]
    for tt in range(NT):
        bi = tt % 2
        for h in range(2):
            for i in range(16):
                w0 = []
                if i == 0:
                    w0 = [(sld, xt_sld[tt])]
                    if q_evac_ac[h]:
                        w0.append((ac, q_evac_ac[h]))
                MM(q_ps[h][:, 0:512],
                   qw_sb[:, i * 256 + h * 128: i * 256 + (h + 1) * 128],
                   xt_sb[bi][:, i * 512:(i + 1) * 512],
                   start=(i == 0), stop=(i == 15),
                   wait=w0, inc=[(pe, 1)] if i == 15 else ())
            ACT(Q_sb[h][:, tt * 512:(tt + 1) * 512], q_ps[h][:, 0:512],
                AF.Copy, scale=SCALE, wait=_w(pe), inc=[(ac, 1)])
            q_evac_ac[h] = ac.n
            qe[tt][h] = ac.n
        xt_used_pe[bi] = pe.n
        if tt + 2 < NT:
            emit_xt_dma(tt + 2)

    rope_prev_dv = [0]

    def rope_step(tgt, stp, ac_val):
        sl = slice(stp * 1024, stp * 1024 + 1024)
        wg = [(ac, ac_val), (gld, gld_csq)]
        if rope_prev_dv[0]:
            wg.append((dv, rope_prev_dv[0]))
        b.add("g", (lambda e, o=rt1_sb[:], a0=tgt[0:64, sl],
                    a1=cs_cos[:, sl]:
                    e.tensor_tensor(o, a0, a1, ALU.mult)), wg, [(gp, 1)])
        VTT(ALU.mult, rt2_ps[:], tgt[0:64, sl], cs_sin[:, sl],
            wait=[(ac, ac_val), (gld, gld_csq)])
        VTT(ALU.add, tgt[0:32, sl], rt1_sb[0:32, :], rt2_ps[32:64, :],
            wait=_w(gp))
        VTT(ALU.add, tgt[32:64, sl], rt1_sb[32:64, :], rt2_ps[0:32, :],
            inc=[(dv, 1)])
        rope_prev_dv[0] = dv.n
        return dv.n

    nrm_state = [0, 0]

    def normmax_tile(tgt, mxslot, tt, rope_dv_val):
        ACT(nsq_sb[:], tgt[:, tt * 512:(tt + 1) * 512], AF.Square,
            wait=[(dv, rope_dv_val), (pe, nrm_state[0])], inc=[(ac, 1)])
        wm = _w(ac)
        if nrm_state[1]:
            wm.append((dv, nrm_state[1]))
        MM(nrm_ps[:, 0:512], onesb_sb[:, 0:1], nsq_sb[:],
           wait=wm, inc=[(pe, 1)])
        nrm_state[0] = pe.n
        VRMAX(mx_sb[:, mxslot: mxslot + 1], nrm_ps[:, 0:512],
              wait=_w(pe), inc=[(dv, 1)])
        nrm_state[1] = dv.n

    for s in range(8):
        d0 = rope_step(Q_sb[0], s, qe[2 * s + 1][0])
        d1 = rope_step(Q_sb[1], s, qe[2 * s + 1][1])
        for tt in (2 * s, 2 * s + 1):
            normmax_tile(Q_sb[0], 0 + tt, tt, d0)
            normmax_tile(Q_sb[1], 16 + tt, tt, d1)

    sbA0.close()
    sbA1.close()
    psA_a.close()

    sbA_kv = contextlib.ExitStack()
    K_sb = [sb(sbA_kv, f"k_sbh{h}", [128, S], BF16) for h in range(2)]
    Vt_sb = [sb(sbA_kv, f"vt_sbh{h}", [128, S], BF16) for h in range(2)]
    psA3 = contextlib.ExitStack()
    k_ps = [ps(psA3, f"k_ps{h}", [128, 512]) for h in range(2)]
    vt_ps = [ps(psA3, f"vt_ps{h}", [128, 512]) for h in range(2)]

    lat_used_pe = [0, 0, 0, 0]
    kv_evac = [0, 0]
    ke = [[0, 0] for _ in range(NT)]
    for tt in range(NT):
        bi = tt % 4
        r = tt // 2
        for j in range(4):
            w0 = []
            if j == 0:
                if tt == 0:
                    w0 = [(cc1, 1)]
                elif lat_used_pe[bi]:
                    w0 = [(pe, lat_used_pe[bi])]
            DMA("g", lat_sb[bi][:, j * 512:(j + 1) * 512],
                xv[:, r, 16 + j, (tt % 2) * 512: (tt % 2) * 512 + 512],
                wait=w0)
        gld_tt = gld.n
        for h in range(2):
            for j in range(4):
                w0 = []
                if j == 0:
                    w0 = [(gld, gld_tt)]
                    if kv_evac[h]:
                        w0.append((ac, kv_evac[h]))
                MM(k_ps[h][:, 0:512],
                   kvbw_sb[:, j * 512 + h * 256: j * 512 + h * 256 + 128],
                   lat_sb[bi][:, j * 512:(j + 1) * 512],
                   start=(j == 0), stop=(j == 3),
                   wait=w0, inc=[(pe, 1)] if j == 3 else ())
            for tc in range(4):
                for j in range(4):
                    MM(vt_ps[h][:, tc * 128:(tc + 1) * 128],
                       lat_sb[bi][:, j * 512 + tc * 128: j * 512 + (tc + 1) * 128],
                       kvbw_sb[:, j * 512 + h * 256 + 128:
                               j * 512 + h * 256 + 256],
                       start=(j == 0), stop=(j == 3),
                       inc=[(pe, 1)] if (tc == 3 and j == 3) else ())
            ACT(K_sb[h][:, tt * 512:(tt + 1) * 512], k_ps[h][:, 0:512],
                AF.Copy, wait=_w(pe), inc=[(ac, 1)])
            ke[tt][h] = ac.n
            ACT(Vt_sb[h][:, tt * 512:(tt + 1) * 512], vt_ps[h][:, 0:512],
                AF.Copy, inc=[(ac, 1)])
            kv_evac[h] = ac.n
        lat_used_pe[bi] = pe.n

    for s in range(8):
        d0 = rope_step(K_sb[0], s, ke[2 * s + 1][0])
        d1 = rope_step(K_sb[1], s, ke[2 * s + 1][1])
        for tt in (2 * s, 2 * s + 1):
            normmax_tile(K_sb[0], 32 + tt, tt, d0)
            normmax_tile(K_sb[1], 48 + tt, tt, d1)

    for h in range(2):
        VRMAX(mx_sb[:, 64 + h: 64 + h + 1], mx_sb[:, h * 16: h * 16 + 16],
              wait=_w(dv))
        VRMAX(mx_sb[:, 66 + h: 66 + h + 1],
              mx_sb[:, 32 + h * 16: 32 + h * 16 + 16],
              inc=[(dv, 1)] if h == 1 else ())
    ACT(mx_sb[:, 68:72], mx_sb[:, 64:68], AF.Ln, wait=_w(dv), inc=[(ac, 1)])
    VTT(ALU.add, mx_sb[:, 72:74], mx_sb[:, 68:70], mx_sb[:, 70:72],
        wait=_w(ac), inc=[(dv, 1)])
    ACT(mx_sb[:, 74:76], mx_sb[:, 72:74], AF.Exp, scale=0.5, wait=_w(dv),
        inc=[(ac, 1)])
    VTS(mx_sb[:, 76:78], mx_sb[:, 74:76], -1.0, wait=_w(ac), inc=[(dv, 1)])
    for h in range(2):
        MM(bc1_ps[:, 0:1], onesf_sb[:], mx_sb[:, 76 + h: 76 + h + 1],
           wait=_w(dv), inc=[(pe, 1)])
        VCP(negm_sb[:, h: h + 1], bc1_ps[:, 0:1], wait=_w(pe), inc=[(dv, 1)])

    if "sg" in tap_outs:
        DMA("y", tap_outs["sg"].ap(), sgate_sb[:], wait=_w(dv))
    if "sup" in tap_outs:
        DMA("y", tap_outs["sup"].ap(), sup_sb[:], wait=_w(dv))
    if "mx" in tap_outs:
        DMA("y", tap_outs["mx"].ap(), mx_sb[:], wait=_w(dv, ac, pe))
    if "q0" in tap_outs:
        DMA("y", tap_outs["q0"].ap(), Q_sb[0][:], wait=_w(dv, ac))
    if "k0" in tap_outs:
        DMA("y", tap_outs["k0"].ap(), K_sb[0][:], wait=_w(dv, ac))
    if "vt0" in tap_outs:
        DMA("y", tap_outs["vt0"].ap(), Vt_sb[0][:], wait=_w(dv, ac))
    if "negm" in tap_outs:
        DMA("y", tap_outs["negm"].ap(), negm_sb[:], wait=_w(dv))
    if "lat" in tap_outs:
        for j in range(4):
            DMA("y", tap_outs["lat"].ap()[:, j * 1024:(j + 1) * 1024],
                cc_xl_out.ap()[D + j * 128: D + (j + 1) * 128, 0:1024],
                wait=[(cc1, 1)])

    sbA2.close()
    psA3.close()
    psA_b.close()
    psB = contextlib.ExitStack()
    s_ps = [ps(psB, f"s_ps{i}", [128, 2 * 512]) for i in range(3)]
    pv_ps = ps(psB, "pv_ps", [128, 512])
    dd_ps = ps(psB, "dd_ps", [128, 512])
    sbA4 = contextlib.ExitStack()
    p_sb = [sb(sbA4, f"p_sb{i}", [128, 2 * 512], BF16) for i in range(3)]
    padd_sb = [sb(sbA4, f"padd{i}", [128, 512], BF16) for i in range(3)]
    padd2_sb = [sb(sbA4, f"padd2_{i}", [128, 512], BF16) for i in range(3)]
    araw_sb = [sb(sbA4, f"araw{i}", [128, 512], BF16) for i in range(2)]
    asb_sb = [sb(sbA4, f"asbx{i}", [128, 512], BF16) for i in range(2)]
    rec_sb = [sb(sbA4, f"rec{i}", [128, 512], F32) for i in range(2)]

    NGRP = 32
    s_bank_free = [0, 0, 0]
    p_buf_free = [0, 0, 0]
    pd2_free = [0, 0, 0]
    araw_free_dv = [0, 0]
    asb_free_sld = [0, 0]
    prev_mult_dv = 0
    prev_recip_dv = 0
    pre_attn = _w(pe, ac, dv, sld, gld)
    attn_entry = _w(dv, ac, pe)
    first_attn = True
    for parity in range(2):
      for h in range(2):
        for qt in range(parity, NT, 2):
            exp_sac = [0] * NGRP
            padd_dv = [0] * NGRP
            padd2_dv = [0] * (NGRP // 2)

            def emit_S(g):
                nonlocal first_attn
                bi = g % 3
                for i in range(2):
                    w0 = []
                    if i == 0:
                        if first_attn:
                            w0 += attn_entry
                            first_attn = False
                        if s_bank_free[bi]:
                            w0.append((sac, s_bank_free[bi]))
                    kcc = g * 2 + i
                    MM(s_ps[bi][:, i * 512:(i + 1) * 512],
                       K_sb[h][:, kcc * 128:(kcc + 1) * 128],
                       Q_sb[h][:, qt * 512:(qt + 1) * 512],
                       start=True, stop=True,
                       wait=w0, inc=[(spe, 1)] if i == 1 else ())

            def emit_exp(g):
                bi = g % 3
                we = _w(spe)
                if p_buf_free[bi]:
                    we.append((spv, p_buf_free[bi]))
                ACT(p_sb[bi][:, 0:1024], s_ps[bi][:, 0:1024], AF.Exp,
                    bias=negm_sb[:, h: h + 1], wait=we, inc=[(sac, 1)])
                s_bank_free[bi] = sac.n
                exp_sac[g] = sac.n

            def emit_padd(g):
                bi = g % 3
                VTT(ALU.add, padd_sb[bi][:], p_sb[bi][:, 0:512],
                    p_sb[bi][:, 512:1024], wait=[(sac, exp_sac[g])],
                    inc=[(dv, 1)])
                padd_dv[g] = dv.n
                if g % 2 == 1:
                    j = (g // 2) % 3
                    wp2 = [(dv, padd_dv[g])]
                    if pd2_free[j]:
                        wp2.append((spv, pd2_free[j]))
                    VTT(ALU.add, padd2_sb[j][:], padd_sb[(g - 1) % 3][:],
                        padd_sb[bi][:], wait=wp2, inc=[(dv, 1)])
                    padd2_dv[g // 2] = dv.n

            def emit_PVD(g):
                bi = g % 3
                for i in range(2):
                    kcc = g * 2 + i
                    wpv = []
                    if i == 0:
                        wpv = [(sac, exp_sac[g])]
                        if g == 0 and prev_mult_dv:
                            wpv.append((dv, prev_mult_dv))
                    MM(pv_ps[:, 0:512],
                       Vt_sb[h][:, kcc * 128:(kcc + 1) * 128],
                       p_sb[bi][:, i * 512:(i + 1) * 512],
                       start=(g == 0 and i == 0),
                       stop=(g == NGRP - 1 and i == 1),
                       wait=wpv,
                       inc=[(spv, 1)] if (g % 2 == 0 and i == 1) else ())
                if g % 2 == 1:
                    wdd = [(dv, padd2_dv[g // 2])]
                    if g == 1 and prev_recip_dv:
                        wdd.append((dv, prev_recip_dv))
                    MM(dd_ps[:, 0:512], onesb_sb[:, 0:128],
                       padd2_sb[(g // 2) % 3][:],
                       start=(g == 1), stop=(g == NGRP - 1),
                       wait=wdd, inc=[(spv, 1)])
                    pd2_free[(g // 2) % 3] = spv.n
                p_buf_free[bi] = spv.n

            emit_S(0)
            emit_exp(0)
            emit_padd(0)
            emit_S(1)
            emit_exp(1)
            emit_padd(1)
            for g in range(2, NGRP):
                emit_S(g)
                emit_exp(g)
                emit_padd(g)
                emit_PVD(g - 2)
            emit_PVD(NGRP - 2)
            emit_PVD(NGRP - 1)

            abi = qt % 2
            wq = _w(spv)
            if araw_free_dv[abi]:
                wq.append((dv, araw_free_dv[abi]))
            VCP(araw_sb[abi][:], pv_ps[:, 0:512], wait=wq, inc=[(dv, 1)])
            prev_mult_dv = dv.n
            VRECIP(rec_sb[abi][:], dd_ps[:, 0:512], inc=[(dv, 1)])
            prev_recip_dv = dv.n
            wv = _w(dv)
            if asb_free_sld[abi]:
                wv.append((sld, asb_free_sld[abi]))
            VTT(ALU.mult, asb_sb[abi][:], araw_sb[abi][:],
                rec_sb[abi][:], wait=wv, inc=[(dv, 1)])
            araw_free_dv[abi] = dv.n
            blk = qt // 2
            cc_t = cc_a_in0 if parity == 0 else cc_a_in1
            DMA("y", cc_t.ap()[h * 128:(h + 1) * 128,
                               blk * 512:(blk + 1) * 512],
                asb_sb[abi][:], wait=_w(dv))
            asb_free_sld[abi] = sld.n
            if "asb" in tap_outs and h == 0 and qt == 0:
                DMA("y", tap_outs["asb"].ap(), asb_sb[abi][:], wait=_w(dv))
      if parity == 0:
          sld_a0 = sld.n
          b.add("g", (lambda e: e.collective_compute(
              "AllGather", ALU.bypass, replica_groups=[list(range(NCORES))],
              ins=[cc_a_in0.ap().opt()], outs=[cc_a_out0.ap().opt()])),
              [(sld, sld_a0), (cc1, 1)], [(cc2, 1)])

    sld_a1 = sld.n
    b.add("g", (lambda e: e.collective_compute(
        "AllGather", ALU.bypass, replica_groups=[list(range(NCORES))],
        ins=[cc_a_in1.ap().opt()], outs=[cc_a_out1.ap().opt()])),
        [(sld, sld_a1), (cc2, 1)], [(cc2b, 1)])

    end_attn = _w(pe, ac, dv, sld, gld)
    sbA4.close()
    sbA_kv.close()
    sbA.close()
    psB.close()

    sbB2 = contextlib.ExitStack()
    xs_sb = sb(sbB2, "xs_sb", [128, 16 * TOK], BF16, side="right")

    sbB3 = contextlib.ExitStack()
    vb_sb = sb(sbB3, "vb_sb", [128, 4 * TOK], BF16, side="right")
    sqb_sb = sb(sbB3, "sqb_sb", [128, 4 * TOK], BF16, side="right")
    mbc_sb = sb(sbB3, "mbc_sb", [128, TOK], F32, side="right")
    rbc_sb = sb(sbB3, "rbc_sb", [128, TOK], F32, side="right")
    t1f_sb = sb(sbB3, "t1f_sb", [128, TOK], F32, side="right")

    sbB1 = contextlib.ExitStack()
    aown_sb = sb(sbB1, "aown_sb", [128, 16 * TOK], BF16, side="right")
    v1_sb = sb(sbB1, "v1_sb", [128, 16 * TOK], F32, side="right")
    owb_sb = [sb(sbB1, f"owb{i}", [128, 16 * 128], BF16, side="right")
              for i in range(2)]
    xr_sb = [sb(sbB1, f"xr{i}", [128, TOK], F32, side="right")
             for i in range(2)]

    psC1 = contextlib.ExitStack()
    o_ps = ps(psC1, "o_ps", [128, 1024])
    psC2 = contextlib.ExitStack()
    mean_ps = ps(psC2, "mean_ps", [128, 1024])
    msq_ps = ps(psC2, "msq_ps", [128, 1024])

    for i in range(16):
        src = cc_a_out0.ap()[i * 128:(i + 1) * 128, 0:512]
        src = dataclasses.replace(src, offset=src.offset + pid512)
        DMA("y", aown_sb[:, i * TOK: i * TOK + 512], src,
            wait=[(cc2, 1)] if i == 0 else ())
    sld_aown0 = sld.n
    for i in range(16):
        src = cc_a_out1.ap()[i * 128:(i + 1) * 128, 0:512]
        src = dataclasses.replace(src, offset=src.offset + pid512)
        DMA("y", aown_sb[:, i * TOK + 512: i * TOK + 1024], src,
            wait=[(cc2b, 1)] if i == 0 else ())
    sld_aown1 = sld.n

    ow_v = ow.ap().rearrange("(i p) o -> p i o", p=128)
    ow_used = [0, 0]
    o_done_dv = 0
    stats_pe = 0
    for jo in range(16):
        bi = jo % 2
        DMA("g", owb_sb[bi].ap().rearrange("p (i o) -> p i o", i=16),
            ow_v[:, :, jo * 128:(jo + 1) * 128],
            wait=[(pe, ow_used[bi])] if ow_used[bi] else pre_attn)
        gld_ow = gld.n
        DMA("y", xr_sb[bi][:], xo.ap()[jo * 128:(jo + 1) * 128, :],
            wait=[(dv, o_done_dv)] if o_done_dv else ())
        sld_xr = sld.n
        for h2 in range(2):
            for i in range(16):
                w0 = []
                if i == 0:
                    w0 = [(gld, gld_ow),
                          (sld, sld_aown0 if h2 == 0 else sld_aown1)]
                    if h2 == 0 and o_done_dv:
                        w0.append((dv, o_done_dv))
                MM(o_ps[:, h2 * 512: h2 * 512 + 512],
                   owb_sb[bi][:, i * 128:(i + 1) * 128],
                   aown_sb[:, i * TOK + h2 * 512: i * TOK + h2 * 512 + 512],
                   start=(i == 0), stop=(i == 15),
                   wait=w0,
                   inc=[(pe, 1)] if (i == 15 and h2 == 1) else ())
        ow_used[bi] = pe.n
        VTT(ALU.add, v1_sb[:, jo * TOK:(jo + 1) * TOK], o_ps[:], xr_sb[bi][:],
            wait=_w(pe) + [(sld, sld_xr)], inc=[(dv, 1)])
        o_done_dv = dv.n
        q4 = jo % 4
        VCP(vb_sb[:, q4 * TOK:(q4 + 1) * TOK],
            v1_sb[:, jo * TOK:(jo + 1) * TOK],
            wait=[(pe, stats_pe)] if (q4 == 0 and stats_pe) else (),
            inc=[(dv, 1)])
        ACT(sqb_sb[:, q4 * TOK:(q4 + 1) * TOK],
            v1_sb[:, jo * TOK:(jo + 1) * TOK], AF.Square,
            wait=_w(dv), inc=[(ac, 1)])
        if q4 == 3:
            for qq in range(4):
                jj = jo - 3 + qq
                for h2 in range(2):
                    off = qq * TOK + h2 * 512
                    MM(mean_ps[:, h2 * 512: h2 * 512 + 512],
                       onesb_sb[:, 0:128], vb_sb[:, off: off + 512],
                       start=(jj == 0), stop=(jj == 15),
                       wait=_w(dv, ac) if (qq == 0 and h2 == 0) else ())
                    MM(msq_ps[:, h2 * 512: h2 * 512 + 512],
                       onesb_sb[:, 0:128], sqb_sb[:, off: off + 512],
                       start=(jj == 0), stop=(jj == 15),
                       inc=[(pe, 1)] if (qq == 3 and h2 == 1) else ())
            stats_pe = pe.n

    def ln_stats(mean_src, msq_src, mbc_t, rbc_t, sca, scb):
        VTS(mbc_t, mean_src[:, 0:1024], 1.0 / D, wait=_w(pe),
            inc=[(dv, 1)])
        VTS(sca, msq_src[:, 0:1024], 1.0 / D, inc=[(dv, 1)])
        ACT(scb, mbc_t, AF.Square, wait=_w(dv), inc=[(ac, 1)])
        VTT(ALU.subtract, sca, sca, scb, wait=_w(ac), inc=[(dv, 1)])
        ACT(scb, sca, AF.Ln, bias=epsv_sb[:, 1:2], wait=_w(dv),
            inc=[(ac, 1)])
        ACT(rbc_t, scb, AF.Exp, scale=-0.5, inc=[(ac, 1)])

    ln1sc = sqb_sb.ap().bitcast(F32)
    ln_stats(mean_ps, msq_ps, mbc_sb[:], rbc_sb[:],
             ln1sc[:, 0:1024], ln1sc[:, 1024:2048])
    psC2.close()
    psC3 = contextlib.ExitStack()
    o2_ps = ps(psC3, "o2_ps", [128, 1024])

    for jo in range(16):
        sl = slice(jo * TOK, (jo + 1) * TOK)
        VTT(ALU.subtract, t1f_sb[:], v1_sb[:, sl], mbc_sb[:],
            wait=_w(dv, ac))
        VTT(ALU.mult, v1_sb[:, sl], t1f_sb[:], rbc_sb[:], inc=[(dv, 1)])
        DMA("y", x1_sp.ap()[jo * 128:(jo + 1) * 128, :], v1_sb[:, sl],
            wait=_w(dv))
        VTS(xs_sb[:, sl], v1_sb[:, sl], suug_sb[:, jo: jo + 1],
            inc=[(dv, 1)])
        lvl = 0
        while (jo >> lvl) & 1:
            wdt2 = (1 << lvl) * TOK
            base = (jo - (2 << lvl) + 1) * TOK
            a_ap = xs_sb[:, base: base + wdt2]
            b_ap = xs_sb[:, base + wdt2: base + 2 * wdt2]
            VTT(ALU.add, a_ap, a_ap, b_ap)
            VSTT(b_ap, b_ap, -2.0, a_ap, ALU.mult, ALU.add, inc=[(dv, 1)])
            lvl += 1
    sld_x1 = sld.n

    def butterflies(tgt, nck, width):
        stv = 1
        while stv < nck:
            hi = nck // (2 * stv)
            blkw = stv * width
            if 2 * blkw <= 16384:
                v3 = tgt.ap().rearrange("p (hi rest) -> p hi rest", hi=hi)
                a_ap = v3[:, :, 0:blkw]
                b_ap = v3[:, :, blkw: 2 * blkw]
                VTT(ALU.add, a_ap, a_ap, b_ap, wait=_w(dv))
                VSTT(b_ap, b_ap, -2.0, a_ap, ALU.mult, ALU.add,
                     inc=[(dv, 1)])
            else:
                for g2 in range(hi):
                    base = g2 * 2 * blkw
                    a_ap = tgt[:, base: base + blkw]
                    b_ap = tgt[:, base + blkw: base + 2 * blkw]
                    VTT(ALU.add, a_ap, a_ap, b_ap,
                        wait=_w(dv) if g2 == 0 else ())
                    VSTT(b_ap, b_ap, -2.0, a_ap, ALU.mult, ALU.add,
                         inc=[(dv, 1)] if g2 == hi - 1 else ())
            stv *= 2

    h16_evac = [0, 0]
    for c in range(16):
        hps = o_ps if c % 2 == 0 else o2_ps
        for h2 in range(2):
            w0 = []
            if h2 == 0:
                w0 = _w(dv)
                if h16_evac[c % 2]:
                    w0.append((ac, h16_evac[c % 2]))
            MM(hps[:, h2 * 512: h2 * 512 + 512], h128_sb[:],
               xs_sb[:, c * TOK + h2 * 512: c * TOK + h2 * 512 + 512],
               wait=w0, inc=[(pe, 1)] if h2 == 1 else ())
        ACT(xs_sb[:, c * TOK:(c + 1) * TOK], hps[:], AF.Copy, wait=_w(pe),
            inc=[(ac, 1)])
        h16_evac[c % 2] = ac.n

    if "aown" in tap_outs:
        DMA("y", tap_outs["aown"].ap(), aown_sb[:], wait=[(sld, sld_aown1)])
    if "xs" in tap_outs:
        DMA("y", tap_outs["xs"].ap(), xs_sb[:], wait=_w(dv))

    p6_end = _w(pe, ac, dv, sld, gld)
    sbB1.close()
    sbB3.close()
    psC3.close()
    psC1.close()

    sbC = contextlib.ExitStack()
    u_sb = sb(sbC, "u_sb", [128, 64 * TOK], BF16)
    sbC_w = contextlib.ExitStack()
    wub_sb = [sb(sbC_w, f"wub{i}", [128, 16 * 128], BF16) for i in range(2)]
    wgb_sb = [sb(sbC_w, f"wgb{i}", [128, 16 * 128], BF16) for i in range(2)]
    gsil_sb = sb(sbC_w, "gsil_sb", [128, 1024], F32)

    psD = contextlib.ExitStack()
    up_ps = [ps(psD, "up_ps0", [128, 1024])]
    psD1 = contextlib.ExitStack()
    up_ps.append(ps(psD1, "up_ps1", [128, 1024]))
    psDg = contextlib.ExitStack()
    gt_ps = [ps(psDg, f"gt_ps{i}", [128, 1024]) for i in range(2)]

    wup_v = wup.ap().rearrange("(i p) o -> p i o", p=128)
    wg_v = wg.ap().rearrange("(i p) o -> p i o", p=128)
    ug_used = [0, 0]
    u_done = [0, 0]
    u_done_any = 0
    gt_free = [0, 0]
    for jo in range(64):
        bi = jo % 2
        DMA("g", wub_sb[bi].ap().rearrange("p (i o) -> p i o", i=16),
            wup_v[:, :, jo * 128:(jo + 1) * 128],
            wait=[(pe, ug_used[bi])] if ug_used[bi] else p6_end)
        DMA("g", wgb_sb[bi].ap().rearrange("p (i o) -> p i o", i=16),
            wg_v[:, :, jo * 128:(jo + 1) * 128])
        gld_b = gld.n
        for h2 in range(2):
            for i in range(16):
                w0 = []
                if i == 0 and h2 == 0:
                    w0 = [(gld, gld_b)]
                    if jo < 2:
                        w0 += p6_end
                    if u_done[bi]:
                        w0.append((dv, u_done[bi]))
                MM(up_ps[bi][:, h2 * 512: h2 * 512 + 512],
                   wub_sb[bi][:, i * 128:(i + 1) * 128],
                   xs_sb[:, i * TOK + h2 * 512: i * TOK + h2 * 512 + 512],
                   start=(i == 0), stop=(i == 15),
                   wait=w0,
                   inc=[(pe, 1)] if (i == 15 and h2 == 1) else ())
        for h2 in range(2):
            for i in range(16):
                w0 = []
                if i == 0 and h2 == 0 and gt_free[bi]:
                    w0 = [(ac, gt_free[bi])]
                MM(gt_ps[bi][:, h2 * 512: h2 * 512 + 512],
                   wgb_sb[bi][:, i * 128:(i + 1) * 128],
                   xs_sb[:, i * TOK + h2 * 512: i * TOK + h2 * 512 + 512],
                   start=(i == 0), stop=(i == 15),
                   wait=w0,
                   inc=[(pe, 1)] if (i == 15 and h2 == 1) else ())
        ug_used[bi] = pe.n
        wsil = _w(pe)
        if u_done_any:
            wsil.append((dv, u_done_any))
        ACT(gsil_sb[:], gt_ps[bi][:], AF.Silu, scale=sgate_sb[:, jo: jo + 1],
            wait=wsil, inc=[(ac, 1)])
        gt_free[bi] = ac.n
        VSTT(u_sb[:, jo * TOK:(jo + 1) * TOK], up_ps[bi][:],
             sup_sb[:, jo: jo + 1], gsil_sb[:], wait=_w(ac), inc=[(dv, 1)])
        u_done[bi] = dv.n
        u_done_any = dv.n
        lvl = 0
        while CASCADE and ((jo >> lvl) & 1) and lvl < 5:
            wdt2 = (1 << lvl) * TOK
            base = (jo - (2 << lvl) + 1) * TOK
            a_ap = u_sb[:, base: base + wdt2]
            b_ap = u_sb[:, base + wdt2: base + 2 * wdt2]
            VTT(ALU.add, a_ap, a_ap, b_ap)
            VSTT(b_ap, b_ap, -2.0, a_ap, ALU.mult, ALU.add, inc=[(dv, 1)])
            lvl += 1

    if "u0" in tap_outs:
        for c in range(8):
            DMA("y", tap_outs["u0"].ap()[:, c * TOK:(c + 1) * TOK],
                u_sb[:, c * TOK:(c + 1) * TOK], wait=_w(dv))

    if not CASCADE:
        butterflies(u_sb, 64, TOK)

    h_evac_ac = [0, 0]
    for p4 in range(4):
        base_a = p4 * 8 * TOK
        base_b = (32 + p4 * 8) * TOK
        a_ap = u_sb[:, base_a: base_a + 8 * TOK]
        b_ap = u_sb[:, base_b: base_b + 8 * TOK]
        VTT(ALU.add, a_ap, a_ap, b_ap, wait=_w(dv) if p4 == 0 else ())
        VSTT(b_ap, b_ap, -2.0, a_ap, ALU.mult, ALU.add, inc=[(dv, 1)])
        piece_dv = dv.n
        for c in (list(range(p4 * 8, p4 * 8 + 8))
                  + list(range(32 + p4 * 8, 32 + p4 * 8 + 8))):
            bb = c % 2
            for h2 in range(2):
                w0 = []
                if h2 == 0:
                    w0 = [(dv, piece_dv)]
                    if h_evac_ac[bb]:
                        w0.append((ac, h_evac_ac[bb]))
                MM(up_ps[bb][:, h2 * 512: h2 * 512 + 512], h128_sb[:],
                   u_sb[:, c * TOK + h2 * 512: c * TOK + h2 * 512 + 512],
                   wait=w0, inc=[(pe, 1)] if h2 == 1 else ())
            ACT(u_sb[:, c * TOK:(c + 1) * TOK], up_ps[bb][:], AF.Copy,
                wait=_w(pe), inc=[(ac, 1)])
            h_evac_ac[bb] = ac.n
    u_ready_ac = ac.n

    if "u" in tap_outs:
        for c in range(8):
            DMA("y", tap_outs["u"].ap()[:, c * TOK:(c + 1) * TOK],
                u_sb[:, c * TOK:(c + 1) * TOK], wait=_w(ac))

    psDg.close()
    psD1.close()
    sbB2.close()
    sbC_w.close()
    sbD = contextlib.ExitStack()
    wdb_sb = [sb(sbD, f"wdb{i}", [128, 32 * 128], BF16) for i in range(2)]
    x1r_sb = [sb(sbD, f"x1r{i}", [128, TOK], F32) for i in range(2)]
    vread_sb = x1r_sb + [sb(sbD, f"vread{i}", [128, TOK], F32)
                         for i in range(2)]
    v2t_sb = sb(sbD, "v2t_sb", [128, TOK], F32)
    out_sb = [v2t_sb, sb(sbD, "outsb1", [128, TOK], F32)]
    vb2_sb = sb(sbD, "vb2_sb", [128, 4 * TOK], BF16)
    sqb2_sb = sb(sbD, "sqb2_sb", [128, 4 * TOK], BF16)
    mbc2_sb = sb(sbD, "mbc2_sb", [128, TOK], F32)
    rbc2_sb = sb(sbD, "rbc2_sb", [128, TOK], F32)

    psE = contextlib.ExitStack()
    mean2_ps = ps(psE, "mean2_ps", [128, 1024])
    msq2_ps = ps(psE, "msq2_ps", [128, 1024])

    wd_v = wd.ap().rearrange("(a p) o -> p a o", p=128)
    wd_used = [0, 0]
    v2_done = 0
    stats2_pe = 0
    sq2_ac = 0
    v2st = [0] * 16
    for jo in range(16):
        for half in range(2):
            hb = (jo * 2 + half) % 2
            DMA("g",
                wdb_sb[hb].ap().rearrange("p (a o) -> p a o", a=32),
                wd_v[:, half * 32:(half + 1) * 32, jo * 128:(jo + 1) * 128],
                wait=([(pe, wd_used[hb])] if wd_used[hb]
                      else [(ac, u_ready_ac)]))
            gld_wd = gld.n
            for a2 in range(32):
                for h2 in range(2):
                    a = half * 32 + a2
                    w0 = []
                    if h2 == 0 and a2 == 0:
                        w0 = [(gld, gld_wd), (ac, u_ready_ac)]
                        if half == 0 and v2_done:
                            w0.append((dv, v2_done))
                    MM(up_ps[0][:, h2 * 512: h2 * 512 + 512],
                       wdb_sb[hb][:, a2 * 128:(a2 + 1) * 128],
                       u_sb[:, a * TOK + h2 * 512: a * TOK + h2 * 512 + 512],
                       start=(a == 0), stop=(a == 63),
                       wait=w0,
                       inc=[(pe, 1)] if (h2 == 1 and a2 == 31) else ())
            wd_used[hb] = pe.n
        DMA("y", x1r_sb[jo % 2][:], x1_sp.ap()[jo * 128:(jo + 1) * 128, :],
            wait=[(sld, sld_x1), (ac, u_ready_ac)]
                 + ([(dv, v2_done)] if v2_done else []))
        sld_x1r = sld.n
        VSTT(v2t_sb[:], up_ps[0][:], sdp_sb[:, jo: jo + 1], x1r_sb[jo % 2][:],
             ALU.mult, ALU.add,
             wait=_w(pe) + [(sld, sld_x1r)]
                  + ([(ac, sq2_ac)] if sq2_ac else []),
             inc=[(dv, 1)])
        v2_done = dv.n
        DMA("y", v2_sp.ap()[jo * 128:(jo + 1) * 128, :], v2t_sb[:],
            wait=_w(dv))
        v2st[jo] = sld.n
        q4 = jo % 4
        VCP(vb2_sb[:, q4 * TOK:(q4 + 1) * TOK], v2t_sb[:],
            wait=[(pe, stats2_pe)] if (q4 == 0 and stats2_pe) else (),
            inc=[(dv, 1)])
        ACT(sqb2_sb[:, q4 * TOK:(q4 + 1) * TOK], v2t_sb[:], AF.Square,
            wait=_w(dv), inc=[(ac, 1)])
        sq2_ac = ac.n
        if q4 == 3:
            for qq in range(4):
                jj = jo - 3 + qq
                for h2 in range(2):
                    off = qq * TOK + h2 * 512
                    MM(mean2_ps[:, h2 * 512: h2 * 512 + 512],
                       onesb_sb[:, 0:128], vb2_sb[:, off: off + 512],
                       start=(jj == 0), stop=(jj == 15),
                       wait=_w(dv, ac) if (qq == 0 and h2 == 0) else ())
                    MM(msq2_ps[:, h2 * 512: h2 * 512 + 512],
                       onesb_sb[:, 0:128], sqb2_sb[:, off: off + 512],
                       start=(jj == 0), stop=(jj == 15),
                       inc=[(pe, 1)] if (qq == 3 and h2 == 1) else ())
            stats2_pe = pe.n
    sld_v2 = sld.n

    ln2sc = sqb2_sb.ap().bitcast(F32)
    ln_stats(mean2_ps, msq2_ps, mbc2_sb[:], rbc2_sb[:],
             ln2sc[:, 0:1024], ln2sc[:, 1024:2048])

    vrd_gld = [0] * 16

    def v2_read(jo, wextra=()):
        DMA("y", vread_sb[jo % 4][:], v2_sp.ap()[jo * 128:(jo + 1) * 128, :],
            wait=[(sld, v2st[jo])] + list(wextra))
        vrd_gld[jo] = sld.n

    for jo in range(4):
        v2_read(jo, [(dv, v2_done)] if jo < 2 else ())
    outst = [0, 0]
    for jo in range(16):
        bi = jo % 2
        VTT(ALU.subtract, vread_sb[jo % 4][:], vread_sb[jo % 4][:],
            mbc2_sb[:], wait=[(sld, vrd_gld[jo])] + _w(ac))
        wm = []
        if outst[bi]:
            wm = [(sld, outst[bi])]
        VTT(ALU.mult, out_sb[bi][:], vread_sb[jo % 4][:], rbc2_sb[:],
            wait=wm, inc=[(dv, 1)])
        if jo + 4 < 16:
            v2_read(jo + 4, [(dv, dv.n)])
        DMA("y", out_t.ap()[jo * 128:(jo + 1) * 128, :], out_sb[bi][:],
            wait=_w(dv))
        outst[bi] = sld.n

    b.add("y", (lambda e: None), _w(sld, gld), ())

    sbD.close()
    psE.close()
    psD.close()
    sbC.close()

    with nc.Block() as block:
        block.tensor(lambda e: b.run("t", e))
        block.vector(lambda e: b.run("v", e))
        block.scalar(lambda e: b.run("a", e))
        block.sync(lambda e: b.run("y", e))

        block.gpsimd(lambda e: b.run("g", e))

    ctx.close()
    return nc


def _prep_inputs(ins):
    hs = np.asarray(ins["L0_attn_hidden_states"])[0]
    q_w = np.asarray(ins["L0_attn_q_w"])
    kv_a_w = np.asarray(ins["L0_attn_kv_a_w"])
    kv_b_w = np.asarray(ins["L0_attn_kv_b_w"])
    o_w = np.asarray(ins["L0_attn_o_w"])
    kv_norm_w = np.asarray(ins["L0_attn_kv_norm_w"])
    SU_ug = np.asarray(ins["L0_ffn_SU_ug"])
    SU_dp = np.asarray(ins["L0_ffn_SU_dp"])
    Wsc_ug = np.asarray(ins["L0_ffn_Wscale_ug"])
    Wsc_dp = np.asarray(ins["L0_ffn_Wscale_dp"])
    W_up = np.asarray(ins["L0_ffn_W_up"])
    W_gate = np.asarray(ins["L0_ffn_W_gate"])
    W_down = np.asarray(ins["L0_ffn_W_down"])

    xT = np.ascontiguousarray(hs.T)
    kvawT = np.ascontiguousarray(kv_a_w[RD:, :].T)
    owT = np.ascontiguousarray(o_w.T)
    wupT = np.ascontiguousarray(W_up.T)
    wgT = np.ascontiguousarray(W_gate.T)
    wdT = np.ascontiguousarray(W_down.T)

    cs_tab = _cs_table()
    h128b = _hadamard(128).astype(ml_dtypes.bfloat16)
    onesb = np.ones((128, 128), ml_dtypes.bfloat16)
    onesf = np.ones((1, 128), np.float32)

    kvnw = np.ascontiguousarray(kv_norm_w.reshape(4, 128).T)
    suug = np.ascontiguousarray(SU_ug.reshape(16, 128).T)
    wscup = np.ascontiguousarray(Wsc_ug[:INTER].reshape(64, 128).T)
    wscg = np.ascontiguousarray(Wsc_ug[INTER:].reshape(64, 128).T)
    sudp = np.ascontiguousarray(SU_dp.reshape(64, 128).T)
    wscdp = np.ascontiguousarray(Wsc_dp.reshape(16, 128).T)
    epsv = np.tile(np.array([[EPS_RMS, EPS_LN]], np.float32), (128, 1))

    maps = []
    for c in range(NCORES):
        maps.append({
            "xo": np.ascontiguousarray(xT[:, c * TOK:(c + 1) * TOK]),
            "cs": cs_tab,
            "h128": h128b,
            "onesb": onesb,
            "onesf": onesf,
            "qw": np.ascontiguousarray(q_w[c * 256:(c + 1) * 256, :].T),
            "kvaw": kvawT,
            "kvbw": np.ascontiguousarray(kv_b_w[c * 512:(c + 1) * 512, :].T),
            "ow": owT,
            "wup": wupT,
            "wg": wgT,
            "wd": wdT,
            "kvnw": kvnw,
            "suug": suug,
            "wscup": wscup,
            "wscg": wscg,
            "sudp": sudp,
            "wscdp": wscdp,
            "epsv": epsv,
        })
    return maps


def kernel(**ins):
    maps = _prep_inputs(ins)
    nc = build_nc()
    res = run_bass_kernel_spmd(nc, maps, core_ids=list(range(NCORES)))
    full = np.empty((D, S), np.float32)
    for c in range(NCORES):
        full[:, c * TOK:(c + 1) * TOK] = res.results[c]["out"]
    return np.ascontiguousarray(full.T)[None]
